# version 04
# baseline (speedup 1.0000x reference)
import numpy as np
import ml_dtypes
from contextlib import ExitStack
import concourse.bass as bass
import concourse.mybir as mybir
from concourse.bass_utils import run_bass_kernel_spmd
from concourse.bass_core_rust import IndirectOffsetOnAxis

F32 = mybir.dt.float32
BF16 = mybir.dt.bfloat16
I32 = mybir.dt.int32
AF = mybir.ActivationFunctionType
ALU = mybir.AluOpType
AX = mybir.AxisListType
BF = ml_dtypes.bfloat16

B, T, D, DI, DS, DTR = 2, 8192, 1024, 2048, 16, 64
NTOK = B * T
NC = 8
TOKC = NTOK // NC
CH = 512
LN_EPS = 1e-5

ENGS = ("sync", "vector", "scalar", "tensor", "gpsimd")


class Sem:
    def __init__(self, prog, name):
        self.h = prog.es.enter_context(prog.nc.semaphore(name))
        self.n = 0


class Prog:

    def __init__(self, nc):
        self.nc = nc
        self.es = ExitStack()
        self.q = {e: [] for e in ENGS}
        self._sb = {}

    def sem(self, name):
        return Sem(self, name)

    def sb(self, name, shape, dtype):
        t = self.es.enter_context(self.nc.sbuf_tensor(name, list(shape), dtype))
        self._sb[name] = t
        return t

    def ps(self, name, shape, dtype=F32):
        t = self.es.enter_context(self.nc.psum_tensor(name, list(shape), dtype))
        self._sb[name] = t
        return t

    def do(self, eng, fn, inc=None, k=1):
        if inc is not None:
            inc.n += k
        self.q[eng].append((fn, inc, k))

    def wait(self, eng, *sems):
        for s in sems:
            v = s.n
            if v > 0:
                self.q[eng].append((None, s, v))

    def scope(self):
        from contextlib import contextmanager

        @contextmanager
        def _s():
            saved_sb = self.nc.sbuf_base
            saved_ps = self.nc.psum_base
            inner = ExitStack()
            outer_es = self.es
            self.es = inner
            try:
                yield
            finally:
                self.es = outer_es
                inner.close()
                self.nc.sbuf_base = saved_sb
                self.nc.psum_base = saved_ps
        return _s()

    def barrier(self, sems):
        for e in ENGS:
            self.wait(e, *sems)

    mock_cc = False

    def collective(self, kind, op, in_ap, out_ap, sem):
        if Prog.mock_cc:
            nrow = in_ap.shape[0]
            if kind == "AllGather":
                for r in range(NC):
                    self.do("gpsimd", lambda e, i=in_ap, o=out_ap, n=nrow, r=r:
                            e.dma_start(out=o[r * n:(r + 1) * n, :], in_=i),
                            inc=sem, k=16)
            else:
                self.do("gpsimd", lambda e, i=in_ap, o=out_ap, n=out_ap.shape[0]:
                        e.dma_start(out=o, in_=i[0:n, :]), inc=sem, k=16)
            return
        self.do("gpsimd", lambda e, i=in_ap, o=out_ap, k=kind, p=op:
                e.collective_compute(k, p, replica_groups=[list(range(NC))],
                                     ins=[i.opt()], outs=[o.opt()]), inc=sem, k=1)

    def dma(self, eng, out, in_, sem, fence=True):
        self.do(eng, lambda e, o=out, i=in_: e.dma_start(out=o, in_=i), inc=sem, k=16)
        if fence:
            self.wait(eng, sem)

    def fence(self, eng, sem):
        self.wait(eng, sem)

    def finish(self):
        blk = self.es.enter_context(self.nc.Block())
        for name in ENGS:
            ops = self.q[name]

            def make(ops=ops):
                def run(eng):
                    for fn, sem, k in ops:
                        if fn is None:
                            eng.wait_ge(sem.h, k)
                        else:
                            inst = fn(eng)
                            if sem is not None:
                                inst.then_inc(sem.h, k)
                return run
            getattr(blk, name)(make())
        self.es.close()
        return self.nc


def _nc():
    return bass.Bass("TRN2", target_bir_lowering=False)


def build_front(upto='all', nch=None, upto_sub='all'):
    nc = _nc()
    P = Prog(nc)
    NH = 3
    CE = CH + NH
    NCH = TOKC // CH

    x_ext = nc.dram_tensor("x_ext", [8, 128, NH + TOKC], F32, kind="ExternalInput").ap()
    inpw_t = nc.dram_tensor("inpw_t", [8, 128, 2 * DI], F32, kind="ExternalInput").ap()
    norm_w = nc.dram_tensor("norm_w", [8, 128, 1], F32, kind="ExternalInput").ap()
    conv_wa = nc.dram_tensor("conv_wa", [128, 16 * 4], F32, kind="ExternalInput").ap()
    conv_ba = nc.dram_tensor("conv_ba", [128, 16], F32, kind="ExternalInput").ap()
    xpw_t = nc.dram_tensor("xpw_t", [16, 128, 96], F32, kind="ExternalInput").ap()
    dtpw_t = nc.dram_tensor("dtpw_t", [64, DI], F32, kind="ExternalInput").ap()
    dtb_a = nc.dram_tensor("dtb_a", [128, 16], F32, kind="ExternalInput").ap()
    ones_c = nc.dram_tensor("ones_c", [128, 1], F32, kind="ExternalInput").ap()
    cst_i = nc.dram_tensor("cst_i", [128, 8], F32, kind="ExternalInput").ap()
    ones_r = nc.dram_tensor("ones_r", [1, 128], F32, kind="ExternalInput").ap()

    xn_o = nc.dram_tensor("xn_o", [8, 128, TOKC], F32, kind="ExternalOutput").ap()
    res_o = nc.dram_tensor("res_o", [16, 128, TOKC], BF16, kind="ExternalOutput").ap()
    u_o = nc.dram_tensor("u_o", [16, 128, TOKC], BF16, kind="ExternalOutput").ap()
    dl_o = nc.dram_tensor("dl_o", [16, 128, TOKC], BF16, kind="ExternalOutput").ap()
    bc_o = nc.dram_tensor("bc_o", [32, TOKC], F32, kind="ExternalOutput").ap()

    w_in = P.sb("w_in", (128, 8 * 2 * DI), BF16)
    w_xp = P.sb("w_xp", (128, 16 * 96), BF16)
    w_dt = P.sb("w_dt", (64, DI), BF16)
    wtmp = P.sb("wtmp", (128, 2 * DI), F32)
    cw = P.sb("cw", (128, 64), F32)
    cb = P.sb("cb", (128, 16), F32)
    dtb = P.sb("dtb", (128, 16), F32)
    nwc = P.sb("nwc", (128, 8), F32)
    ones_bf = P.sb("ones_bf", (128, 1), BF16)
    cst = P.sb("cst", (128, 8), F32)
    oner_f = P.sb("oner_f", (1, 128), F32)
    xc = P.sb("xc", (128, 8 * CH), F32)
    xbf = P.sb("xbf", (128, 8 * CH), BF16)
    sq = P.sb("sq", (128, 8 * CH), BF16)
    xnt = P.sb("xnt", (128, 8 * CH), F32)
    ht = P.sb("ht", (128, 8 * CH), BF16)
    srow = P.sb("srow", (1, 6 * CH), F32)
    xs_ext = P.sb("xs_ext", (128, 16 * CE), BF16)
    rbuf = P.sb("rbuf", (128, 2 * CH), BF16)
    convt = P.sb("convt", (128, CH), F32)
    u_full = P.sb("u_full", (128, 16 * CH), BF16)
    din_bf = P.sb("din_bf", (64, CH), BF16)
    bc_sb = P.sb("bc_sb", (32, CH), F32)
    sp1 = P.sb("sp1", (128, CH), F32)
    sp2 = P.sb("sp2", (128, CH), F32)
    dbuf = P.sb("dbuf", (128, 2 * CH), BF16)

    ps_s1 = P.ps("ps_s1", (1, CH))
    ps_s2 = P.ps("ps_s2", (1, CH))
    ps_r1 = P.ps("ps_r1", (128, CH))
    ps_r2 = P.ps("ps_r2", (128, CH))
    ps_a = P.ps("ps_a", (128, CH))
    ps_b = P.ps("ps_b", (128, CH))
    ps_x = P.ps("ps_x", (96, CH))

    ld = P.sem("ld")
    gd = P.sem("gd")
    vv = P.sem("vv")
    aa = P.sem("aa")
    tt = P.sem("tt")
    st = P.sem("st")

    for k in range(8):
        P.dma("sync", wtmp[:, :], inpw_t[k, :, :], ld)
        P.dma("sync", nwc[:, k:k + 1], norm_w[k, :, :], ld)
        P.wait("vector", ld)
        P.do("vector", lambda e, k=k: e.tensor_scalar(
            out=w_in[:, 4096 * k:4096 * (k + 1)], in0=wtmp[:, :],
            scalar1=nwc[:, k:k + 1], scalar2=None, op0=ALU.mult), inc=vv)
        P.wait("sync", vv)
    for k in range(16):
        P.dma("sync", wtmp[:, 0:96], xpw_t[k, :, :], ld)
        P.wait("vector", ld)
        P.do("vector", lambda e, k=k: e.tensor_copy(
            out=w_xp[:, 96 * k:96 * (k + 1)], in_=wtmp[:, 0:96]), inc=vv)
        P.wait("sync", vv)
    P.dma("sync", wtmp[0:64, 0:DI], dtpw_t, ld)
    P.wait("vector", ld)
    P.do("vector", lambda e: e.tensor_copy(out=w_dt[:, :], in_=wtmp[0:64, 0:DI]), inc=vv)
    P.dma("sync", cw[:, :], conv_wa, ld)
    P.dma("sync", cb[:, :], conv_ba, ld)
    P.dma("sync", dtb[:, :], dtb_a, ld)
    P.dma("gpsimd", ones_bf[:, :], ones_c, gd)
    P.dma("sync", cst[:, :], cst_i, ld)
    P.dma("sync", oner_f[:, :], ones_r, ld)

    MU, EX2, VAR, RSTD, MS, RSTD2 = range(6)

    def row(i):
        return srow[0:1, CH * i:CH * i + CH]

    def chunk_barrier():
        P.wait("sync", vv, aa, tt)
        P.wait("vector", tt, aa)
        P.wait("scalar", tt, vv)
        P.wait("tensor", vv, aa)
        P.wait("gpsimd", vv)

    def ln_rms_chunk(n, col0, halo):
        chunk_barrier()
        for k in range(8):
            P.dma("sync", xc[:, CH * k:CH * k + n], x_ext[k, :, col0:col0 + n], ld)
        P.wait("vector", ld)
        P.wait("scalar", ld)
        for k in range(8):
            P.do("vector", lambda e, k=k: e.tensor_copy(
                out=xbf[:, CH * k:CH * k + n], in_=xc[:, CH * k:CH * k + n]), inc=vv)
            P.do("scalar", lambda e, k=k: e.activation(
                out=sq[:, CH * k:CH * k + n], in_=xc[:, CH * k:CH * k + n],
                func=AF.Square), inc=aa)
        P.wait("tensor", vv, aa, gd)
        for k in range(8):
            P.do("tensor", lambda e, k=k: e.matmul(
                ps_s1[0:1, 0:n], ones_bf[:, :], xbf[:, CH * k:CH * k + n],
                start=(k == 0), stop=(k == 7)), inc=(tt if k == 7 else None))
        for k in range(8):
            P.do("tensor", lambda e, k=k: e.matmul(
                ps_s2[0:1, 0:n], ones_bf[:, :], sq[:, CH * k:CH * k + n],
                start=(k == 0), stop=(k == 7)), inc=(tt if k == 7 else None))
        P.wait("vector", tt)
        P.do("vector", lambda e: e.tensor_scalar(
            out=row(MU)[:, 0:n], in0=ps_s1[0:1, 0:n], scalar1=1.0 / D,
            scalar2=None, op0=ALU.mult), inc=vv)
        P.do("vector", lambda e: e.tensor_scalar(
            out=row(EX2)[:, 0:n], in0=ps_s2[0:1, 0:n], scalar1=1.0 / D,
            scalar2=None, op0=ALU.mult), inc=vv)
        P.wait("vector", vv)
        P.do("vector", lambda e: e.tensor_mul(
            out=row(VAR)[:, 0:n], in0=row(MU)[:, 0:n], in1=row(MU)[:, 0:n]), inc=vv)
        P.wait("vector", vv)
        P.do("vector", lambda e: e.tensor_sub(
            out=row(VAR)[:, 0:n], in0=row(EX2)[:, 0:n], in1=row(VAR)[:, 0:n]), inc=vv)
        P.wait("scalar", vv)
        P.do("scalar", lambda e: e.activation(
            out=row(RSTD)[:, 0:n], in_=row(VAR)[:, 0:n], func=AF.Ln, bias=cst[0:1, 3:4]), inc=aa)
        P.wait("scalar", aa)
        P.do("scalar", lambda e: e.activation(
            out=row(RSTD)[:, 0:n], in_=row(RSTD)[:, 0:n], func=AF.Exp, scale=cst[0:1, 2:3]), inc=aa)
        P.wait("tensor", aa)
        P.do("tensor", lambda e: e.matmul(
            ps_r1[:, 0:n], oner_f[:, :], row(MU)[:, 0:n], start=True, stop=True), inc=tt)
        P.do("tensor", lambda e: e.matmul(
            ps_r2[:, 0:n], oner_f[:, :], row(RSTD)[:, 0:n], start=True, stop=True), inc=tt)
        P.wait("vector", tt)
        for k in range(8):
            P.do("vector", lambda e, k=k: e.tensor_sub(
                out=xnt[:, CH * k:CH * k + n], in0=xc[:, CH * k:CH * k + n],
                in1=ps_r1[:, 0:n]), inc=vv)
        P.wait("vector", vv)
        for k in range(8):
            P.do("vector", lambda e, k=k: e.tensor_mul(
                out=xnt[:, CH * k:CH * k + n], in0=xnt[:, CH * k:CH * k + n],
                in1=ps_r2[:, 0:n]), inc=vv)
        if not halo:
            P.wait("sync", vv)
            for k in range(8):
                P.dma("sync", xn_o[k, :, col0 - NH:col0 - NH + n],
                      xnt[:, CH * k:CH * k + n], st)
        P.wait("scalar", vv)
        for k in range(8):
            P.do("scalar", lambda e, k=k: e.activation(
                out=sq[:, CH * k:CH * k + n], in_=xnt[:, CH * k:CH * k + n],
                func=AF.Square), inc=aa)
        P.wait("tensor", aa, vv)
        for k in range(8):
            P.do("tensor", lambda e, k=k: e.matmul(
                ps_s1[0:1, 0:n], ones_bf[:, :], sq[:, CH * k:CH * k + n],
                start=(k == 0), stop=(k == 7)), inc=(tt if k == 7 else None))
        P.wait("vector", tt)
        P.do("vector", lambda e: e.tensor_scalar(
            out=row(MS)[:, 0:n], in0=ps_s1[0:1, 0:n], scalar1=1.0 / D,
            scalar2=None, op0=ALU.mult), inc=vv)
        P.wait("scalar", vv)
        P.do("scalar", lambda e: e.activation(
            out=row(RSTD2)[:, 0:n], in_=row(MS)[:, 0:n], func=AF.Ln, bias=cst[0:1, 3:4]), inc=aa)
        P.wait("scalar", aa)
        P.do("scalar", lambda e: e.activation(
            out=row(RSTD2)[:, 0:n], in_=row(RSTD2)[:, 0:n], func=AF.Exp, scale=cst[0:1, 2:3]), inc=aa)
        P.wait("tensor", aa)
        P.do("tensor", lambda e: e.matmul(
            ps_r1[:, 0:n], oner_f[:, :], row(RSTD2)[:, 0:n], start=True, stop=True), inc=tt)
        P.wait("vector", tt)
        for k in range(8):
            P.do("vector", lambda e, k=k: e.tensor_mul(
                out=ht[:, CH * k:CH * k + n], in0=xnt[:, CH * k:CH * k + n],
                in1=ps_r1[:, 0:n]), inc=vv)

    def in_proj_chunk(n, c, halo):
        irange = range(16) if halo else range(32)
        P.wait("tensor", vv)
        for i in irange:
            pm = (ps_a, ps_b)[i % 2]
            for k in range(8):
                P.do("tensor", lambda e, i=i, k=k, pm=pm: e.matmul(
                    pm[:, 0:n], w_in[:, 4096 * k + 128 * i:4096 * k + 128 * (i + 1)],
                    ht[:, CH * k:CH * k + n], start=(k == 0), stop=(k == 7)),
                    inc=(tt if k == 7 else None))
            P.wait("vector", tt)
            if i < 16:
                off = CE * i + (0 if halo else NH)
                P.do("vector", lambda e, i=i, off=off, pm=pm: e.tensor_copy(
                    out=xs_ext[:, off:off + n], in_=pm[:, 0:n]), inc=vv)
            else:
                g = i - 16
                rb = rbuf[:, CH * (i % 2):CH * (i % 2) + n]
                P.wait("vector", st)
                P.do("vector", lambda e, rb=rb, pm=pm: e.tensor_copy(
                    out=rb, in_=pm[:, 0:n]), inc=vv)
                P.wait("sync", vv)
                P.dma("sync", res_o[g, :, CH * c:CH * c + n], rb, st)
            P.wait("tensor", vv)

    def conv_u_chunk(n, c):
        for g in range(16):
            base = CE * g
            P.do("vector", lambda e, g=g, base=base: e.tensor_scalar(
                out=convt[:, 0:n], in0=xs_ext[:, base:base + n],
                scalar1=cw[:, 4 * g:4 * g + 1], scalar2=None, op0=ALU.mult), inc=vv)
            for m in (1, 2, 3):
                P.wait("vector", vv)
                P.do("vector", lambda e, g=g, base=base, m=m: e.scalar_tensor_tensor(
                    out=convt[:, 0:n], in0=xs_ext[:, base + m:base + m + n],
                    scalar=cw[:, 4 * g + m:4 * g + m + 1], in1=convt[:, 0:n],
                    op0=ALU.mult, op1=ALU.add), inc=vv)
            P.wait("scalar", vv)
            P.do("scalar", lambda e, g=g: e.activation(
                out=u_full[:, CH * g:CH * g + n], in_=convt[:, 0:n],
                func=AF.Silu, bias=cb[:, g:g + 1]), inc=aa)
            P.wait("sync", aa)
            P.dma("sync", u_o[g, :, CH * c:CH * c + n], u_full[:, CH * g:CH * g + n], st)
            P.wait("vector", aa)
            P.do("vector", lambda e, g=g, base=base: e.tensor_copy(
                out=xs_ext[:, base:base + NH],
                in_=xs_ext[:, base + CH:base + CH + NH]), inc=vv)

    def xdbl_chunk(n, c, sub='all'):
        P.wait("tensor", aa, vv)
        for g in range(16):
            P.do("tensor", lambda e, g=g: e.matmul(
                ps_x[:, 0:n], w_xp[:, 96 * g:96 * (g + 1)],
                u_full[:, CH * g:CH * g + n], start=(g == 0), stop=(g == 15)),
                inc=(tt if g == 15 else None))
        P.wait("vector", tt)
        P.do("vector", lambda e: e.tensor_copy(out=bc_sb[:, 0:n], in_=ps_x[64:96, 0:n]), inc=vv)
        P.do("vector", lambda e: e.tensor_copy(out=din_bf[:, 0:n], in_=ps_x[0:64, 0:n]), inc=vv)
        P.wait("sync", vv)
        P.dma("sync", bc_o[:, CH * c:CH * c + n], bc_sb[:, 0:n], st)
        if sub == 'xd':
            return
        P.wait("tensor", vv)
        for i in range(16):
            pm = (ps_a, ps_b)[i % 2]
            P.do("tensor", lambda e, i=i, pm=pm: e.matmul(
                pm[:, 0:n], w_dt[:, 128 * i:128 * (i + 1)], din_bf[:, 0:n],
                start=True, stop=True), inc=tt)
            P.wait("scalar", tt)
            P.wait("vector", tt)
            if sub == 'dtmm':
                P.wait("tensor", tt)
                continue
            P.do("scalar", lambda e, i=i, pm=pm: e.activation(
                out=sp1[:, 0:n], in_=pm[:, 0:n], func=AF.Abs,
                bias=dtb[:, i:i + 1]), inc=aa)
            if sub == 'spa':
                P.wait("tensor", aa)
                continue
            P.wait("scalar", aa)
            P.do("scalar", lambda e: e.activation(
                out=sp1[:, 0:n], in_=sp1[:, 0:n], func=AF.Exp, scale=cst[:, 1:2]), inc=aa)
            P.wait("scalar", aa)
            P.do("scalar", lambda e: e.activation(
                out=sp1[:, 0:n], in_=sp1[:, 0:n], func=AF.Ln, bias=cst[:, 0:1]), inc=aa)
            if sub == 'spc':
                P.wait("tensor", aa)
                continue
            P.wait("vector", aa)
            P.do("vector", lambda e, i=i, pm=pm: e.tensor_scalar(
                out=sp2[:, 0:n], in0=pm[:, 0:n], scalar1=dtb[:, i:i + 1],
                scalar2=cst[:, 5:6], op0=ALU.add, op1=ALU.max), inc=vv)
            if sub == 'spv':
                P.wait("tensor", vv, aa)
                continue
            P.wait("vector", aa, st, vv)
            db = dbuf[:, CH * (i % 2):CH * (i % 2) + n]
            P.do("vector", lambda e, db=db: e.tensor_add(
                out=db, in0=sp1[:, 0:n], in1=sp2[:, 0:n]), inc=vv)
            P.wait("sync", vv)
            P.dma("sync", dl_o[i, :, CH * c:CH * c + n], db, st)
            P.wait("tensor", vv, aa)

    order = ["w", "ln0", "ip0", "ln", "ip", "conv", "xdbl"]
    lim = order.index(upto) if upto in order else len(order)
    if lim >= 1:
        ln_rms_chunk(NH, 0, halo=True)
    if lim >= 2:
        in_proj_chunk(NH, -1, halo=True)
    for c in range(NCH if nch is None else nch):
        if lim >= 3:
            ln_rms_chunk(CH, NH + CH * c, halo=False)
        if lim >= 4:
            in_proj_chunk(CH, c, halo=False)
        if lim >= 5:
            conv_u_chunk(CH, c)
        if lim >= 6:
            xdbl_chunk(CH, c, sub=(upto_sub if lim >= 6 else 'all'))
    P.wait("sync", st)
    return P.finish()


def _pad3(xdt_slice, prev):
    xe = np.concatenate([prev, xdt_slice], axis=1)
    return np.ascontiguousarray(xe.reshape(8, 128, 3 + TOKC))


def host_front_inputs(L0_attn_x, L0_attn_norm_w, L0_attn_in_proj_w, L0_attn_conv_w,
                      L0_attn_conv_b, L0_attn_x_proj_w, L0_attn_dt_proj_w,
                      L0_attn_dt_proj_b):
    x_dt = np.ascontiguousarray(L0_attn_x.reshape(NTOK, D).T)
    inpw_t = np.ascontiguousarray(L0_attn_in_proj_w.T.reshape(8, 128, 2 * DI))
    norm_w = np.ascontiguousarray(L0_attn_norm_w.reshape(8, 128, 1))
    conv_wa = np.ascontiguousarray(
        L0_attn_conv_w.reshape(DI, 4).reshape(16, 128, 4).transpose(1, 0, 2).reshape(128, 64))
    conv_ba = np.ascontiguousarray(L0_attn_conv_b.reshape(16, 128).T)
    xpw_t = np.ascontiguousarray(L0_attn_x_proj_w.T.reshape(16, 128, 96))
    dtpw_t = np.ascontiguousarray(L0_attn_dt_proj_w.T)
    dtb_a = np.ascontiguousarray(L0_attn_dt_proj_b.reshape(16, 128).T)
    ones_c = np.ones((128, 1), np.float32)
    ones_r = np.ones((1, 128), np.float32)
    cst_arr = np.tile(np.array([1.0, -1.0, -0.5, LN_EPS, 1.0 / 32, 0, 0, 0],
                               np.float32), (128, 1))
    maps = []
    for c in range(NC):
        lo = c * TOKC
        if lo % T == 0:
            prev = np.zeros((D, 3), np.float32)
        else:
            prev = x_dt[:, lo - 3:lo]
        maps.append({
            "x_ext": _pad3(x_dt[:, lo:lo + TOKC], prev),
            "inpw_t": inpw_t, "norm_w": norm_w, "conv_wa": conv_wa,
            "conv_ba": conv_ba, "xpw_t": xpw_t, "dtpw_t": dtpw_t, "dtb_a": dtb_a,
            "ones_c": ones_c, "ones_r": ones_r,
            "cst_i": cst_arr,
        })
    return maps


def run_front(inputs):
    nc = build_front()
    maps = host_front_inputs(
        inputs["L0_attn_x"], inputs["L0_attn_norm_w"], inputs["L0_attn_in_proj_w"],
        inputs["L0_attn_conv_w"], inputs["L0_attn_conv_b"], inputs["L0_attn_x_proj_w"],
        inputs["L0_attn_dt_proj_w"], inputs["L0_attn_dt_proj_b"])
    res = run_bass_kernel_spmd(nc, maps, core_ids=list(range(NC)))
    out = {}
    for key, nt, dt in [("xn_o", 8, np.float32), ("res_o", 16, BF), ("u_o", 16, BF),
                        ("dl_o", 16, BF), ("bc_o", None, np.float32)]:
        if nt is None:
            out[key] = np.concatenate([res.results[c][key] for c in range(NC)], axis=1)
        else:
            out[key] = np.concatenate(
                [res.results[c][key].reshape(nt * 128, TOKC) for c in range(NC)], axis=1)
    return out


def kernel(**inputs):
    raise NotImplementedError("in progress")


def build_scan():
    nc = _nc()
    P = Prog(nc)
    TC = 512
    NTC = T // TC

    dl_i = nc.dram_tensor("dl_i", [B, 2, 128, T], BF16, kind="ExternalInput").ap()
    u_i = nc.dram_tensor("u_i", [B, 2, 128, T], BF16, kind="ExternalInput").ap()
    bc_i = nc.dram_tensor("bc_i", [B, 32, T], F32, kind="ExternalInput").ap()
    alog_a = nc.dram_tensor("alog_a", [128, 32], F32, kind="ExternalInput").ap()
    d_i = nc.dram_tensor("d_i", [128, 2], F32, kind="ExternalInput").ap()
    sel_i = nc.dram_tensor("sel_i", [16, 128, 128], F32, kind="ExternalInput").ap()
    nsel_i = nc.dram_tensor("nsel_i", [16, 128, 128], F32, kind="ExternalInput").ap()
    sbsel_i = nc.dram_tensor("sbsel_i", [16, 128], F32, kind="ExternalInput").ap()
    y_o = nc.dram_tensor("y_o", [B, 2, 128, T], BF16, kind="ExternalOutput").ap()

    dl_sb = P.sb("dl_sb", (128, 2 * T), BF16)
    u_sb = P.sb("u_sb", (128, 2 * T), BF16)
    du_sb = P.sb("du_sb", (128, 2 * T), BF16)
    brep = P.sb("brep", (128, T), BF16)
    crep = P.sb("crep", (128, T), BF16)
    brow = P.sb("brow", (16, T), BF16)
    crow = P.sb("crow", (16, T), BF16)
    sel = P.sb("sel", (128, 16 * 128), BF16)
    nsel = P.sb("nsel", (128, 16 * 128), BF16)
    sbsel = P.sb("sbsel", (16, 128), BF16)
    seltmp = P.sb("seltmp", (128, 16 * 128), F32)
    acol = P.sb("acol", (128, 32), F32)
    dcol = P.sb("dcol", (128, 2), F32)
    state = P.sb("state", (128, 32), F32)
    da_sb = P.sb("da_sb", (128, TC), F32)
    dbu_sb = P.sb("dbu_sb", (128, TC), F32)
    h_sb = P.sb("h_sb", (128, TC), F32)
    hc_sb = P.sb("hc_sb", (128, TC), BF16)
    ybuf = P.sb("ybuf", (128, 2 * TC), BF16)

    ps_dr = P.ps("ps_dr", (128, TC))
    ps_ur = P.ps("ps_ur", (128, TC))
    ps_y = P.ps("ps_y", (128, TC))
    ps_br = P.ps("ps_br", (128, TC))

    ld = P.sem("ld")
    gd = P.sem("gd")
    vv = P.sem("vv")
    aa = P.sem("aa")
    tt = P.sem("tt")
    st = P.sem("st")

    for g in range(16):
        P.dma("sync", seltmp[:, 128 * g:128 * (g + 1)], sel_i[g, :, :], ld, fence=False)
    P.fence("sync", ld)
    P.wait("vector", ld)
    P.do("vector", lambda e: e.tensor_copy(out=sel[:, :], in_=seltmp[:, :]), inc=vv)
    P.wait("sync", vv)
    for g in range(16):
        P.dma("sync", seltmp[:, 128 * g:128 * (g + 1)], nsel_i[g, :, :], ld, fence=False)
    P.fence("sync", ld)
    P.wait("vector", ld)
    P.do("vector", lambda e: e.tensor_copy(out=nsel[:, :], in_=seltmp[:, :]), inc=vv)
    P.wait("sync", vv)
    P.dma("sync", seltmp[0:16, 0:128], sbsel_i, ld)
    P.wait("vector", ld)
    P.do("vector", lambda e: e.tensor_copy(out=sbsel[:, :], in_=seltmp[0:16, 0:128]), inc=vv)
    P.dma("sync", acol[:, :], alog_a, ld)
    P.dma("sync", dcol[:, :], d_i, ld)
    P.wait("scalar", ld)
    P.do("scalar", lambda e: e.activation(out=acol[:, :], in_=acol[:, :], func=AF.Exp), inc=aa)
    P.wait("vector", aa)
    P.do("vector", lambda e: e.tensor_scalar(
        out=acol[:, :], in0=acol[:, :], scalar1=-1.0, scalar2=None, op0=ALU.mult), inc=vv)

    for b in range(B):
        P.wait("sync", vv, aa, tt, st)
        for r in range(2):
            P.dma("sync", dl_sb[:, T * r:T * (r + 1)], dl_i[b, r, :, :], ld, fence=False)
            P.dma("sync", u_sb[:, T * r:T * (r + 1)], u_i[b, r, :, :], ld, fence=False)
        P.fence("sync", ld)
        P.wait("gpsimd", tt, vv)
        P.dma("gpsimd", brow[:, :], bc_i[b, 0:16, :], gd, fence=False)
        P.dma("gpsimd", crow[:, :], bc_i[b, 16:32, :], gd)
        P.wait("vector", ld, gd)
        for r in range(2):
            P.do("vector", lambda e, r=r: e.tensor_mul(
                out=du_sb[:, T * r:T * (r + 1)], in0=dl_sb[:, T * r:T * (r + 1)],
                in1=u_sb[:, T * r:T * (r + 1)]), inc=vv)
        P.wait("tensor", vv, ld, gd)
        for cc0 in range(NTC):
            lo = TC * cc0
            P.do("tensor", lambda e, lo=lo: e.matmul(
                ps_br[:, :], sbsel[:, :], brow[0:16, lo:lo + TC],
                start=True, stop=True), inc=tt)
            P.wait("vector", tt)
            P.do("vector", lambda e, lo=lo: e.tensor_copy(
                out=brep[:, lo:lo + TC], in_=ps_br[:, :]), inc=vv)
            P.wait("tensor", vv)
            P.do("tensor", lambda e, lo=lo: e.matmul(
                ps_br[:, :], sbsel[:, :], crow[0:16, lo:lo + TC],
                start=True, stop=True), inc=tt)
            P.wait("vector", tt)
            P.do("vector", lambda e, lo=lo: e.tensor_copy(
                out=crep[:, lo:lo + TC], in_=ps_br[:, :]), inc=vv)
            P.wait("tensor", vv)
        P.do("vector", lambda e: e.memset(state[:, :], 0.0), inc=vv)
        for cc0 in range(NTC):
            lo = TC * cc0
            for r in range(2):
                for gg in range(16):
                    g = 16 * r + gg
                    P.wait("tensor", aa)
                    P.do("tensor", lambda e, r=r, gg=gg, lo=lo: e.matmul(
                        ps_dr[:, :], sel[:, 128 * gg:128 * (gg + 1)],
                        dl_sb[:, T * r + lo:T * r + lo + TC],
                        start=True, stop=True), inc=tt)
                    P.wait("tensor", vv)
                    P.do("tensor", lambda e, r=r, gg=gg, lo=lo: e.matmul(
                        ps_ur[:, :], sel[:, 128 * gg:128 * (gg + 1)],
                        du_sb[:, T * r + lo:T * r + lo + TC],
                        start=True, stop=True), inc=tt)
                    P.wait("scalar", tt, vv)
                    P.do("scalar", lambda e, g=g: e.activation(
                        out=da_sb[:, :], in_=ps_dr[:, :], func=AF.Exp,
                        scale=acol[:, g:g + 1]), inc=aa)
                    P.wait("vector", tt)
                    P.do("vector", lambda e, lo=lo: e.tensor_mul(
                        out=dbu_sb[:, :], in0=ps_ur[:, :], in1=brep[:, lo:lo + TC]), inc=vv)
                    P.wait("vector", aa, vv)
                    P.do("vector", lambda e, g=g: e.tensor_tensor_scan(
                        out=h_sb[:, :], data0=da_sb[:, :], data1=dbu_sb[:, :],
                        initial=state[:, g:g + 1], op0=ALU.mult, op1=ALU.add), inc=vv)
                    P.wait("vector", vv)
                    P.do("vector", lambda e, g=g: e.tensor_copy(
                        out=state[:, g:g + 1], in_=h_sb[:, TC - 1:TC]), inc=vv)
                    P.do("vector", lambda e, lo=lo: e.tensor_mul(
                        out=hc_sb[:, :], in0=h_sb[:, :], in1=crep[:, lo:lo + TC]), inc=vv)
                    P.wait("tensor", vv)
                    P.do("tensor", lambda e, gg=gg: e.matmul(
                        ps_y[:, :], nsel[:, 128 * gg:128 * (gg + 1)], hc_sb[:, :],
                        start=(gg == 0), stop=(gg == 15)),
                        inc=(tt if gg == 15 else None))
                P.wait("vector", tt, st)
                yb = ybuf[:, TC * r:TC * r + TC]
                P.do("vector", lambda e, r=r, lo=lo, yb=yb: e.scalar_tensor_tensor(
                    out=yb, in0=u_sb[:, T * r + lo:T * r + lo + TC],
                    scalar=dcol[:, r:r + 1], in1=ps_y[:, :],
                    op0=ALU.mult, op1=ALU.add), inc=vv)
                P.wait("sync", vv)
                P.dma("sync", y_o[b, r, :, lo:lo + TC], yb, st)
                P.wait("tensor", vv)
    P.wait("sync", st)
    return P.finish()


def host_scan_inputs(dl_full, u_full, bc_full, A_log, Dv):
    sel = np.zeros((16, 128, 128), np.float32)
    for p in range(16):
        for dh in range(8):
            for n in range(16):
                sel[p, 8 * p + dh, 16 * dh + n] = 1.0
    nselm = np.ascontiguousarray(np.transpose(sel, (0, 2, 1)))
    sbsel = np.zeros((16, 128), np.float32)
    for n in range(16):
        for dh in range(8):
            sbsel[n, 16 * dh + n] = 1.0
    maps = []
    for j in range(NC):
        ch0 = 256 * j
        dl_c = np.moveaxis(dl_full[ch0:ch0 + 256].reshape(2, 128, B, T), 2, 0)
        u_c = np.moveaxis(u_full[ch0:ch0 + 256].reshape(2, 128, B, T), 2, 0)
        bc_c = np.moveaxis(bc_full.reshape(32, B, T), 1, 0)
        alog_a = np.zeros((128, 32), np.float32)
        for g in range(32):
            for dh in range(8):
                for n in range(16):
                    alog_a[16 * dh + n, g] = A_log[ch0 + 8 * g + dh, n]
        d_a = Dv[ch0:ch0 + 256].reshape(2, 128).T
        maps.append({
            "dl_i": np.ascontiguousarray(dl_c), "u_i": np.ascontiguousarray(u_c),
            "bc_i": np.ascontiguousarray(bc_c), "alog_a": alog_a,
            "d_i": np.ascontiguousarray(d_a), "sel_i": sel, "nsel_i": nselm,
            "sbsel_i": sbsel,
        })
    return maps


def run_scan(front_out, inputs):
    nc = build_scan()
    maps = host_scan_inputs(front_out["dl_o"], front_out["u_o"], front_out["bc_o"],
                            inputs["L0_attn_A_log"], inputs["L0_attn_D"])
    res = run_bass_kernel_spmd(nc, maps, core_ids=list(range(NC)))
    ys = []
    for j in range(NC):
        y = res.results[j]["y_o"]
        ys.append(np.moveaxis(y, 0, 2).reshape(256, NTOK))
    return np.concatenate(ys, axis=0)


def build_mid():
    nc = _nc()
    P = Prog(nc)
    CHM = 256
    NCH = TOKC // CHM
    E, HH = 8, 256
    NT4 = CHM // 128

    x_i = nc.dram_tensor("x_i", [8, 128, TOKC], F32, kind="ExternalInput").ap()
    xn_i = nc.dram_tensor("xn_i", [8, 128, TOKC], F32, kind="ExternalInput").ap()
    res_i = nc.dram_tensor("res_i", [16, 128, TOKC], BF16, kind="ExternalInput").ap()
    y_i = nc.dram_tensor("y_i", [16, 128, TOKC], BF16, kind="ExternalInput").ap()
    opw_t = nc.dram_tensor("opw_t", [16, 128, D], F32, kind="ExternalInput").ap()
    wg_t = nc.dram_tensor("wg_t", [8, 128, E], F32, kind="ExternalInput").ap()
    lnsc_i = nc.dram_tensor("lnsc_i", [8, 128, E], F32, kind="ExternalInput").ap()
    lnbi_i = nc.dram_tensor("lnbi_i", [8, 128, E], F32, kind="ExternalInput").ap()
    w1_i = nc.dram_tensor("w1_i", [E, 8, 128, HH], F32, kind="ExternalInput").ap()
    b1_i = nc.dram_tensor("b1_i", [128, 16], F32, kind="ExternalInput").ap()
    w2_i = nc.dram_tensor("w2_i", [E, 2, 128, D], F32, kind="ExternalInput").ap()
    b2_i = nc.dram_tensor("b2_i", [1, E * D], F32, kind="ExternalInput").ap()
    gw_t = nc.dram_tensor("gw_t", [8, 128, E], F32, kind="ExternalInput").ap()
    ident_i = nc.dram_tensor("ident_i", [128, 128], F32, kind="ExternalInput").ap()
    ones_c = nc.dram_tensor("ones_c", [128, 1], F32, kind="ExternalInput").ap()
    ones_r = nc.dram_tensor("ones_r", [1, 128], F32, kind="ExternalInput").ap()
    cst_i = nc.dram_tensor("cst_i", [128, 8], F32, kind="ExternalInput").ap()

    x1_o = nc.dram_tensor("x1_o", [8, 128, TOKC], F32, kind="ExternalOutput").ap()
    xn1_o = nc.dram_tensor("xn1_o", [8, 128, TOKC], BF16, kind="ExternalOutput").ap()
    aff_o = nc.dram_tensor("aff_o", [16, 128, E], F32, kind="ExternalOutput").ap()

    w_op = P.sb("w_op", (128, 16 * D), BF16)
    w1 = P.sb("w1", (128, E * 8 * HH), BF16)
    w2 = P.sb("w2", (128, E * 2 * D), BF16)
    b2r = P.sb("b2r", (1, E * D), BF16)
    wgf = P.sb("wgf", (128, 8 * E), F32)
    gwf = P.sb("gwf", (128, 8 * E), F32)
    lnsc = P.sb("lnsc", (128, 8 * E), F32)
    lnbi = P.sb("lnbi", (128, 8 * E), F32)
    b1c = P.sb("b1c", (128, 16), F32)
    ident = P.sb("ident", (128, 128), BF16)
    wtmp = P.sb("wtmp", (128, D), F32)
    ones_bf = P.sb("ones_bf", (128, 1), BF16)
    oner_f = P.sb("oner_f", (1, 128), F32)
    oner_b = P.sb("oner_b", (1, 128), BF16)
    cst = P.sb("cst", (128, 8), F32)

    xc = P.sb("xc", (128, 8 * CHM), F32)
    xnc = P.sb("xnc", (128, 8 * CHM), F32)
    resc = P.sb("resc", (128, 16 * CHM), BF16)
    yc = P.sb("yc", (128, 16 * CHM), BF16)
    xnb = P.sb("xnb", (128, 8 * CHM), BF16)
    sq = P.sb("sq", (128, 8 * CHM), BF16)
    tmpf = P.sb("tmpf", (128, 8 * CHM), F32)
    mrow = P.sb("mrow", (1, E * CHM), BF16)
    mask_sb = P.sb("mask_sb", (128, NT4 * E), BF16)
    nmx = P.sb("nmx", (128, NT4), F32)
    rsum = P.sb("rsum", (128, NT4), F32)
    affc = P.sb("affc", (128, NT4 * E), F32)
    srow = P.sb("srow", (1, 6 * CHM), F32)
    xn1b = P.sb("xn1b", (128, 8 * CHM), BF16)

    ps_a = P.ps("ps_a", (128, CHM))
    ps_b = P.ps("ps_b", (128, CHM))
    ps_r1 = P.ps("ps_r1", (128, CHM))
    ps_r2 = P.ps("ps_r2", (128, CHM))
    ps_l = P.ps("ps_l", (128, E))
    ps_m = P.ps("ps_m", (1, 1024), BF16)
    ps_mr = P.ps("ps_mr", (128, CHM))

    ld = P.sem("ld")
    gd = P.sem("gd")
    vv = P.sem("vv")
    aa = P.sem("aa")
    tt = P.sem("tt")
    st = P.sem("st")

    MU, EX2, VAR, RSTD = range(4)

    def row(i):
        return srow[0:1, CHM * i:CHM * i + CHM]

    for k in range(16):
        P.dma("sync", wtmp[:, 0:D], opw_t[k, :, :], ld)
        P.wait("vector", ld)
        P.do("vector", lambda e, k=k: e.tensor_copy(
            out=w_op[:, D * k:D * (k + 1)], in_=wtmp[:, 0:D]), inc=vv)
        P.wait("sync", vv)
    for e_ in range(E):
        for k in range(8):
            P.dma("sync", wtmp[:, 0:HH], w1_i[e_, k, :, :], ld)
            P.wait("vector", ld)
            P.do("vector", lambda e, e_=e_, k=k: e.tensor_copy(
                out=w1[:, (8 * e_ + k) * HH:(8 * e_ + k + 1) * HH], in_=wtmp[:, 0:HH]), inc=vv)
            P.wait("sync", vv)
        for k in range(2):
            P.dma("sync", wtmp[:, 0:D], w2_i[e_, k, :, :], ld)
            P.wait("vector", ld)
            P.do("vector", lambda e, e_=e_, k=k: e.tensor_copy(
                out=w2[:, (2 * e_ + k) * D:(2 * e_ + k + 1) * D], in_=wtmp[:, 0:D]), inc=vv)
            P.wait("sync", vv)
    P.dma("gpsimd", b2r[:, :], b2_i, gd, fence=False)
    P.dma("gpsimd", ident[:, :], ident_i, gd, fence=False)
    P.dma("gpsimd", ones_bf[:, :], ones_c, gd, fence=False)
    P.dma("gpsimd", oner_b[:, :], ones_r, gd)
    for k in range(8):
        P.dma("sync", wgf[:, E * k:E * (k + 1)], wg_t[k, :, :], ld, fence=False)
        P.dma("sync", gwf[:, E * k:E * (k + 1)], gw_t[k, :, :], ld, fence=False)
        P.dma("sync", lnsc[:, E * k:E * (k + 1)], lnsc_i[k, :, :], ld, fence=False)
        P.dma("sync", lnbi[:, E * k:E * (k + 1)], lnbi_i[k, :, :], ld, fence=False)
    P.dma("sync", b1c[:, :], b1_i, ld, fence=False)
    P.dma("sync", oner_f[:, :], ones_r, ld, fence=False)
    P.dma("sync", cst[:, :], cst_i, ld)

    def stats_ln(src_f32, n, out_bf):
        P.wait("scalar", vv, tt)
        for k in range(8):
            P.do("scalar", lambda e, k=k: e.activation(
                out=sq[:, CHM * k:CHM * k + n], in_=src_f32[:, CHM * k:CHM * k + n],
                func=AF.Square), inc=aa)
        P.wait("vector", vv, tt)
        for k in range(8):
            P.do("vector", lambda e, k=k: e.tensor_copy(
                out=xnb[:, CHM * k:CHM * k + n],
                in_=src_f32[:, CHM * k:CHM * k + n]), inc=vv)
        P.wait("tensor", vv, aa, gd)
        for k in range(8):
            P.do("tensor", lambda e, k=k: e.matmul(
                ps_a[0:1, 0:n], ones_bf[:, :], xnb[:, CHM * k:CHM * k + n],
                start=(k == 0), stop=(k == 7)), inc=(tt if k == 7 else None))
        for k in range(8):
            P.do("tensor", lambda e, k=k: e.matmul(
                ps_b[0:1, 0:n], ones_bf[:, :], sq[:, CHM * k:CHM * k + n],
                start=(k == 0), stop=(k == 7)), inc=(tt if k == 7 else None))
        P.wait("vector", tt)
        P.do("vector", lambda e: e.tensor_scalar(
            out=row(MU)[:, 0:n], in0=ps_a[0:1, 0:n], scalar1=1.0 / D,
            scalar2=None, op0=ALU.mult), inc=vv)
        P.do("vector", lambda e: e.tensor_scalar(
            out=row(EX2)[:, 0:n], in0=ps_b[0:1, 0:n], scalar1=1.0 / D,
            scalar2=None, op0=ALU.mult), inc=vv)
        P.wait("vector", vv)
        P.do("vector", lambda e: e.tensor_mul(
            out=row(VAR)[:, 0:n], in0=row(MU)[:, 0:n], in1=row(MU)[:, 0:n]), inc=vv)
        P.wait("vector", vv)
        P.do("vector", lambda e: e.tensor_sub(
            out=row(VAR)[:, 0:n], in0=row(EX2)[:, 0:n], in1=row(VAR)[:, 0:n]), inc=vv)
        P.wait("scalar", vv)
        P.do("scalar", lambda e: e.activation(
            out=row(RSTD)[:, 0:n], in_=row(VAR)[:, 0:n], func=AF.Ln,
            bias=cst[0:1, 3:4]), inc=aa)
        P.wait("scalar", aa)
        P.do("scalar", lambda e: e.activation(
            out=row(RSTD)[:, 0:n], in_=row(RSTD)[:, 0:n], func=AF.Exp,
            scale=cst[0:1, 2:3]), inc=aa)
        P.wait("tensor", aa, vv)
        P.do("tensor", lambda e: e.matmul(
            ps_r1[:, 0:n], oner_f[:, :], row(MU)[:, 0:n], start=True, stop=True), inc=tt)
        P.do("tensor", lambda e: e.matmul(
            ps_r2[:, 0:n], oner_f[:, :], row(RSTD)[:, 0:n], start=True, stop=True), inc=tt)
        P.wait("vector", tt)
        for k in range(8):
            P.do("vector", lambda e, k=k: e.tensor_sub(
                out=tmpf[:, CHM * k:CHM * k + n],
                in0=src_f32[:, CHM * k:CHM * k + n], in1=ps_r1[:, 0:n]), inc=vv)
        P.wait("vector", vv)
        for k in range(8):
            P.do("vector", lambda e, k=k: e.tensor_mul(
                out=tmpf[:, CHM * k:CHM * k + n], in0=tmpf[:, CHM * k:CHM * k + n],
                in1=ps_r2[:, 0:n]), inc=vv)
        P.wait("vector", vv)
        for k in range(8):
            P.do("vector", lambda e, k=k: e.tensor_copy(
                out=out_bf[:, CHM * k:CHM * k + n], in_=tmpf[:, CHM * k:CHM * k + n]), inc=vv)

    xhat = resc
    h1m = yc
    inpe = xnb
    xn1f = xnc

    for c in range(NCH):
        co = CHM * c
        P.wait("sync", vv, aa, tt, st)
        for k in range(8):
            P.dma("sync", xc[:, CHM * k:CHM * (k + 1)], x_i[k, :, co:co + CHM], ld, fence=False)
            P.dma("sync", xnc[:, CHM * k:CHM * (k + 1)], xn_i[k, :, co:co + CHM], ld, fence=False)
        for g in range(16):
            P.dma("sync", resc[:, CHM * g:CHM * (g + 1)], res_i[g, :, co:co + CHM], ld, fence=False)
            P.dma("sync", yc[:, CHM * g:CHM * (g + 1)], y_i[g, :, co:co + CHM], ld, fence=False)
        P.fence("sync", ld)
        for t4 in range(NT4):
            P.wait("tensor", vv, aa, ld)
            for k in range(8):
                P.do("tensor", lambda e, t4=t4, k=k: e.matmul(
                    ps_l[:, :], xnc[:, CHM * k + 128 * t4:CHM * k + 128 * (t4 + 1)],
                    wgf[:, E * k:E * (k + 1)], start=(k == 0), stop=(k == 7)),
                    inc=(tt if k == 7 else None))
            P.wait("vector", tt)
            P.do("vector", lambda e, t4=t4: e.reduce_max(
                out=nmx[:, t4:t4 + 1], in_=ps_l[:, :], axis=AX.X), inc=vv)
            P.wait("vector", vv)
            P.do("vector", lambda e, t4=t4: e.tensor_scalar(
                out=mask_sb[:, E * t4:E * (t4 + 1)], in0=ps_l[:, :],
                scalar1=nmx[:, t4:t4 + 1], scalar2=None, op0=ALU.is_ge), inc=vv)
            P.wait("tensor", vv, gd)
            for e_ in range(E):
                P.do("tensor", lambda e, t4=t4, e_=e_: e.transpose(
                    ps_m[0:1, 128 * e_:128 * (e_ + 1)],
                    mask_sb[:, E * t4 + e_:E * t4 + e_ + 1], ident[:, :]),
                    inc=(tt if e_ == E - 1 else None))
            P.wait("vector", tt)
            for e_ in range(E):
                P.do("vector", lambda e, t4=t4, e_=e_: e.tensor_copy(
                    out=mrow[0:1, CHM * e_ + 128 * t4:CHM * e_ + 128 * (t4 + 1)],
                    in_=ps_m[0:1, 128 * e_:128 * (e_ + 1)]), inc=vv)
            P.wait("tensor", vv)
        P.wait("vector", ld)
        for k in range(8):
            P.do("vector", lambda e, k=k: e.tensor_add(
                out=xc[:, CHM * k:CHM * (k + 1)], in0=xc[:, CHM * k:CHM * (k + 1)],
                in1=xnc[:, CHM * k:CHM * (k + 1)]), inc=vv)
        stats_ln(xnc, CHM, xn1b)
        P.wait("scalar", ld, vv)
        for g in range(16):
            P.do("scalar", lambda e, g=g: e.activation(
                out=resc[:, CHM * g:CHM * (g + 1)], in_=resc[:, CHM * g:CHM * (g + 1)],
                func=AF.Silu), inc=aa)
        P.wait("vector", aa)
        for g in range(16):
            P.do("vector", lambda e, g=g: e.tensor_mul(
                out=resc[:, CHM * g:CHM * (g + 1)], in0=resc[:, CHM * g:CHM * (g + 1)],
                in1=yc[:, CHM * g:CHM * (g + 1)]), inc=vv)
        P.wait("tensor", vv)
        for i in range(8):
            pm = (ps_a, ps_b)[i % 2]
            for k in range(16):
                P.do("tensor", lambda e, i=i, k=k, pm=pm: e.matmul(
                    pm[:, :], w_op[:, D * k + 128 * i:D * k + 128 * (i + 1)],
                    resc[:, CHM * k:CHM * (k + 1)], start=(k == 0), stop=(k == 15)),
                    inc=(tt if k == 15 else None))
            P.wait("vector", tt)
            P.do("vector", lambda e, i=i, pm=pm: e.tensor_add(
                out=xc[:, CHM * i:CHM * (i + 1)], in0=xc[:, CHM * i:CHM * (i + 1)],
                in1=pm[:, :]), inc=vv)
            P.wait("tensor", vv)
        P.wait("vector", vv)
        for k in range(8):
            P.do("vector", lambda e, k=k: e.tensor_copy(
                out=xhat[:, CHM * k:CHM * (k + 1)], in_=xn1b[:, CHM * k:CHM * (k + 1)]), inc=vv)
        for e_ in range(E):
            P.wait("vector", vv, tt)
            for k in range(8):
                P.do("vector", lambda e, e_=e_, k=k: e.tensor_scalar(
                    out=inpe[:, CHM * k:CHM * (k + 1)], in0=xhat[:, CHM * k:CHM * (k + 1)],
                    scalar1=lnsc[:, E * k + e_:E * k + e_ + 1],
                    scalar2=lnbi[:, E * k + e_:E * k + e_ + 1],
                    op0=ALU.mult, op1=ALU.add), inc=vv)
            P.wait("tensor", vv, aa)
            P.do("tensor", lambda e, e_=e_: e.matmul(
                ps_mr[:, :], oner_b[:, :], mrow[0:1, CHM * e_:CHM * (e_ + 1)],
                start=True, stop=True), inc=tt)
            for hi in range(2):
                pm = (ps_a, ps_b)[hi]
                for k in range(8):
                    P.do("tensor", lambda e, e_=e_, hi=hi, k=k, pm=pm: e.matmul(
                        pm[:, :], w1[:, (8 * e_ + k) * HH + 128 * hi:(8 * e_ + k) * HH + 128 * (hi + 1)],
                        inpe[:, CHM * k:CHM * (k + 1)], start=(k == 0), stop=(k == 7)),
                        inc=(tt if k == 7 else None))
                P.wait("scalar", tt)
                P.do("scalar", lambda e, e_=e_, hi=hi, pm=pm: e.activation(
                    out=h1m[:, CHM * (2 * e_ + hi):CHM * (2 * e_ + hi + 1)], in_=pm[:, :],
                    func=AF.Gelu, bias=b1c[:, 2 * e_ + hi:2 * e_ + hi + 1]), inc=aa)
            P.wait("vector", tt, aa)
            for hi in range(2):
                P.do("vector", lambda e, e_=e_, hi=hi: e.tensor_mul(
                    out=h1m[:, CHM * (2 * e_ + hi):CHM * (2 * e_ + hi + 1)],
                    in0=h1m[:, CHM * (2 * e_ + hi):CHM * (2 * e_ + hi + 1)],
                    in1=ps_mr[:, :]), inc=vv)
            P.wait("tensor", vv)
        for i in range(8):
            pm = (ps_a, ps_b)[i % 2]
            P.wait("tensor", vv, aa)
            nmm = 3 * E
            cnt = 0
            for e_ in range(E):
                for hi in range(2):
                    cnt += 1
                    s0, p0 = (cnt == 1), (cnt == nmm)
                    P.do("tensor", lambda e, i=i, e_=e_, hi=hi, pm=pm, s=s0, p=p0: e.matmul(
                        pm[:, :], w2[:, (2 * e_ + hi) * D + 128 * i:(2 * e_ + hi) * D + 128 * (i + 1)],
                        h1m[:, CHM * (2 * e_ + hi):CHM * (2 * e_ + hi + 1)],
                        start=s, stop=p), inc=(tt if p0 else None))
                cnt += 1
                p0 = (cnt == nmm)
                P.do("tensor", lambda e, i=i, e_=e_, pm=pm, p=p0: e.matmul(
                    pm[:, :], b2r[0:1, D * e_ + 128 * i:D * e_ + 128 * (i + 1)],
                    mrow[0:1, CHM * e_:CHM * (e_ + 1)], start=False, stop=p),
                    inc=(tt if p0 else None))
            P.wait("vector", tt)
            P.do("vector", lambda e, i=i, pm=pm: e.tensor_add(
                out=xc[:, CHM * i:CHM * (i + 1)], in0=xc[:, CHM * i:CHM * (i + 1)],
                in1=pm[:, :]), inc=vv)
            P.wait("tensor", vv)
        P.wait("sync", vv)
        for k in range(8):
            P.dma("sync", x1_o[k, :, co:co + CHM], xc[:, CHM * k:CHM * (k + 1)], st, fence=False)
        P.fence("sync", st)
        stats_ln(xc, CHM, xn1b)
        P.wait("vector", vv)
        for k in range(8):
            P.do("vector", lambda e, k=k: e.tensor_copy(
                out=xn1f[:, CHM * k:CHM * (k + 1)], in_=tmpf[:, CHM * k:CHM * (k + 1)]), inc=vv)
        P.wait("sync", vv)
        for k in range(8):
            P.dma("sync", xn1_o[k, :, co:co + CHM], xn1b[:, CHM * k:CHM * (k + 1)], st, fence=False)
        P.fence("sync", st)
        for t4 in range(NT4):
            P.wait("tensor", vv, aa)
            for k in range(8):
                P.do("tensor", lambda e, t4=t4, k=k: e.matmul(
                    ps_l[:, :], xn1f[:, CHM * k + 128 * t4:CHM * k + 128 * (t4 + 1)],
                    gwf[:, E * k:E * (k + 1)], start=(k == 0), stop=(k == 7)),
                    inc=(tt if k == 7 else None))
            P.wait("vector", tt)
            P.do("vector", lambda e, t4=t4: e.reduce_max(
                out=nmx[:, t4:t4 + 1], in_=ps_l[:, :], axis=AX.X, negate=True), inc=vv)
            P.wait("scalar", vv)
            P.do("scalar", lambda e, t4=t4: e.activation(
                out=affc[:, E * t4:E * (t4 + 1)], in_=ps_l[:, :], func=AF.Exp,
                bias=nmx[:, t4:t4 + 1], accum_out=rsum[:, t4:t4 + 1]), inc=aa)
            P.wait("vector", aa)
            P.do("vector", lambda e, t4=t4: e.reciprocal(
                out=rsum[:, t4:t4 + 1], in_=rsum[:, t4:t4 + 1]), inc=vv)
            P.wait("vector", vv)
            P.do("vector", lambda e, t4=t4: e.tensor_scalar(
                out=affc[:, E * t4:E * (t4 + 1)], in0=affc[:, E * t4:E * (t4 + 1)],
                scalar1=rsum[:, t4:t4 + 1], scalar2=None, op0=ALU.mult), inc=vv)
            P.wait("sync", vv)
            P.dma("sync", aff_o[NT4 * c + t4, :, :], affc[:, E * t4:E * (t4 + 1)], st)
            P.wait("tensor", aa, vv)
    P.wait("sync", st)
    return P.finish()


def host_mid_inputs(inputs, xn_full, res_full, y_full):
    x_dt = np.ascontiguousarray(inputs["L0_attn_x"].reshape(NTOK, D).T)
    opw_t = np.ascontiguousarray(inputs["L0_attn_out_proj_w"].T.reshape(16, 128, D))
    wg_t = np.ascontiguousarray(inputs["L0_ffn_Wg"].T.reshape(8, 128, 8))
    lnsc_i = np.ascontiguousarray(inputs["L0_ffn_ln_scale"].T.reshape(8, 128, 8))
    lnbi_i = np.ascontiguousarray(inputs["L0_ffn_ln_bias"].T.reshape(8, 128, 8))
    w1_i = np.ascontiguousarray(inputs["L0_ffn_W1"].reshape(8, 8, 128, 256))
    b1_i = np.ascontiguousarray(
        inputs["L0_ffn_b1"].reshape(8, 2, 128).transpose(2, 0, 1).reshape(128, 16))
    w2_i = np.ascontiguousarray(inputs["L0_ffn_W2"].reshape(8, 2, 128, D))
    b2_i = np.ascontiguousarray(inputs["L0_ffn_b2"].reshape(1, 8 * D))
    gw_t = np.ascontiguousarray(inputs["L1_ffn_gate_w"].T.reshape(8, 128, 8))
    ident = np.eye(128, dtype=np.float32)
    ones_c = np.ones((128, 1), np.float32)
    ones_r = np.ones((1, 128), np.float32)
    cst_arr = np.tile(np.array([1.0, -1.0, -0.5, LN_EPS, 1.0 / 32, 0, 0, 0],
                               np.float32), (128, 1))
    maps = []
    for c in range(NC):
        lo = c * TOKC
        maps.append({
            "x_i": np.ascontiguousarray(x_dt[:, lo:lo + TOKC].reshape(8, 128, TOKC)),
            "xn_i": np.ascontiguousarray(xn_full[:, lo:lo + TOKC].reshape(8, 128, TOKC)),
            "res_i": np.ascontiguousarray(res_full[:, lo:lo + TOKC].reshape(16, 128, TOKC)),
            "y_i": np.ascontiguousarray(y_full[:, lo:lo + TOKC].reshape(16, 128, TOKC)),
            "opw_t": opw_t, "wg_t": wg_t, "lnsc_i": lnsc_i, "lnbi_i": lnbi_i,
            "w1_i": w1_i, "b1_i": b1_i, "w2_i": w2_i, "b2_i": b2_i, "gw_t": gw_t,
            "ident_i": ident, "ones_c": ones_c, "ones_r": ones_r, "cst_i": cst_arr,
        })
    return maps


def run_mid(inputs, xn_full, res_full, y_full):
    nc = build_mid()
    maps = host_mid_inputs(inputs, xn_full, res_full, y_full)
    res = run_bass_kernel_spmd(nc, maps, core_ids=list(range(NC)))
    out = {}
    out["x1"] = np.concatenate(
        [res.results[c]["x1_o"].reshape(D, TOKC) for c in range(NC)], axis=1)
    out["xn1"] = np.concatenate(
        [res.results[c]["xn1_o"].reshape(D, TOKC) for c in range(NC)], axis=1)
    out["aff"] = np.concatenate(
        [res.results[c]["aff_o"].reshape(TOKC, 8) for c in range(NC)], axis=0)
    return out


def build_attn():
    nc = _nc()
    P = Prog(nc)
    NS = 16
    KC = 512
    NKC = T // KC

    qx_i = nc.dram_tensor("qx_i", [8, 128, NS * 128], BF16, kind="ExternalInput").ap()
    keys_i = nc.dram_tensor("keys_i", [8, 128, T], BF16, kind="ExternalInput").ap()
    qk_i = nc.dram_tensor("qk_i", [8, 128, D], F32, kind="ExternalInput").ap()
    ov_i = nc.dram_tensor("ov_i", [8, 128, D], F32, kind="ExternalInput").ap()
    qidx_i = nc.dram_tensor("qidx_i", [128, NS], F32, kind="ExternalInput").ap()
    kio_i = nc.dram_tensor("kio_i", [128, KC], F32, kind="ExternalInput").ap()
    ident_i = nc.dram_tensor("ident_i", [128, 128], F32, kind="ExternalInput").ap()
    cst_i = nc.dram_tensor("cst_i", [128, 8], F32, kind="ExternalInput").ap()
    attn_o = nc.dram_tensor("attn_o", [8, 128, NS * 128], BF16, kind="ExternalOutput").ap()

    q_dt = P.sb("q_dt", (128, 8 * NS * 128), BF16)
    acc = P.sb("acc", (128, NS * 1024), F32)
    wqk = P.sb("wqk", (128, 8 * D), BF16)
    wov = P.sb("wov", (128, 8 * D), BF16)
    kb = P.sb("kb", (128, 8 * KC), BF16)
    vb = P.sb("vb", (128, 4 * 1024), BF16)
    qxb = P.sb("qxb", (128, 8 * KC), BF16)
    wtmp = P.sb("wtmp", (128, D), F32)
    p_sb = P.sb("p_sb", (128, KC), BF16)
    pt_sb = P.sb("pt_sb", (128, KC), BF16)
    pen = P.sb("pen", (128, KC), F32)
    m_sb = P.sb("m_sb", (128, NS), F32)
    l_sb = P.sb("l_sb", (128, NS), F32)
    mx = P.sb("mx", (128, 1), F32)
    dif = P.sb("dif", (128, 1), F32)
    alp = P.sb("alp", (128, 1), F32)
    bcol = P.sb("bcol", (128, 1), F32)
    thr = P.sb("thr", (128, 1), F32)
    ls = P.sb("ls", (128, 1), F32)
    rcol = P.sb("rcol", (128, 1), F32)
    qidx = P.sb("qidx", (128, NS), F32)
    kio = P.sb("kio", (128, KC), F32)
    ident = P.sb("ident", (128, 128), BF16)
    cst = P.sb("cst", (128, 8), F32)
    ax = P.sb("ax", (128, 1024), BF16)
    axt = P.sb("axt", (128, 8 * 128), BF16)

    ps_s = P.ps("ps_s", (128, KC))
    ps_t = P.ps("ps_t", (128, KC), BF16)
    ps_av = P.ps("ps_av", (128, 1024))
    ps_q = P.ps("ps_q", (128, KC))

    ld = P.sem("ld")
    gd = P.sem("gd")
    vv = P.sem("vv")
    aa = P.sem("aa")
    tt = P.sem("tt")
    st = P.sem("st")

    for k in range(8):
        P.dma("sync", wtmp[:, :], qk_i[k, :, :], ld)
        P.wait("vector", ld)
        P.do("vector", lambda e, k=k: e.tensor_copy(
            out=wqk[:, D * k:D * (k + 1)], in_=wtmp[:, :]), inc=vv)
        P.wait("sync", vv)
        P.dma("sync", wtmp[:, :], ov_i[k, :, :], ld)
        P.wait("vector", ld)
        P.do("vector", lambda e, k=k: e.tensor_copy(
            out=wov[:, D * k:D * (k + 1)], in_=wtmp[:, :]), inc=vv)
        P.wait("sync", vv)
    P.dma("sync", qidx[:, :], qidx_i, ld, fence=False)
    P.dma("sync", kio[:, :], kio_i, ld, fence=False)
    P.dma("sync", cst[:, :], cst_i, ld, fence=False)
    P.fence("sync", ld)
    P.dma("gpsimd", ident[:, :], ident_i, gd)
    P.wait("vector", ld)
    P.do("vector", lambda e: e.memset(m_sb[:, :], -1e30), inc=vv)
    P.do("vector", lambda e: e.memset(l_sb[:, :], 0.0), inc=vv)
    for s in range(NS):
        P.do("vector", lambda e, s=s: e.memset(acc[:, 1024 * s:1024 * (s + 1)], 0.0), inc=vv)
    for qc in range(4):
        P.wait("sync", vv, tt)
        for k in range(8):
            P.dma("sync", qxb[:, KC * k:KC * (k + 1)], qx_i[k, :, KC * qc:KC * (qc + 1)],
                  ld, fence=False)
        P.fence("sync", ld)
        P.wait("tensor", ld, vv)
        for i in range(8):
            for k in range(8):
                P.do("tensor", lambda e, i=i, k=k: e.matmul(
                    ps_q[:, :], wqk[:, D * k + 128 * i:D * k + 128 * (i + 1)],
                    qxb[:, KC * k:KC * (k + 1)], start=(k == 0), stop=(k == 7)),
                    inc=(tt if k == 7 else None))
            P.wait("vector", tt)
            P.do("vector", lambda e, i=i, qc=qc: e.tensor_copy(
                out=q_dt[:, NS * 128 * i + KC * qc:NS * 128 * i + KC * (qc + 1)],
                in_=ps_q[:, :]), inc=vv)
            P.wait("tensor", vv)
    for j in range(NKC):
        P.wait("sync", vv, aa, tt, st)
        for k in range(8):
            P.dma("sync", kb[:, KC * k:KC * (k + 1)], keys_i[k, :, KC * j:KC * (j + 1)],
                  ld, fence=False)
        P.fence("sync", ld)
        for m4 in range(4):
            for kd in range(8):
                P.do("sync", lambda e, m4=m4, kd=kd, j=j: e.dma_start_transpose(
                    out=vb[:, 1024 * m4 + 128 * kd:1024 * m4 + 128 * (kd + 1)],
                    in_=keys_i[kd, :, KC * j + 128 * m4:KC * j + 128 * (m4 + 1)]),
                    inc=ld, k=16)
        P.fence("sync", ld)
        for s in range(j, NS):
            P.wait("tensor", ld, aa, vv)
            for k in range(8):
                P.do("tensor", lambda e, s=s, k=k: e.matmul(
                    ps_s[:, :], q_dt[:, NS * 128 * k + 128 * s:NS * 128 * k + 128 * (s + 1)],
                    kb[:, KC * k:KC * (k + 1)], start=(k == 0), stop=(k == 7)),
                    inc=(tt if k == 7 else None))
            P.wait("vector", tt)
            if s == j:
                P.do("vector", lambda e, s=s, j=j: e.tensor_scalar(
                    out=thr[:, :], in0=qidx[:, s:s + 1], scalar1=float(-KC * j),
                    scalar2=None, op0=ALU.add), inc=vv)
                P.wait("vector", vv)
                P.do("vector", lambda e: e.tensor_scalar(
                    out=pen[:, :], in0=kio[:, :], scalar1=thr[:, 0:1],
                    scalar2=cst[:, 6:7], op0=ALU.is_gt, op1=ALU.mult), inc=vv)
                P.wait("vector", vv)
                P.do("vector", lambda e: e.tensor_add(
                    out=ps_s[:, :], in0=ps_s[:, :], in1=pen[:, :]), inc=vv)
                P.wait("vector", vv)
            P.do("vector", lambda e: e.reduce_max(out=mx[:, :], in_=ps_s[:, :], axis=AX.X), inc=vv)
            P.wait("vector", vv)
            P.do("vector", lambda e, s=s: e.tensor_sub(
                out=dif[:, :], in0=m_sb[:, s:s + 1], in1=mx[:, :]), inc=vv)
            P.wait("vector", vv)
            P.do("vector", lambda e: e.tensor_scalar(
                out=dif[:, :], in0=dif[:, :], scalar1=0.0, scalar2=None,
                op0=ALU.min), inc=vv)
            P.do("vector", lambda e, s=s: e.tensor_max(
                out=m_sb[:, s:s + 1], in0=m_sb[:, s:s + 1], in1=mx[:, :]), inc=vv)
            P.wait("scalar", vv)
            P.do("scalar", lambda e: e.activation(
                out=alp[:, :], in_=dif[:, :], func=AF.Exp, scale=cst[:, 4:5]), inc=aa)
            P.wait("vector", vv)
            P.do("vector", lambda e, s=s: e.tensor_scalar(
                out=bcol[:, :], in0=m_sb[:, s:s + 1], scalar1=-1.0 / 32, scalar2=None,
                op0=ALU.mult), inc=vv)
            P.wait("scalar", vv, aa)
            P.do("scalar", lambda e: e.activation(
                out=p_sb[:, :], in_=ps_s[:, :], func=AF.Exp, scale=cst[:, 4:5],
                bias=bcol[:, 0:1], accum_out=ls[:, :]), inc=aa)
            P.wait("vector", aa)
            P.do("vector", lambda e, s=s: e.scalar_tensor_tensor(
                out=l_sb[:, s:s + 1], in0=l_sb[:, s:s + 1], scalar=alp[:, 0:1],
                in1=ls[:, :], op0=ALU.mult, op1=ALU.add), inc=vv)
            P.wait("tensor", aa, vv, gd)
            for m4 in range(4):
                P.do("tensor", lambda e, m4=m4: e.transpose(
                    ps_t[:, 128 * m4:128 * (m4 + 1)], p_sb[:, 128 * m4:128 * (m4 + 1)],
                    ident[:, :]), inc=(tt if m4 == 3 else None))
            P.wait("vector", tt)
            P.do("vector", lambda e: e.tensor_copy(out=pt_sb[:, :], in_=ps_t[:, :]), inc=vv)
            P.wait("tensor", vv)
            for m4 in range(4):
                for dh in range(2):
                    p0 = (m4 == 3)
                    P.do("tensor", lambda e, m4=m4, dh=dh: e.matmul(
                        ps_av[:, 512 * dh:512 * (dh + 1)],
                        pt_sb[:, 128 * m4:128 * (m4 + 1)],
                        vb[:, 1024 * m4 + 512 * dh:1024 * m4 + 512 * (dh + 1)],
                        start=(m4 == 0), stop=(m4 == 3)),
                        inc=(tt if (p0 and dh == 1) else None))
            P.wait("vector", tt)
            for dh in range(2):
                sl = acc[:, 1024 * s + 512 * dh:1024 * s + 512 * (dh + 1)]
                P.do("vector", lambda e, sl=sl: e.tensor_scalar(
                    out=sl, in0=sl, scalar1=alp[:, 0:1], scalar2=None, op0=ALU.mult), inc=vv)
            P.wait("vector", vv)
            for dh in range(2):
                sl = acc[:, 1024 * s + 512 * dh:1024 * s + 512 * (dh + 1)]
                P.do("vector", lambda e, sl=sl, dh=dh: e.tensor_add(
                    out=sl, in0=sl, in1=ps_av[:, 512 * dh:512 * (dh + 1)]), inc=vv)
            P.wait("tensor", vv)
    for s in range(NS):
        P.wait("vector", tt, aa, st)
        P.do("vector", lambda e, s=s: e.reciprocal(out=rcol[:, :], in_=l_sb[:, s:s + 1]), inc=vv)
        P.wait("vector", vv)
        P.do("vector", lambda e, s=s: e.tensor_scalar(
            out=ax[:, :], in0=acc[:, 1024 * s:1024 * (s + 1)], scalar1=rcol[:, 0:1],
            scalar2=None, op0=ALU.mult), inc=vv)
        P.wait("tensor", vv)
        for i8 in range(8):
            P.do("tensor", lambda e, i8=i8: e.transpose(
                ps_t[:, 128 * (i8 % 4):128 * (i8 % 4 + 1)], ax[:, 128 * i8:128 * (i8 + 1)],
                ident[:, :]), inc=tt)
            P.wait("vector", tt)
            P.do("vector", lambda e, i8=i8: e.tensor_copy(
                out=axt[:, 128 * i8:128 * (i8 + 1)],
                in_=ps_t[:, 128 * (i8 % 4):128 * (i8 % 4 + 1)]), inc=vv)
            P.wait("tensor", vv)
        for i in range(8):
            for kd in range(8):
                P.do("tensor", lambda e, i=i, kd=kd: e.matmul(
                    ps_q[:, 0:128], wov[:, D * kd + 128 * i:D * kd + 128 * (i + 1)],
                    axt[:, 128 * kd:128 * (kd + 1)], start=(kd == 0), stop=(kd == 7)),
                    inc=(tt if kd == 7 else None))
            P.wait("vector", tt)
            P.do("vector", lambda e, i=i: e.tensor_copy(
                out=ax[:, 128 * i:128 * (i + 1)], in_=ps_q[:, 0:128]), inc=vv)
            P.wait("tensor", vv)
        P.wait("sync", vv)
        for i in range(8):
            P.dma("sync", attn_o[i, :, 128 * s:128 * (s + 1)], ax[:, 128 * i:128 * (i + 1)],
                  st, fence=False)
        P.fence("sync", st)
    P.wait("sync", st)
    return P.finish()


def host_attn_inputs(inputs, xn1_full_bf):
    qk_i = np.ascontiguousarray(inputs["L1_attn_qk"].reshape(8, 128, D))
    ov_i = np.ascontiguousarray(inputs["L1_attn_ov"].reshape(8, 128, D))
    kio = np.tile(np.arange(512, dtype=np.float32), (128, 1))
    ident = np.eye(128, dtype=np.float32)
    cst_arr = np.tile(np.array([1.0, -1.0, -0.5, LN_EPS, 1.0 / 32, 0, -1e9, 0],
                               np.float32), (128, 1))
    maps = []
    for c in range(NC):
        beta = c // 4
        xb = xn1_full_bf[:, beta * T:(beta + 1) * T]
        slots = [4 * s + (c % 4) for s in range(16)]
        qcols = np.concatenate([np.arange(128 * i, 128 * (i + 1)) for i in slots])
        qx = np.ascontiguousarray(xb[:, qcols].reshape(8, 128, 16 * 128))
        keys = np.ascontiguousarray(xb.reshape(8, 128, T))
        qidx = np.zeros((128, 16), np.float32)
        for s, i in enumerate(slots):
            qidx[:, s] = 128 * i + np.arange(128)
        maps.append({
            "qx_i": qx, "keys_i": keys, "qk_i": qk_i, "ov_i": ov_i,
            "qidx_i": qidx, "kio_i": kio, "ident_i": ident, "cst_i": cst_arr,
        })
    return maps


def run_attn(inputs, xn1_full_bf):
    nc = build_attn()
    maps = host_attn_inputs(inputs, xn1_full_bf)
    res = run_bass_kernel_spmd(nc, maps, core_ids=list(range(NC)))
    attn_full = np.zeros((D, NTOK), dtype=BF)
    for c in range(NC):
        beta = c // 4
        a = res.results[c]["attn_o"].reshape(D, 16 * 128)
        for s in range(16):
            i = 4 * s + (c % 4)
            attn_full[:, beta * T + 128 * i: beta * T + 128 * (i + 1)] = a[:, 128 * s:128 * (s + 1)]
    return attn_full


def build_moe(nch5=None):
    nc = _nc()
    P = Prog(nc)
    I2 = 2048
    CH5 = 512
    NCH5 = (T // CH5) if nch5 is None else nch5
    KTOP = 2048

    xn1_i = nc.dram_tensor("xn1_i", [8, 128, NTOK], BF16, kind="ExternalInput").ap()
    affe_i = nc.dram_tensor("affe_i", [B, 128, 64], F32, kind="ExternalInput").ap()
    affr_i = nc.dram_tensor("affr_i", [B, 1, T], F32, kind="ExternalInput").ap()
    wg_i = nc.dram_tensor("wg_i", [2, 8, 128, I2], F32, kind="ExternalInput").ap()
    wu_i = nc.dram_tensor("wu_i", [2, 8, 128, I2], F32, kind="ExternalInput").ap()
    wd_i = nc.dram_tensor("wd_i", [2, 16, 128, D], F32, kind="ExternalInput").ap()
    ones_c = nc.dram_tensor("ones_c", [128, 1], F32, kind="ExternalInput").ap()
    ones_r = nc.dram_tensor("ones_r", [1, 128], F32, kind="ExternalInput").ap()
    dense_o = nc.dram_tensor("dense_o", [8, 128, NTOK], BF16, kind="ExternalOutput").ap()

    wg = P.sb("wg", (128, 8 * I2), BF16)
    wu = P.sb("wu", (128, 8 * I2), BF16)
    wd = P.sb("wd", (128, 16 * D), BF16)
    xc = P.sb("xc", (128, 8 * CH5), BF16)
    hh = P.sb("hh", (128, 16 * CH5), BF16)
    gsb = P.sb("gsb", (128, CH5), BF16)
    mgrep = P.sb("mgrep", (128, CH5), BF16)
    ob = P.sb("ob", (128, 8 * CH5), BF16)
    aff2 = P.sb("aff2", (128, B * 64), F32)
    affr_sh = P.sb("affr_sh", (1, T), F32)
    mg0 = P.sb("mg0", (1, T), BF16)
    mg1 = P.sb("mg1", (1, T), BF16)
    bis = P.sb("bis", (1, 8), I32)
    cnt1 = P.sb("cnt1", (128, 1), F32)
    cmp64 = P.sb("cmp64", (128, 64), F32)
    cntb = P.sb("cntb", (128, 1), BF16)
    ones_bf = P.sb("ones_bf", (128, 1), BF16)
    oner_f = P.sb("oner_f", (1, 128), F32)
    oner_b = P.sb("oner_b", (1, 128), BF16)
    tau = P.sb("tau", (1, 2), I32)

    ps_a = P.ps("ps_a", (128, CH5))
    ps_b = P.ps("ps_b", (128, CH5))
    ps_c = P.ps("ps_c", (128, CH5))
    ps_mr = P.ps("ps_mr", (128, CH5))
    ps_r = P.ps("ps_r", (128, 8))

    ld = P.sem("ld")
    gd = P.sem("gd")
    vv = P.sem("vv")
    aa = P.sem("aa")
    tt = P.sem("tt")
    st = P.sem("st")

    P.dma("sync", aff2[:, 0:64], affe_i[0, :, :], ld, fence=False)
    P.dma("sync", aff2[:, 64:128], affe_i[1, :, :], ld, fence=False)
    P.dma("sync", oner_f[:, :], ones_r, ld)
    P.dma("gpsimd", ones_bf[:, :], ones_c, gd, fence=False)
    P.dma("gpsimd", oner_b[:, :], ones_r, gd)

    ONE_F_BITS = 0x3F800000
    for b in range(B):
        P.wait("sync", vv)
        P.dma("sync", affr_sh[:, :], affr_i[b, :, :], ld)
        P.wait("vector", ld, gd)
        P.do("vector", lambda e: e.memset(bis[0:1, 0:1], 0), inc=vv)
        P.do("vector", lambda e: e.memset(bis[0:1, 1:2], ONE_F_BITS), inc=vv)
        arow = affr_sh
        a2 = aff2[:, 64 * b:64 * (b + 1)]
        for it in range(31):
            P.wait("vector", vv, tt)
            P.do("vector", lambda e: e.tensor_tensor(
                out=bis[0:1, 2:3], in0=bis[0:1, 0:1], in1=bis[0:1, 1:2], op=ALU.add), inc=vv)
            P.wait("vector", vv)
            P.do("vector", lambda e: e.tensor_scalar(
                out=bis[0:1, 2:3], in0=bis[0:1, 2:3], scalar1=1, scalar2=None,
                op0=ALU.arith_shift_right), inc=vv)
            P.wait("tensor", vv, ld)
            P.do("tensor", lambda e: e.matmul(
                ps_r[:, 0:1], oner_f[:, :], bis[0:1, 2:3].bitcast(F32),
                start=True, stop=True), inc=tt)
            P.wait("vector", tt)
            P.do("vector", lambda e, a2=a2: e.tensor_scalar(
                out=cmp64[:, :], in0=a2, scalar1=ps_r[:, 0:1], scalar2=None,
                op0=ALU.is_ge, op1=ALU.add, accum_out=cnt1[:, 0:1]), inc=vv)
            P.wait("vector", vv)
            P.do("vector", lambda e: e.tensor_copy(out=cntb[:, :], in_=cnt1[:, :]), inc=vv)
            P.wait("tensor", vv, gd)
            P.do("tensor", lambda e: e.matmul(
                ps_r[0:1, 1:2], ones_bf[:, :], cntb[:, :], start=True, stop=True), inc=tt)
            P.wait("vector", tt)
            P.do("vector", lambda e: e.tensor_scalar(
                out=bis[0:1, 3:4], in0=ps_r[0:1, 1:2], scalar1=float(KTOP), scalar2=None,
                op0=ALU.is_ge), inc=vv)
            P.wait("vector", vv)
            P.do("vector", lambda e: e.tensor_sub(
                out=bis[0:1, 4:5], in0=bis[0:1, 2:3], in1=bis[0:1, 0:1]), inc=vv)
            P.wait("vector", vv)
            P.do("vector", lambda e: e.tensor_mul(
                out=bis[0:1, 4:5], in0=bis[0:1, 4:5], in1=bis[0:1, 3:4]), inc=vv)
            P.wait("vector", vv)
            P.do("vector", lambda e: e.tensor_add(
                out=bis[0:1, 0:1], in0=bis[0:1, 0:1], in1=bis[0:1, 4:5]), inc=vv)
            P.do("vector", lambda e: e.tensor_sub(
                out=bis[0:1, 5:6], in0=bis[0:1, 1:2], in1=bis[0:1, 2:3]), inc=vv)
            P.do("vector", lambda e: e.tensor_scalar(
                out=bis[0:1, 6:7], in0=bis[0:1, 3:4], scalar1=-1, scalar2=None,
                op0=ALU.add), inc=vv)
            P.wait("vector", vv)
            P.do("vector", lambda e: e.tensor_mul(
                out=bis[0:1, 5:6], in0=bis[0:1, 5:6], in1=bis[0:1, 6:7]), inc=vv)
            P.wait("vector", vv)
            P.do("vector", lambda e: e.tensor_add(
                out=bis[0:1, 1:2], in0=bis[0:1, 1:2], in1=bis[0:1, 5:6]), inc=vv)
        P.wait("vector", vv)
        P.do("vector", lambda e, b=b: e.tensor_copy(out=tau[0:1, b:b + 1], in_=bis[0:1, 0:1]), inc=vv)
        P.wait("vector", vv)
        mgr = (mg0, mg1)[b]
        P.do("vector", lambda e, arow=arow, mgr=mgr, b=b: e.scalar_tensor_tensor(
            out=mgr[0:1, :], in0=arow[0:1, :], scalar=tau[0:1, b:b + 1].bitcast(F32),
            in1=arow[0:1, :], op0=ALU.is_ge, op1=ALU.mult), inc=vv)

    for ih in range(2):
        P.wait("gpsimd", tt, aa, vv)
        for k in range(8):
            P.dma("gpsimd", wg[:, I2 * k:I2 * (k + 1)], wg_i[ih, k, :, :], gd, fence=False)
            P.dma("gpsimd", wu[:, I2 * k:I2 * (k + 1)], wu_i[ih, k, :, :], gd, fence=False)
        for k in range(16):
            P.dma("gpsimd", wd[:, D * k:D * (k + 1)], wd_i[ih, k, :, :], gd, fence=False)
        P.fence("gpsimd", gd)
        for b in range(B):
            mgr = (mg0, mg1)[b]
            for c5 in range(NCH5):
                lo5 = CH5 * c5
                gco = b * T + lo5
                P.wait("sync", vv, aa, tt, st)
                for k in range(8):
                    P.dma("sync", xc[:, CH5 * k:CH5 * (k + 1)],
                          xn1_i[k, :, gco:gco + CH5], ld, fence=False)
                P.fence("sync", ld)
                P.wait("tensor", vv, ld, gd)
                P.do("tensor", lambda e, mgr=mgr, lo5=lo5: e.matmul(
                    ps_mr[:, :], oner_b[:, :], mgr[0:1, lo5:lo5 + CH5],
                    start=True, stop=True), inc=tt)
                P.wait("vector", tt)
                P.do("vector", lambda e: e.tensor_copy(out=mgrep[:, :], in_=ps_mr[:, :]), inc=vv)
                for ic in range(16):
                    P.wait("tensor", aa, vv)
                    for k in range(8):
                        P.do("tensor", lambda e, ic=ic, k=k: e.matmul(
                            ps_a[:, :], wg[:, I2 * k + 128 * ic:I2 * k + 128 * (ic + 1)],
                            xc[:, CH5 * k:CH5 * (k + 1)], start=(k == 0), stop=(k == 7)),
                            inc=(tt if k == 7 else None))
                    for k in range(8):
                        P.do("tensor", lambda e, ic=ic, k=k: e.matmul(
                            ps_b[:, :], wu[:, I2 * k + 128 * ic:I2 * k + 128 * (ic + 1)],
                            xc[:, CH5 * k:CH5 * (k + 1)], start=(k == 0), stop=(k == 7)),
                            inc=(tt if k == 7 else None))
                    P.wait("scalar", tt)
                    P.do("scalar", lambda e: e.activation(
                        out=gsb[:, :], in_=ps_a[:, :], func=AF.Silu), inc=aa)
                    P.wait("vector", aa, tt)
                    P.do("vector", lambda e, ic=ic: e.tensor_mul(
                        out=hh[:, CH5 * ic:CH5 * (ic + 1)], in0=ps_b[:, :],
                        in1=gsb[:, :]), inc=vv)
                for i in range(8):
                    P.wait("tensor", vv, aa)
                    for k in range(16):
                        P.do("tensor", lambda e, i=i, k=k: e.matmul(
                            ps_c[:, :], wd[:, D * k + 128 * i:D * k + 128 * (i + 1)],
                            hh[:, CH5 * k:CH5 * (k + 1)], start=(k == 0), stop=(k == 15)),
                            inc=(tt if k == 15 else None))
                    P.wait("vector", tt, st)
                    P.do("vector", lambda e, i=i: e.tensor_mul(
                        out=ob[:, CH5 * i:CH5 * (i + 1)], in0=ps_c[:, :],
                        in1=mgrep[:, :]), inc=vv)
                    P.wait("tensor", vv)
                P.wait("gpsimd", vv)
                for i in range(8):
                    if ih == 0:
                        P.dma("gpsimd", dense_o[i, :, gco:gco + CH5],
                              ob[:, CH5 * i:CH5 * (i + 1)], gd, fence=False)
                    else:
                        P.do("gpsimd", lambda e, i=i, gco=gco: e.dma_start(
                            out=dense_o[i, :, gco:gco + CH5],
                            in_=ob[:, CH5 * i:CH5 * (i + 1)],
                            accum_op=ALU.add), inc=gd, k=16)
                P.fence("gpsimd", gd)
    P.wait("sync", st)
    P.wait("gpsimd", gd)
    return P.finish()


def host_moe_inputs(inputs, xn1_full_bf, aff_full):
    ones_c = np.ones((128, 1), np.float32)
    ones_r = np.ones((1, 128), np.float32)
    xn1_a = np.ascontiguousarray(xn1_full_bf.reshape(8, 128, NTOK))
    maps = []
    for e_ in range(NC):
        affe = np.ascontiguousarray(
            aff_full[:, e_].reshape(B, 128, 64).astype(np.float32))
        affr = np.ascontiguousarray(aff_full[:, e_].reshape(B, 1, T).astype(np.float32))
        wg = np.ascontiguousarray(
            inputs["L1_ffn_w_gate"][e_].T.reshape(8, 128, 4096)
            .reshape(8, 128, 2, 2048).transpose(2, 0, 1, 3))
        wu = np.ascontiguousarray(
            inputs["L1_ffn_w_up"][e_].T.reshape(8, 128, 4096)
            .reshape(8, 128, 2, 2048).transpose(2, 0, 1, 3))
        wd = np.ascontiguousarray(inputs["L1_ffn_w_down"][e_].T.reshape(2, 16, 128, D))
        maps.append({
            "xn1_i": xn1_a, "affe_i": affe, "affr_i": affr,
            "wg_i": wg, "wu_i": wu, "wd_i": wd,
            "ones_c": ones_c, "ones_r": ones_r,
        })
    return maps


def run_moe(inputs, xn1_full_bf, aff_full):
    nc = build_moe()
    maps = host_moe_inputs(inputs, xn1_full_bf, aff_full)
    res = run_bass_kernel_spmd(nc, maps, core_ids=list(range(NC)))
    return [res.results[e_]["dense_o"].reshape(D, NTOK) for e_ in range(NC)]


def build_final():
    nc = _nc()
    P = Prog(nc)
    x1_i = nc.dram_tensor("x1_i", [8, 128, TOKC], F32, kind="ExternalInput").ap()
    attn_i = nc.dram_tensor("attn_i", [8, 128, TOKC], BF16, kind="ExternalInput").ap()
    dense_i = nc.dram_tensor("dense_i", [8, 8, 128, TOKC], BF16, kind="ExternalInput").ap()
    out_o = nc.dram_tensor("out_o", [8, 128, TOKC], F32, kind="ExternalOutput").ap()

    x1t = P.sb("x1t", (128, TOKC), F32)
    bt = P.sb("bt", (128, TOKC), BF16)
    btf = P.sb("btf", (128, TOKC), F32)
    ld = P.sem("ld")
    vv = P.sem("vv")
    st = P.sem("st")
    for k in range(8):
        P.wait("sync", vv, st)
        P.dma("sync", x1t[:, :], x1_i[k, :, :], ld)
        for si in range(9):
            src = attn_i[k, :, :] if si == 0 else dense_i[si - 1, k, :, :]
            P.wait("sync", vv)
            P.dma("sync", bt[:, :], src, ld)
            P.wait("vector", ld)
            P.do("vector", lambda e: e.tensor_copy(out=btf[:, :], in_=bt[:, :]), inc=vv)
            P.wait("vector", vv)
            P.do("vector", lambda e: e.tensor_add(
                out=x1t[:, :], in0=x1t[:, :], in1=btf[:, :]), inc=vv)
        P.wait("sync", vv)
        P.dma("sync", out_o[k, :, :], x1t[:, :], st)
    P.wait("sync", st)
    return P.finish()


def run_final(x1_full, attn_full_bf, dense_list):
    nc = build_final()
    maps = []
    for c in range(NC):
        lo = c * TOKC
        dense = np.stack([d[:, lo:lo + TOKC].reshape(8, 128, TOKC) for d in dense_list])
        maps.append({
            "x1_i": np.ascontiguousarray(x1_full[:, lo:lo + TOKC].reshape(8, 128, TOKC)),
            "attn_i": np.ascontiguousarray(attn_full_bf[:, lo:lo + TOKC].reshape(8, 128, TOKC)),
            "dense_i": np.ascontiguousarray(dense),
        })
    res = run_bass_kernel_spmd(nc, maps, core_ids=list(range(NC)))
    return np.concatenate(
        [res.results[c]["out_o"].reshape(D, TOKC) for c in range(NC)], axis=1)




import math


def build_fused(upto=6, mid_skip=()):
    nc = _nc()
    P = Prog(nc)
    NH = 3
    CE = CH + NH
    E, HH = 8, 256
    CHM = 256
    I2 = 2048
    CH5 = 512
    KC = 512
    NS = 16
    TC = 512

    def EIN(name, shape, dt=F32):
        return nc.dram_tensor(name, list(shape), dt, kind="ExternalInput").ap()

    x_ext = EIN("x_ext", [8, 128, NH + TOKC])
    inpw_t = EIN("inpw_t", [8, 128, 2 * DI])
    norm_w = EIN("norm_w", [8, 128, 1])
    conv_wa = EIN("conv_wa", [128, 64])
    conv_ba = EIN("conv_ba", [128, 16])
    xpw_t = EIN("xpw_t", [16, 128, 96])
    dtpw_t = EIN("dtpw_t", [64, DI])
    dtb_a = EIN("dtb_a", [128, 16])
    ones_c = EIN("ones_c", [128, 1])
    ones_r = EIN("ones_r", [1, 128])
    cst_i = EIN("cst_i", [128, 8])
    alog_a = EIN("alog_a", [128, 32])
    d_i = EIN("d_i", [128, 2])
    sel_i = EIN("sel_i", [16, 128, 128])
    nsel_i = EIN("nsel_i", [16, 128, 128])
    sbsel_i = EIN("sbsel_i", [16, 128])
    sidx_i = EIN("sidx_i", [128, 32], I32)
    yidx_i = EIN("yidx_i", [128, 16], I32)
    opw_t = EIN("opw_t", [16, 128, D])
    wg_t = EIN("wg_t", [8, 128, E])
    lnsc_i = EIN("lnsc_i", [8, 128, E])
    lnbi_i = EIN("lnbi_i", [8, 128, E])
    w1_i = EIN("w1_i", [E, 8, 128, HH])
    b1_i = EIN("b1_i", [128, 16])
    w2_i = EIN("w2_i", [E, 2, 128, D])
    b2_i = EIN("b2_i", [1, E * D])
    gw_t = EIN("gw_t", [8, 128, E])
    ident_i = EIN("ident_i", [128, 128])
    qk_i = EIN("qk_i", [8, 128, D])
    ov_i = EIN("ov_i", [8, 128, D])
    qidx_i = EIN("qidx_i", [128, NS])
    kio_i = EIN("kio_i", [128, KC])
    kidx_i = EIN("kidx_i", [128, 16 * 8], I32)
    vidx_i = EIN("vidx_i", [128, 16 * 4], I32)
    aidx_i = EIN("aidx_i", [8, 1], I32)
    wg5_i = EIN("wg5_i", [2, 8, 128, I2])
    wu5_i = EIN("wu5_i", [2, 8, 128, I2])
    wd5_i = EIN("wd5_i", [2, 16, 128, D])

    def IDR(name, shape, dt):
        return nc.dram_tensor(name, list(shape), dt, kind="Internal").ap()

    xn_scr = IDR("xn_scr", [8, 128, TOKC], F32)
    res_scr = IDR("res_scr", [16, 128, TOKC], BF16)
    x1_scr = IDR("x1_scr", [8, 128, TOKC], F32)
    xn1_scr = IDR("xn1_scr", [8, 128, TOKC], BF16)
    attn_scr = IDR("attn_scr", [8, 128, TOKC], BF16)
    ag1_in = IDR("ag1_in", [33 * 128, TOKC], BF16)
    ag1_out = IDR("ag1_out", [8 * 33 * 128, TOKC], BF16)
    ag2_in = IDR("ag2_in", [2048, TOKC], BF16)
    ag2_out = IDR("ag2_out", [8 * 2048, TOKC], BF16)
    ag3a_in = IDR("ag3a_in", [1024, TOKC], BF16)
    ag3a_out = IDR("ag3a_out", [8 * 1024, TOKC], BF16)
    ag3b_in = IDR("ag3b_in", [TOKC, 1024], BF16)
    ag3b_out = IDR("ag3b_out", [NTOK, 1024], BF16)
    ag3c_in = IDR("ag3c_in", [8, TOKC], F32)
    ag3c_out = IDR("ag3c_out", [64, TOKC], F32)
    affg = IDR("affg", [8, TOKC], F32)
    rs_in = IDR("rs_in", [8192, TOKC], BF16)
    rs_out = IDR("rs_out", [1024, TOKC], BF16)
    out_o = nc.dram_tensor("out_o", [8, 128, TOKC], F32, kind="ExternalOutput").ap()

    ld = P.sem("ld")
    gd = P.sem("gd")
    vv = P.sem("vv")
    aa = P.sem("aa")
    tt = P.sem("tt")
    st = P.sem("st")
    cc1 = P.sem("cc1")
    cc2 = P.sem("cc2")
    cc3a = P.sem("cc3a")
    cc3b = P.sem("cc3b")
    cc3c = P.sem("cc3c")
    cc4 = P.sem("cc4")
    allsems = [ld, gd, vv, aa, tt, st]

    ag1v = ag1_in.rearrange("(a p) t -> a p t", p=128)
    ag1ov = ag1_out.rearrange("(q a p) t -> q a p t", q=8, p=128)
    ag2v = ag2_in.rearrange("(a p) t -> a p t", p=128)
    ag3av = ag3a_in.rearrange("(a p) t -> a p t", p=128)
    ag3aov = ag3a_out.rearrange("(a p) t -> a p t", p=128)
    rsv = rs_in.rearrange("(o i p) t -> o i p t", o=8, p=128)
    rsov = rs_out.rearrange("(i p) t -> i p t", p=128)

    def _phase_1():
        with P.scope():
            w_in = P.sb("w_in", (128, 8 * 2 * DI), BF16)
            w_xp = P.sb("w_xp", (128, 16 * 96), BF16)
            w_dt = P.sb("w_dt", (64, DI), BF16)
            wtmp = P.sb("wtmp", (128, 2 * DI), F32)
            cw = P.sb("cw", (128, 64), F32)
            cb = P.sb("cb", (128, 16), F32)
            dtb = P.sb("dtb", (128, 16), F32)
            nwc = P.sb("nwc", (128, 8), F32)
            ones_bf = P.sb("ones_bf", (128, 1), BF16)
            cst = P.sb("cst", (128, 8), F32)
            oner_f = P.sb("oner_f", (1, 128), F32)
            xc = P.sb("xc", (128, 8 * CH), F32)
            xbf = P.sb("xbf", (128, 8 * CH), BF16)
            sq = P.sb("sq", (128, 8 * CH), BF16)
            xnt = P.sb("xnt", (128, 8 * CH), F32)
            ht = P.sb("ht", (128, 8 * CH), BF16)
            srow = P.sb("srow", (1, 6 * CH), F32)
            xs_ext = P.sb("xs_ext", (128, 16 * CE), BF16)
            rbuf = P.sb("rbuf", (128, 2 * CH), BF16)
            convt = P.sb("convt", (128, CH), F32)
            u_full = P.sb("u_full", (128, 16 * CH), BF16)
            din_bf = P.sb("din_bf", (64, CH), BF16)
            bc_sb = P.sb("bc_sb", (32, CH), BF16)
            sp1 = P.sb("sp1", (128, CH), F32)
            sp2 = P.sb("sp2", (128, CH), F32)
            dbuf = P.sb("dbuf", (128, 2 * CH), BF16)
            ps_s1 = P.ps("ps_s1", (1, CH))
            ps_s2 = P.ps("ps_s2", (1, CH))
            ps_r1 = P.ps("ps_r1", (128, CH))
            ps_r2 = P.ps("ps_r2", (128, CH))
            ps_a = P.ps("ps_a", (128, CH))
            ps_b = P.ps("ps_b", (128, CH))
            ps_x = P.ps("ps_x", (96, CH))

            for k in range(8):
                P.dma("sync", wtmp[:, :], inpw_t[k, :, :], ld)
                P.dma("sync", nwc[:, k:k + 1], norm_w[k, :, :], ld)
                P.wait("vector", ld)
                P.do("vector", lambda e, k=k: e.tensor_scalar(
                    out=w_in[:, 4096 * k:4096 * (k + 1)], in0=wtmp[:, :],
                    scalar1=nwc[:, k:k + 1], scalar2=None, op0=ALU.mult), inc=vv)
                P.wait("sync", vv)
            for k in range(16):
                P.dma("sync", wtmp[:, 0:96], xpw_t[k, :, :], ld)
                P.wait("vector", ld)
                P.do("vector", lambda e, k=k: e.tensor_copy(
                    out=w_xp[:, 96 * k:96 * (k + 1)], in_=wtmp[:, 0:96]), inc=vv)
                P.wait("sync", vv)
            P.dma("sync", wtmp[0:64, 0:DI], dtpw_t, ld)
            P.wait("vector", ld)
            P.do("vector", lambda e: e.tensor_copy(out=w_dt[:, :], in_=wtmp[0:64, 0:DI]), inc=vv)
            P.dma("sync", cw[:, :], conv_wa, ld)
            P.dma("sync", cb[:, :], conv_ba, ld)
            P.dma("sync", dtb[:, :], dtb_a, ld)
            P.dma("gpsimd", ones_bf[:, :], ones_c, gd)
            P.dma("sync", cst[:, :], cst_i, ld)
            P.dma("sync", oner_f[:, :], ones_r, ld)

            MU, EX2, VAR, RSTD, MS, RSTD2 = range(6)

            def row(i):
                return srow[0:1, CH * i:CH * i + CH]

            def chunk_barrier():
                P.wait("sync", vv, aa, tt)
                P.wait("vector", tt, aa)
                P.wait("scalar", tt, vv)
                P.wait("tensor", vv, aa)
                P.wait("gpsimd", vv)

            def ln_rms_chunk(n, col0, halo):
                chunk_barrier()
                for k in range(8):
                    P.dma("sync", xc[:, CH * k:CH * k + n], x_ext[k, :, col0:col0 + n],
                          ld, fence=False)
                P.fence("sync", ld)
                P.wait("vector", ld)
                P.wait("scalar", ld)
                for k in range(8):
                    P.do("vector", lambda e, k=k: e.tensor_copy(
                        out=xbf[:, CH * k:CH * k + n], in_=xc[:, CH * k:CH * k + n]), inc=vv)
                    P.do("scalar", lambda e, k=k: e.activation(
                        out=sq[:, CH * k:CH * k + n], in_=xc[:, CH * k:CH * k + n],
                        func=AF.Square), inc=aa)
                P.wait("tensor", vv, aa, gd)
                for k in range(8):
                    P.do("tensor", lambda e, k=k: e.matmul(
                        ps_s1[0:1, 0:n], ones_bf[:, :], xbf[:, CH * k:CH * k + n],
                        start=(k == 0), stop=(k == 7)), inc=(tt if k == 7 else None))
                for k in range(8):
                    P.do("tensor", lambda e, k=k: e.matmul(
                        ps_s2[0:1, 0:n], ones_bf[:, :], sq[:, CH * k:CH * k + n],
                        start=(k == 0), stop=(k == 7)), inc=(tt if k == 7 else None))
                P.wait("vector", tt)
                P.do("vector", lambda e: e.tensor_scalar(
                    out=row(MU)[:, 0:n], in0=ps_s1[0:1, 0:n], scalar1=1.0 / D,
                    scalar2=None, op0=ALU.mult), inc=vv)
                P.do("vector", lambda e: e.tensor_scalar(
                    out=row(EX2)[:, 0:n], in0=ps_s2[0:1, 0:n], scalar1=1.0 / D,
                    scalar2=None, op0=ALU.mult), inc=vv)
                P.wait("vector", vv)
                P.do("vector", lambda e: e.tensor_mul(
                    out=row(VAR)[:, 0:n], in0=row(MU)[:, 0:n], in1=row(MU)[:, 0:n]), inc=vv)
                P.wait("vector", vv)
                P.do("vector", lambda e: e.tensor_sub(
                    out=row(VAR)[:, 0:n], in0=row(EX2)[:, 0:n], in1=row(VAR)[:, 0:n]), inc=vv)
                P.wait("scalar", vv)
                P.do("scalar", lambda e: e.activation(
                    out=row(RSTD)[:, 0:n], in_=row(VAR)[:, 0:n], func=AF.Ln,
                    bias=cst[0:1, 3:4]), inc=aa)
                P.wait("scalar", aa)
                P.do("scalar", lambda e: e.activation(
                    out=row(RSTD)[:, 0:n], in_=row(RSTD)[:, 0:n], func=AF.Exp,
                    scale=cst[0:1, 2:3]), inc=aa)
                P.wait("tensor", aa)
                P.do("tensor", lambda e: e.matmul(
                    ps_r1[:, 0:n], oner_f[:, :], row(MU)[:, 0:n], start=True, stop=True), inc=tt)
                P.do("tensor", lambda e: e.matmul(
                    ps_r2[:, 0:n], oner_f[:, :], row(RSTD)[:, 0:n], start=True, stop=True), inc=tt)
                P.wait("vector", tt)
                for k in range(8):
                    P.do("vector", lambda e, k=k: e.tensor_sub(
                        out=xnt[:, CH * k:CH * k + n], in0=xc[:, CH * k:CH * k + n],
                        in1=ps_r1[:, 0:n]), inc=vv)
                P.wait("vector", vv)
                for k in range(8):
                    P.do("vector", lambda e, k=k: e.tensor_mul(
                        out=xnt[:, CH * k:CH * k + n], in0=xnt[:, CH * k:CH * k + n],
                        in1=ps_r2[:, 0:n]), inc=vv)
                if not halo:
                    P.wait("sync", vv)
                    for k in range(8):
                        P.dma("sync", xn_scr[k, :, col0 - NH:col0 - NH + n],
                              xnt[:, CH * k:CH * k + n], st, fence=False)
                    P.fence("sync", st)
                P.wait("scalar", vv)
                for k in range(8):
                    P.do("scalar", lambda e, k=k: e.activation(
                        out=sq[:, CH * k:CH * k + n], in_=xnt[:, CH * k:CH * k + n],
                        func=AF.Square), inc=aa)
                P.wait("tensor", aa, vv)
                for k in range(8):
                    P.do("tensor", lambda e, k=k: e.matmul(
                        ps_s1[0:1, 0:n], ones_bf[:, :], sq[:, CH * k:CH * k + n],
                        start=(k == 0), stop=(k == 7)), inc=(tt if k == 7 else None))
                P.wait("vector", tt)
                P.do("vector", lambda e: e.tensor_scalar(
                    out=row(MS)[:, 0:n], in0=ps_s1[0:1, 0:n], scalar1=1.0 / D,
                    scalar2=None, op0=ALU.mult), inc=vv)
                P.wait("scalar", vv)
                P.do("scalar", lambda e: e.activation(
                    out=row(RSTD2)[:, 0:n], in_=row(MS)[:, 0:n], func=AF.Ln,
                    bias=cst[0:1, 3:4]), inc=aa)
                P.wait("scalar", aa)
                P.do("scalar", lambda e: e.activation(
                    out=row(RSTD2)[:, 0:n], in_=row(RSTD2)[:, 0:n], func=AF.Exp,
                    scale=cst[0:1, 2:3]), inc=aa)
                P.wait("tensor", aa)
                P.do("tensor", lambda e: e.matmul(
                    ps_r1[:, 0:n], oner_f[:, :], row(RSTD2)[:, 0:n], start=True, stop=True), inc=tt)
                P.wait("vector", tt)
                for k in range(8):
                    P.do("vector", lambda e, k=k: e.tensor_mul(
                        out=ht[:, CH * k:CH * k + n], in0=xnt[:, CH * k:CH * k + n],
                        in1=ps_r1[:, 0:n]), inc=vv)

            def in_proj_chunk(n, c, halo):
                irange = range(16) if halo else range(32)
                P.wait("tensor", vv)
                for i in irange:
                    pm = (ps_a, ps_b)[i % 2]
                    for k in range(8):
                        P.do("tensor", lambda e, i=i, k=k, pm=pm: e.matmul(
                            pm[:, 0:n], w_in[:, 4096 * k + 128 * i:4096 * k + 128 * (i + 1)],
                            ht[:, CH * k:CH * k + n], start=(k == 0), stop=(k == 7)),
                            inc=(tt if k == 7 else None))
                    P.wait("vector", tt)
                    if i < 16:
                        off = CE * i + (0 if halo else NH)
                        P.do("vector", lambda e, i=i, off=off, pm=pm: e.tensor_copy(
                            out=xs_ext[:, off:off + n], in_=pm[:, 0:n]), inc=vv)
                    else:
                        g = i - 16
                        rb = rbuf[:, CH * (i % 2):CH * (i % 2) + n]
                        P.wait("vector", st)
                        P.do("vector", lambda e, rb=rb, pm=pm: e.tensor_copy(
                            out=rb, in_=pm[:, 0:n]), inc=vv)
                        P.wait("sync", vv)
                        P.dma("sync", res_scr[g, :, CH * c:CH * c + n], rb, st)
                    P.wait("tensor", vv)

            def conv_u_chunk(n, c):
                for g in range(16):
                    base = CE * g
                    P.do("vector", lambda e, g=g, base=base: e.tensor_scalar(
                        out=convt[:, 0:n], in0=xs_ext[:, base:base + n],
                        scalar1=cw[:, 4 * g:4 * g + 1], scalar2=None, op0=ALU.mult), inc=vv)
                    for m in (1, 2, 3):
                        P.wait("vector", vv)
                        P.do("vector", lambda e, g=g, base=base, m=m: e.scalar_tensor_tensor(
                            out=convt[:, 0:n], in0=xs_ext[:, base + m:base + m + n],
                            scalar=cw[:, 4 * g + m:4 * g + m + 1], in1=convt[:, 0:n],
                            op0=ALU.mult, op1=ALU.add), inc=vv)
                    P.wait("scalar", vv)
                    P.do("scalar", lambda e, g=g: e.activation(
                        out=u_full[:, CH * g:CH * g + n], in_=convt[:, 0:n],
                        func=AF.Silu, bias=cb[:, g:g + 1]), inc=aa)
                    P.wait("sync", aa)
                    P.dma("sync", ag1v[16 + g, :, CH * c:CH * c + n],
                          u_full[:, CH * g:CH * g + n], st)
                    P.wait("vector", aa)
                    P.do("vector", lambda e, g=g, base=base: e.tensor_copy(
                        out=xs_ext[:, base:base + NH],
                        in_=xs_ext[:, base + CH:base + CH + NH]), inc=vv)

            def xdbl_chunk(n, c):
                P.wait("tensor", aa, vv)
                for g in range(16):
                    P.do("tensor", lambda e, g=g: e.matmul(
                        ps_x[:, 0:n], w_xp[:, 96 * g:96 * (g + 1)],
                        u_full[:, CH * g:CH * g + n], start=(g == 0), stop=(g == 15)),
                        inc=(tt if g == 15 else None))
                P.wait("vector", tt)
                P.do("vector", lambda e: e.tensor_copy(out=bc_sb[:, 0:n], in_=ps_x[64:96, 0:n]), inc=vv)
                P.do("vector", lambda e: e.tensor_copy(out=din_bf[:, 0:n], in_=ps_x[0:64, 0:n]), inc=vv)
                P.wait("sync", vv)
                P.dma("sync", ag1v[32, 0:32, CH * c:CH * c + n], bc_sb[:, 0:n], st)
                P.wait("tensor", vv)
                for i in range(16):
                    pm = (ps_a, ps_b)[i % 2]
                    P.do("tensor", lambda e, i=i, pm=pm: e.matmul(
                        pm[:, 0:n], w_dt[:, 128 * i:128 * (i + 1)], din_bf[:, 0:n],
                        start=True, stop=True), inc=tt)
                    P.wait("scalar", tt)
                    P.wait("vector", tt)
                    P.do("scalar", lambda e, i=i, pm=pm: e.activation(
                        out=sp1[:, 0:n], in_=pm[:, 0:n], func=AF.Abs,
                        bias=dtb[:, i:i + 1]), inc=aa)
                    P.wait("scalar", aa)
                    P.do("scalar", lambda e: e.activation(
                        out=sp1[:, 0:n], in_=sp1[:, 0:n], func=AF.Exp, scale=cst[:, 1:2]), inc=aa)
                    P.wait("scalar", aa)
                    P.do("scalar", lambda e: e.activation(
                        out=sp1[:, 0:n], in_=sp1[:, 0:n], func=AF.Ln, bias=cst[:, 0:1]), inc=aa)
                    P.wait("vector", aa)
                    P.do("vector", lambda e, i=i, pm=pm: e.tensor_scalar(
                        out=sp2[:, 0:n], in0=pm[:, 0:n], scalar1=dtb[:, i:i + 1],
                        scalar2=cst[:, 5:6], op0=ALU.add, op1=ALU.max), inc=vv)
                    P.wait("vector", aa, st, vv)
                    db = dbuf[:, CH * (i % 2):CH * (i % 2) + n]
                    P.do("vector", lambda e, db=db: e.tensor_add(
                        out=db, in0=sp1[:, 0:n], in1=sp2[:, 0:n]), inc=vv)
                    P.wait("sync", vv)
                    P.dma("sync", ag1v[i, :, CH * c:CH * c + n], db, st)
                    P.wait("tensor", vv, aa)

            ln_rms_chunk(NH, 0, halo=True)
            in_proj_chunk(NH, -1, halo=True)
            for c in range(TOKC // CH):
                ln_rms_chunk(CH, NH + CH * c, halo=False)
                in_proj_chunk(CH, c, halo=False)
                conv_u_chunk(CH, c)
                xdbl_chunk(CH, c)
            P.wait("sync", st)
            P.barrier(allsems)

    _phase_1()
    if upto >= 2:
        P.wait("gpsimd", st)
        P.collective("AllGather", ALU.bypass, ag1_in, ag1_out, cc1)

    def _phase_2():
        with P.scope():
            NTC = T // TC
            dl_sb = P.sb("s_dl", (128, 2 * T), BF16)
            u_sb = P.sb("s_u", (128, 2 * T), BF16)
            du_sb = P.sb("s_du", (128, 2 * T), BF16)
            brep = P.sb("s_brep", (128, T), BF16)
            crep = P.sb("s_crep", (128, T), BF16)
            brow = P.sb("s_brow", (16, T), BF16)
            crow = P.sb("s_crow", (16, T), BF16)
            sel = P.sb("s_sel", (128, 16 * 128), BF16)
            nsel = P.sb("s_nsel", (128, 16 * 128), BF16)
            sbsel = P.sb("s_sbsel", (16, 128), BF16)
            seltmp = P.sb("s_seltmp", (128, 16 * 128), F32)
            acol = P.sb("s_acol", (128, 32), F32)
            dcol = P.sb("s_dcol", (128, 2), F32)
            state = P.sb("s_state", (128, 32), F32)
            sidx = P.sb("s_sidx", (128, 32), I32)
            da_sb = P.sb("s_da", (128, TC), F32)
            dbu_sb = P.sb("s_dbu", (128, TC), F32)
            h_sb = P.sb("s_h", (128, TC), F32)
            hc_sb = P.sb("s_hc", (128, TC), BF16)
            ybuf = P.sb("s_ybuf", (128, 2 * TC), BF16)
            ps_dr = P.ps("s_psdr", (128, TC))
            ps_ur = P.ps("s_psur", (128, TC))
            ps_y = P.ps("s_psy", (128, TC))
            ps_br = P.ps("s_psbr", (128, TC))

            for g in range(16):
                P.dma("sync", seltmp[:, 128 * g:128 * (g + 1)], sel_i[g, :, :], ld, fence=False)
            P.fence("sync", ld)
            P.wait("vector", ld)
            P.do("vector", lambda e: e.tensor_copy(out=sel[:, :], in_=seltmp[:, :]), inc=vv)
            P.wait("sync", vv)
            for g in range(16):
                P.dma("sync", seltmp[:, 128 * g:128 * (g + 1)], nsel_i[g, :, :], ld, fence=False)
            P.fence("sync", ld)
            P.wait("vector", ld)
            P.do("vector", lambda e: e.tensor_copy(out=nsel[:, :], in_=seltmp[:, :]), inc=vv)
            P.wait("sync", vv)
            P.dma("sync", seltmp[0:16, 0:128], sbsel_i, ld)
            P.wait("vector", ld)
            P.do("vector", lambda e: e.tensor_copy(out=sbsel[:, :], in_=seltmp[0:16, 0:128]), inc=vv)
            P.dma("sync", acol[:, :], alog_a, ld)
            P.dma("sync", dcol[:, :], d_i, ld)
            P.dma("sync", sidx[:, :], sidx_i, ld)
            P.wait("scalar", ld)
            P.do("scalar", lambda e: e.activation(out=acol[:, :], in_=acol[:, :], func=AF.Exp), inc=aa)
            P.wait("vector", aa)
            P.do("vector", lambda e: e.tensor_scalar(
                out=acol[:, :], in0=acol[:, :], scalar1=-1.0, scalar2=None, op0=ALU.mult), inc=vv)

            for b in range(B):
                P.wait("sync", vv, aa, tt, st)
                P.wait("gpsimd", vv, aa, tt, st, ld)
                if b == 0:
                    P.wait("gpsimd", cc1)
                    P.wait("sync", cc1)
                for rr in range(2):
                    for qq in range(4):
                        q = 4 * b + qq
                        P.do("gpsimd", lambda e, rr=rr, q=q, qq=qq: e.indirect_dma_start(
                            out=dl_sb[:, T * rr + TOKC * qq:T * rr + TOKC * (qq + 1)],
                            out_offset=None, in_=ag1_out,
                            in_offset=IndirectOffsetOnAxis(sidx[:, rr * 8 + q:rr * 8 + q + 1], 0)),
                            inc=gd, k=16)
                        P.do("gpsimd", lambda e, rr=rr, q=q, qq=qq: e.indirect_dma_start(
                            out=u_sb[:, T * rr + TOKC * qq:T * rr + TOKC * (qq + 1)],
                            out_offset=None, in_=ag1_out,
                            in_offset=IndirectOffsetOnAxis(sidx[:, 16 + rr * 8 + q:16 + rr * 8 + q + 1], 0)),
                            inc=gd, k=16)
                P.fence("gpsimd", gd)
                for qq in range(4):
                    q = 4 * b + qq
                    P.dma("sync", brow[:, TOKC * qq:TOKC * (qq + 1)],
                          ag1ov[q, 32, 0:16, :], ld, fence=False)
                    P.dma("sync", crow[:, TOKC * qq:TOKC * (qq + 1)],
                          ag1ov[q, 32, 16:32, :], ld, fence=False)
                P.fence("sync", ld)
                P.wait("vector", ld, gd)
                for rr in range(2):
                    P.do("vector", lambda e, rr=rr: e.tensor_mul(
                        out=du_sb[:, T * rr:T * (rr + 1)], in0=dl_sb[:, T * rr:T * (rr + 1)],
                        in1=u_sb[:, T * rr:T * (rr + 1)]), inc=vv)
                P.wait("tensor", vv, ld, gd)
                for cc0 in range(NTC):
                    lo = TC * cc0
                    P.do("tensor", lambda e, lo=lo: e.matmul(
                        ps_br[:, :], sbsel[:, :], brow[0:16, lo:lo + TC],
                        start=True, stop=True), inc=tt)
                    P.wait("vector", tt)
                    P.do("vector", lambda e, lo=lo: e.tensor_copy(
                        out=brep[:, lo:lo + TC], in_=ps_br[:, :]), inc=vv)
                    P.wait("tensor", vv)
                    P.do("tensor", lambda e, lo=lo: e.matmul(
                        ps_br[:, :], sbsel[:, :], crow[0:16, lo:lo + TC],
                        start=True, stop=True), inc=tt)
                    P.wait("vector", tt)
                    P.do("vector", lambda e, lo=lo: e.tensor_copy(
                        out=crep[:, lo:lo + TC], in_=ps_br[:, :]), inc=vv)
                    P.wait("tensor", vv)
                P.do("vector", lambda e: e.memset(state[:, :], 0.0), inc=vv)
                for cc0 in range(NTC):
                    lo = TC * cc0
                    for r in range(2):
                        for gg in range(16):
                            g = 16 * r + gg
                            P.wait("tensor", aa)
                            P.do("tensor", lambda e, r=r, gg=gg, lo=lo: e.matmul(
                                ps_dr[:, :], sel[:, 128 * gg:128 * (gg + 1)],
                                dl_sb[:, T * r + lo:T * r + lo + TC],
                                start=True, stop=True), inc=tt)
                            P.wait("tensor", vv)
                            P.do("tensor", lambda e, r=r, gg=gg, lo=lo: e.matmul(
                                ps_ur[:, :], sel[:, 128 * gg:128 * (gg + 1)],
                                du_sb[:, T * r + lo:T * r + lo + TC],
                                start=True, stop=True), inc=tt)
                            P.wait("scalar", tt, vv)
                            P.do("scalar", lambda e, g=g: e.activation(
                                out=da_sb[:, :], in_=ps_dr[:, :], func=AF.Exp,
                                scale=acol[:, g:g + 1]), inc=aa)
                            P.wait("vector", tt)
                            P.do("vector", lambda e, lo=lo: e.tensor_mul(
                                out=dbu_sb[:, :], in0=ps_ur[:, :], in1=brep[:, lo:lo + TC]), inc=vv)
                            P.wait("vector", aa, vv)
                            P.do("vector", lambda e, g=g: e.tensor_tensor_scan(
                                out=h_sb[:, :], data0=da_sb[:, :], data1=dbu_sb[:, :],
                                initial=state[:, g:g + 1], op0=ALU.mult, op1=ALU.add), inc=vv)
                            P.wait("vector", vv)
                            P.do("vector", lambda e, g=g: e.tensor_copy(
                                out=state[:, g:g + 1], in_=h_sb[:, TC - 1:TC]), inc=vv)
                            P.do("vector", lambda e, lo=lo: e.tensor_mul(
                                out=hc_sb[:, :], in0=h_sb[:, :], in1=crep[:, lo:lo + TC]), inc=vv)
                            P.wait("tensor", vv)
                            P.do("tensor", lambda e, gg=gg: e.matmul(
                                ps_y[:, :], nsel[:, 128 * gg:128 * (gg + 1)], hc_sb[:, :],
                                start=(gg == 0), stop=(gg == 15)),
                                inc=(tt if gg == 15 else None))
                        P.wait("vector", tt, st)
                        yb = ybuf[:, TC * r:TC * r + TC]
                        P.do("vector", lambda e, r=r, lo=lo, yb=yb: e.scalar_tensor_tensor(
                            out=yb, in0=u_sb[:, T * r + lo:T * r + lo + TC],
                            scalar=dcol[:, r:r + 1], in1=ps_y[:, :],
                            op0=ALU.mult, op1=ALU.add), inc=vv)
                        P.wait("sync", vv)
                        cpr = 4 * b + lo // 2048
                        P.dma("sync", ag2v[2 * cpr + r, :, lo % 2048:lo % 2048 + TC], yb, st)
                        P.wait("tensor", vv)
            P.wait("sync", st)
            P.barrier(allsems)

    if upto >= 2:
        _phase_2()
    if upto >= 3:
        P.wait("gpsimd", st)
        P.collective("AllGather", ALU.bypass, ag2_in, ag2_out, cc2)
        P.wait("sync", cc2)
        P.wait("gpsimd", cc2)
    def _phase_3():
        with P.scope():
            CHMN = TOKC // CHM
            NT4 = CHM // 128
            w_op = P.sb("m_wop", (128, 16 * D), BF16)
            w1 = P.sb("m_w1", (128, E * 8 * HH), BF16)
            w2 = P.sb("m_w2", (128, E * 2 * D), BF16)
            b2r = P.sb("m_b2r", (1, E * D), BF16)
            wgf = P.sb("m_wgf", (128, 8 * E), F32)
            gwf = P.sb("m_gwf", (128, 8 * E), F32)
            lnsc = P.sb("m_lnsc", (128, 8 * E), F32)
            lnbi = P.sb("m_lnbi", (128, 8 * E), F32)
            b1c = P.sb("m_b1c", (128, 16), F32)
            ident = P.sb("m_ident", (128, 128), BF16)
            identf = P.sb("m_identf", (128, 128), F32)
            wtmp = P.sb("m_wtmp", (128, D), F32)
            ones_bf = P.sb("m_onesbf", (128, 1), BF16)
            oner_f = P.sb("m_onerf", (1, 128), F32)
            oner_b = P.sb("m_onerb", (1, 128), BF16)
            cst = P.sb("m_cst", (128, 8), F32)
            xc = P.sb("m_xc", (128, 8 * CHM), F32)
            xnc = P.sb("m_xnc", (128, 8 * CHM), F32)
            resc = P.sb("m_resc", (128, 16 * CHM), BF16)
            yc = P.sb("m_yc", (128, 16 * CHM), BF16)
            xnb = P.sb("m_xnb", (128, 8 * CHM), BF16)
            sq = P.sb("m_sq", (128, 8 * CHM), BF16)
            tmpf = P.sb("m_tmpf", (128, 8 * CHM), F32)
            mrow = P.sb("m_mrow", (1, E * CHM), BF16)
            mask_sb = P.sb("m_mask", (128, NT4 * E), BF16)
            nmx = P.sb("m_nmx", (128, NT4), F32)
            rsum = P.sb("m_rsum", (128, NT4), F32)
            affc = P.sb("m_affc", (128, NT4 * E), F32)
            srow = P.sb("m_srow", (1, 6 * CHM), F32)
            xn1b = P.sb("m_xn1b", (128, 8 * CHM), BF16)
            tmrow = P.sb("m_tmrow", (8, 128), F32)
            yidx = P.sb("m_yidx", (128, 16), I32)
            tbuf = P.sb("m_tbuf", (128, 1024), BF16)
            ps_a = P.ps("m_psa", (128, CHM))
            ps_b = P.ps("m_psb", (128, CHM))
            ps_r1 = P.ps("m_psr1", (128, CHM))
            ps_r2 = P.ps("m_psr2", (128, CHM))
            ps_l = P.ps("m_psl", (128, E))
            ps_m = P.ps("m_psm", (1, 1024), BF16)
            ps_mr = P.ps("m_psmr", (128, CHM))
            ps_tb = P.ps("m_pstb", (128, 128), BF16)

            MU, EX2, VAR, RSTD = range(4)

            def row(i):
                return srow[0:1, CHM * i:CHM * i + CHM]

            for k in range(16):
                P.dma("sync", wtmp[:, 0:D], opw_t[k, :, :], ld)
                P.wait("vector", ld)
                P.do("vector", lambda e, k=k: e.tensor_copy(
                    out=w_op[:, D * k:D * (k + 1)], in_=wtmp[:, 0:D]), inc=vv)
                P.wait("sync", vv)
            for e_ in range(E):
                for k in range(8):
                    P.dma("sync", wtmp[:, 0:HH], w1_i[e_, k, :, :], ld)
                    P.wait("vector", ld)
                    P.do("vector", lambda e, e_=e_, k=k: e.tensor_copy(
                        out=w1[:, (8 * e_ + k) * HH:(8 * e_ + k + 1) * HH], in_=wtmp[:, 0:HH]), inc=vv)
                    P.wait("sync", vv)
                for k in range(2):
                    P.dma("sync", wtmp[:, 0:D], w2_i[e_, k, :, :], ld)
                    P.wait("vector", ld)
                    P.do("vector", lambda e, e_=e_, k=k: e.tensor_copy(
                        out=w2[:, (2 * e_ + k) * D:(2 * e_ + k + 1) * D], in_=wtmp[:, 0:D]), inc=vv)
                    P.wait("sync", vv)
            P.dma("gpsimd", b2r[:, :], b2_i, gd, fence=False)
            P.dma("gpsimd", ident[:, :], ident_i, gd, fence=False)
            P.dma("gpsimd", ones_bf[:, :], ones_c, gd, fence=False)
            P.dma("gpsimd", oner_b[:, :], ones_r, gd)
            for k in range(8):
                P.dma("sync", wgf[:, E * k:E * (k + 1)], wg_t[k, :, :], ld, fence=False)
                P.dma("sync", gwf[:, E * k:E * (k + 1)], gw_t[k, :, :], ld, fence=False)
                P.dma("sync", lnsc[:, E * k:E * (k + 1)], lnsc_i[k, :, :], ld, fence=False)
                P.dma("sync", lnbi[:, E * k:E * (k + 1)], lnbi_i[k, :, :], ld, fence=False)
            P.dma("sync", b1c[:, :], b1_i, ld, fence=False)
            P.dma("sync", oner_f[:, :], ones_r, ld, fence=False)
            P.dma("sync", identf[:, :], ident_i, ld, fence=False)
            P.dma("sync", yidx[:, :], yidx_i, ld, fence=False)
            P.dma("sync", cst[:, :], cst_i, ld)

            def stats_ln(src_f32, n, out_bf):
                P.wait("scalar", vv, tt)
                for k in range(8):
                    P.do("scalar", lambda e, k=k: e.activation(
                        out=sq[:, CHM * k:CHM * k + n], in_=src_f32[:, CHM * k:CHM * k + n],
                        func=AF.Square), inc=aa)
                P.wait("vector", vv, tt)
                for k in range(8):
                    P.do("vector", lambda e, k=k: e.tensor_copy(
                        out=xnb[:, CHM * k:CHM * k + n],
                        in_=src_f32[:, CHM * k:CHM * k + n]), inc=vv)
                P.wait("tensor", vv, aa, gd)
                for k in range(8):
                    P.do("tensor", lambda e, k=k: e.matmul(
                        ps_a[0:1, 0:n], ones_bf[:, :], xnb[:, CHM * k:CHM * k + n],
                        start=(k == 0), stop=(k == 7)), inc=(tt if k == 7 else None))
                for k in range(8):
                    P.do("tensor", lambda e, k=k: e.matmul(
                        ps_b[0:1, 0:n], ones_bf[:, :], sq[:, CHM * k:CHM * k + n],
                        start=(k == 0), stop=(k == 7)), inc=(tt if k == 7 else None))
                P.wait("vector", tt)
                P.do("vector", lambda e: e.tensor_scalar(
                    out=row(MU)[:, 0:n], in0=ps_a[0:1, 0:n], scalar1=1.0 / D,
                    scalar2=None, op0=ALU.mult), inc=vv)
                P.do("vector", lambda e: e.tensor_scalar(
                    out=row(EX2)[:, 0:n], in0=ps_b[0:1, 0:n], scalar1=1.0 / D,
                    scalar2=None, op0=ALU.mult), inc=vv)
                P.wait("vector", vv)
                P.do("vector", lambda e: e.tensor_mul(
                    out=row(VAR)[:, 0:n], in0=row(MU)[:, 0:n], in1=row(MU)[:, 0:n]), inc=vv)
                P.wait("vector", vv)
                P.do("vector", lambda e: e.tensor_sub(
                    out=row(VAR)[:, 0:n], in0=row(EX2)[:, 0:n], in1=row(VAR)[:, 0:n]), inc=vv)
                P.wait("scalar", vv)
                P.do("scalar", lambda e: e.activation(
                    out=row(RSTD)[:, 0:n], in_=row(VAR)[:, 0:n], func=AF.Ln,
                    bias=cst[0:1, 3:4]), inc=aa)
                P.wait("scalar", aa)
                P.do("scalar", lambda e: e.activation(
                    out=row(RSTD)[:, 0:n], in_=row(RSTD)[:, 0:n], func=AF.Exp,
                    scale=cst[0:1, 2:3]), inc=aa)
                P.wait("tensor", aa, vv)
                P.do("tensor", lambda e: e.matmul(
                    ps_r1[:, 0:n], oner_f[:, :], row(MU)[:, 0:n], start=True, stop=True), inc=tt)
                P.do("tensor", lambda e: e.matmul(
                    ps_r2[:, 0:n], oner_f[:, :], row(RSTD)[:, 0:n], start=True, stop=True), inc=tt)
                P.wait("vector", tt)
                for k in range(8):
                    P.do("vector", lambda e, k=k: e.tensor_sub(
                        out=tmpf[:, CHM * k:CHM * k + n],
                        in0=src_f32[:, CHM * k:CHM * k + n], in1=ps_r1[:, 0:n]), inc=vv)
                P.wait("vector", vv)
                for k in range(8):
                    P.do("vector", lambda e, k=k: e.tensor_mul(
                        out=tmpf[:, CHM * k:CHM * k + n], in0=tmpf[:, CHM * k:CHM * k + n],
                        in1=ps_r2[:, 0:n]), inc=vv)
                P.wait("vector", vv)
                for k in range(8):
                    P.do("vector", lambda e, k=k: e.tensor_copy(
                        out=out_bf[:, CHM * k:CHM * k + n], in_=tmpf[:, CHM * k:CHM * k + n]), inc=vv)

            xhat = resc
            h1m = yc
            inpe = xnb
            xn1f = xnc

            for c in range(CHMN):
                co = CHM * c
                P.wait("sync", vv, aa, tt, st)
                for k in range(8):
                    P.dma("sync", xc[:, CHM * k:CHM * (k + 1)],
                          x_ext[k, :, NH + co:NH + co + CHM], ld, fence=False)
                    P.dma("sync", xnc[:, CHM * k:CHM * (k + 1)], xn_scr[k, :, co:co + CHM],
                          ld, fence=False)
                for g in range(16):
                    P.dma("sync", resc[:, CHM * g:CHM * (g + 1)], res_scr[g, :, co:co + CHM],
                          ld, fence=False)
                P.fence("sync", ld)
                P.wait("gpsimd", vv, tt, st, ld)
                if c == 0:
                    P.wait("gpsimd", cc2)
                for g in range(16):
                    P.do("gpsimd", lambda e, g=g, co=co: e.indirect_dma_start(
                        out=yc[:, CHM * g:CHM * (g + 1)], out_offset=None,
                        in_=ag2_out, element_offset=co,
                        in_offset=IndirectOffsetOnAxis(yidx[:, g:g + 1], 0)),
                        inc=gd, k=16)
                P.fence("gpsimd", gd)
                for t4 in range(NT4):
                    P.wait("tensor", vv, aa, ld, gd)
                    for k in range(8):
                        P.do("tensor", lambda e, t4=t4, k=k: e.matmul(
                            ps_l[:, :], xnc[:, CHM * k + 128 * t4:CHM * k + 128 * (t4 + 1)],
                            wgf[:, E * k:E * (k + 1)], start=(k == 0), stop=(k == 7)),
                            inc=(tt if k == 7 else None))
                    P.wait("vector", tt)
                    P.do("vector", lambda e, t4=t4: e.reduce_max(
                        out=nmx[:, t4:t4 + 1], in_=ps_l[:, :], axis=AX.X), inc=vv)
                    P.wait("vector", vv)
                    P.do("vector", lambda e, t4=t4: e.tensor_scalar(
                        out=mask_sb[:, E * t4:E * (t4 + 1)], in0=ps_l[:, :],
                        scalar1=nmx[:, t4:t4 + 1], scalar2=None, op0=ALU.is_ge), inc=vv)
                    P.wait("tensor", vv)
                    for e_ in range(E):
                        P.do("tensor", lambda e, t4=t4, e_=e_: e.transpose(
                            ps_m[0:1, 128 * e_:128 * (e_ + 1)],
                            mask_sb[:, E * t4 + e_:E * t4 + e_ + 1], ident[:, :]),
                            inc=(tt if e_ == E - 1 else None))
                    P.wait("vector", tt)
                    for e_ in range(E):
                        P.do("vector", lambda e, t4=t4, e_=e_: e.tensor_copy(
                            out=mrow[0:1, CHM * e_ + 128 * t4:CHM * e_ + 128 * (t4 + 1)],
                            in_=ps_m[0:1, 128 * e_:128 * (e_ + 1)]), inc=vv)
                    P.wait("tensor", vv)
                P.wait("vector", ld)
                for k in range(8):
                    P.do("vector", lambda e, k=k: e.tensor_add(
                        out=xc[:, CHM * k:CHM * (k + 1)], in0=xc[:, CHM * k:CHM * (k + 1)],
                        in1=xnc[:, CHM * k:CHM * (k + 1)]), inc=vv)
                stats_ln(xnc, CHM, xn1b)
                P.wait("scalar", ld, vv, gd)
                for g in range(16):
                    P.do("scalar", lambda e, g=g: e.activation(
                        out=resc[:, CHM * g:CHM * (g + 1)], in_=resc[:, CHM * g:CHM * (g + 1)],
                        func=AF.Silu), inc=aa)
                P.wait("vector", aa, gd)
                for g in range(16):
                    P.do("vector", lambda e, g=g: e.tensor_mul(
                        out=resc[:, CHM * g:CHM * (g + 1)], in0=resc[:, CHM * g:CHM * (g + 1)],
                        in1=yc[:, CHM * g:CHM * (g + 1)]), inc=vv)
                P.wait("tensor", vv)
                for i in range(8):
                    pm = (ps_a, ps_b)[i % 2]
                    for k in range(16):
                        P.do("tensor", lambda e, i=i, k=k, pm=pm: e.matmul(
                            pm[:, :], w_op[:, D * k + 128 * i:D * k + 128 * (i + 1)],
                            resc[:, CHM * k:CHM * (k + 1)], start=(k == 0), stop=(k == 15)),
                            inc=(tt if k == 15 else None))
                    P.wait("vector", tt)
                    P.do("vector", lambda e, i=i, pm=pm: e.tensor_add(
                        out=xc[:, CHM * i:CHM * (i + 1)], in0=xc[:, CHM * i:CHM * (i + 1)],
                        in1=pm[:, :]), inc=vv)
                    P.wait("tensor", vv)
                P.wait("vector", vv)
                for k in range(8):
                    P.do("vector", lambda e, k=k: e.tensor_copy(
                        out=xhat[:, CHM * k:CHM * (k + 1)], in_=xn1b[:, CHM * k:CHM * (k + 1)]), inc=vv)
                for e_ in range(E):
                    P.wait("vector", vv, tt)
                    for k in range(8):
                        P.do("vector", lambda e, e_=e_, k=k: e.tensor_scalar(
                            out=inpe[:, CHM * k:CHM * (k + 1)], in0=xhat[:, CHM * k:CHM * (k + 1)],
                            scalar1=lnsc[:, E * k + e_:E * k + e_ + 1],
                            scalar2=lnbi[:, E * k + e_:E * k + e_ + 1],
                            op0=ALU.mult, op1=ALU.add), inc=vv)
                    P.wait("tensor", vv, aa, gd)
                    P.do("tensor", lambda e, e_=e_: e.matmul(
                        ps_mr[:, :], oner_b[:, :], mrow[0:1, CHM * e_:CHM * (e_ + 1)],
                        start=True, stop=True), inc=tt)
                    for hi in range(2):
                        pm = (ps_a, ps_b)[hi]
                        for k in range(8):
                            P.do("tensor", lambda e, e_=e_, hi=hi, k=k, pm=pm: e.matmul(
                                pm[:, :], w1[:, (8 * e_ + k) * HH + 128 * hi:(8 * e_ + k) * HH + 128 * (hi + 1)],
                                inpe[:, CHM * k:CHM * (k + 1)], start=(k == 0), stop=(k == 7)),
                                inc=(tt if k == 7 else None))
                        P.wait("scalar", tt)
                        P.do("scalar", lambda e, e_=e_, hi=hi, pm=pm: e.activation(
                            out=h1m[:, CHM * (2 * e_ + hi):CHM * (2 * e_ + hi + 1)], in_=pm[:, :],
                            func=AF.Gelu, bias=b1c[:, 2 * e_ + hi:2 * e_ + hi + 1]), inc=aa)
                    P.wait("vector", tt, aa)
                    for hi in range(2):
                        P.do("vector", lambda e, e_=e_, hi=hi: e.tensor_mul(
                            out=h1m[:, CHM * (2 * e_ + hi):CHM * (2 * e_ + hi + 1)],
                            in0=h1m[:, CHM * (2 * e_ + hi):CHM * (2 * e_ + hi + 1)],
                            in1=ps_mr[:, :]), inc=vv)
                    P.wait("tensor", vv)
                for i in range(8):
                    pm = (ps_a, ps_b)[i % 2]
                    P.wait("tensor", vv, aa)
                    nmm = 3 * E
                    cnt = 0
                    for e_ in range(E):
                        for hi in range(2):
                            cnt += 1
                            s0, p0 = (cnt == 1), (cnt == nmm)
                            P.do("tensor", lambda e, i=i, e_=e_, hi=hi, pm=pm, s=s0, p=p0: e.matmul(
                                pm[:, :], w2[:, (2 * e_ + hi) * D + 128 * i:(2 * e_ + hi) * D + 128 * (i + 1)],
                                h1m[:, CHM * (2 * e_ + hi):CHM * (2 * e_ + hi + 1)],
                                start=s, stop=p), inc=(tt if p0 else None))
                        cnt += 1
                        p0 = (cnt == nmm)
                        P.do("tensor", lambda e, i=i, e_=e_, pm=pm, p=p0: e.matmul(
                            pm[:, :], b2r[0:1, D * e_ + 128 * i:D * e_ + 128 * (i + 1)],
                            mrow[0:1, CHM * e_:CHM * (e_ + 1)], start=False, stop=p),
                            inc=(tt if p0 else None))
                    P.wait("vector", tt)
                    P.do("vector", lambda e, i=i, pm=pm: e.tensor_add(
                        out=xc[:, CHM * i:CHM * (i + 1)], in0=xc[:, CHM * i:CHM * (i + 1)],
                        in1=pm[:, :]), inc=vv)
                    P.wait("tensor", vv)
                P.wait("sync", vv)
                for k in range(8):
                    P.dma("sync", x1_scr[k, :, co:co + CHM], xc[:, CHM * k:CHM * (k + 1)], st, fence=False)
                P.fence("sync", st)
                stats_ln(xc, CHM, xn1b)
                P.wait("vector", vv)
                for k in range(8):
                    P.do("vector", lambda e, k=k: e.tensor_copy(
                        out=xn1f[:, CHM * k:CHM * (k + 1)], in_=tmpf[:, CHM * k:CHM * (k + 1)]), inc=vv)
                P.wait("sync", vv)
                for k in range(8):
                    P.dma("sync", xn1_scr[k, :, co:co + CHM], xn1b[:, CHM * k:CHM * (k + 1)], st, fence=False)
                    P.dma("sync", ag3av[k, :, co:co + CHM], xn1b[:, CHM * k:CHM * (k + 1)], st, fence=False)
                P.fence("sync", st)
                for t4 in (() if "agb" in mid_skip else range(NT4)):
                    P.wait("tensor", vv, st, aa)
                    for k in range(8):
                        P.do("tensor", lambda e, t4=t4, k=k: e.transpose(
                            ps_tb[:, :], xn1b[:, CHM * k + 128 * t4:CHM * k + 128 * (t4 + 1)],
                            ident[:, :]), inc=tt)
                        P.wait("vector", tt)
                        P.do("vector", lambda e, k=k: e.tensor_copy(
                            out=tbuf[:, 128 * k:128 * (k + 1)], in_=ps_tb[:, :]), inc=vv)
                        P.wait("tensor", vv)
                    P.wait("sync", vv)
                    P.dma("sync", ag3b_in[co + 128 * t4:co + 128 * (t4 + 1), :], tbuf[:, :], st)
                for t4 in range(NT4):
                    P.wait("tensor", vv, aa)
                    for k in range(8):
                        P.do("tensor", lambda e, t4=t4, k=k: e.matmul(
                            ps_l[:, :], xn1f[:, CHM * k + 128 * t4:CHM * k + 128 * (t4 + 1)],
                            gwf[:, E * k:E * (k + 1)], start=(k == 0), stop=(k == 7)),
                            inc=(tt if k == 7 else None))
                    P.wait("vector", tt)
                    P.do("vector", lambda e, t4=t4: e.reduce_max(
                        out=nmx[:, t4:t4 + 1], in_=ps_l[:, :], axis=AX.X, negate=True), inc=vv)
                    P.wait("scalar", vv)
                    P.do("scalar", lambda e, t4=t4: e.activation(
                        out=affc[:, E * t4:E * (t4 + 1)], in_=ps_l[:, :], func=AF.Exp,
                        bias=nmx[:, t4:t4 + 1], accum_out=rsum[:, t4:t4 + 1]), inc=aa)
                    P.wait("vector", aa)
                    P.do("vector", lambda e, t4=t4: e.reciprocal(
                        out=rsum[:, t4:t4 + 1], in_=rsum[:, t4:t4 + 1]), inc=vv)
                    P.wait("vector", vv)
                    P.do("vector", lambda e, t4=t4: e.tensor_scalar(
                        out=affc[:, E * t4:E * (t4 + 1)], in0=affc[:, E * t4:E * (t4 + 1)],
                        scalar1=rsum[:, t4:t4 + 1], scalar2=None, op0=ALU.mult), inc=vv)
                    if "afft" not in mid_skip:
                        P.wait("tensor", vv, aa)
                        P.do("tensor", lambda e, t4=t4: e.transpose(
                            ps_mr[0:8, 0:128], affc[:, E * t4:E * (t4 + 1)], identf[:, :]), inc=tt)
                        P.wait("vector", tt, st)
                        P.do("vector", lambda e: e.tensor_copy(
                            out=tmrow[:, :], in_=ps_mr[0:8, 0:128]), inc=vv)
                        P.wait("sync", vv)
                        P.dma("sync", ag3c_in[:, co + 128 * t4:co + 128 * (t4 + 1)], tmrow[:, :], st)
                    P.wait("tensor", aa, vv)
            P.wait("sync", st)
            P.barrier(allsems)

    if upto >= 3:
        _phase_3()
    if upto >= 4:
        P.wait("gpsimd", st)
        P.collective("AllGather", ALU.bypass, ag3a_in, ag3a_out, cc3a)
        P.collective("AllGather", ALU.bypass, ag3b_in, ag3b_out, cc3b)
        P.collective("AllGather", ALU.bypass, ag3c_in, ag3c_out, cc3c)
    def _phase_4():
        with P.scope():
            ncha = [math.ceil((48 + s + 1) / 4) for s in range(NS)]
            q_dt = P.sb("a_qdt", (128, 8 * NS * 128), BF16)
            acc = P.sb("a_acc", (128, NS * 1024), F32)
            wqk = P.sb("a_wqk", (128, 8 * D), BF16)
            wov = P.sb("a_wov", (128, 8 * D), BF16)
            kb = P.sb("a_kb", (128, 8 * KC), BF16)
            vb = P.sb("a_vb", (128, 4 * 1024), BF16)
            qxb = P.sb("a_qxb", (128, 8 * KC), BF16)
            wtmp = P.sb("a_wtmp", (128, D), F32)
            p_sb = P.sb("a_p", (128, KC), BF16)
            pt_sb = P.sb("a_pt", (128, KC), BF16)
            pen = P.sb("a_pen", (128, KC), F32)
            m_sb = P.sb("a_m", (128, NS), F32)
            l_sb = P.sb("a_l", (128, NS), F32)
            mx = P.sb("a_mx", (128, 1), F32)
            dif = P.sb("a_dif", (128, 1), F32)
            alp = P.sb("a_alp", (128, 1), F32)
            bcol = P.sb("a_bcol", (128, 1), F32)
            thr = P.sb("a_thr", (128, 1), F32)
            ls = P.sb("a_ls", (128, 1), F32)
            rcol = P.sb("a_rcol", (128, 1), F32)
            qidx = P.sb("a_qidx", (128, NS), F32)
            kio = P.sb("a_kio", (128, KC), F32)
            ident = P.sb("a_ident", (128, 128), BF16)
            cst = P.sb("a_cst", (128, 8), F32)
            ax = P.sb("a_ax", (128, 1024), BF16)
            axt = P.sb("a_axt", (128, 8 * 128), BF16)
            kidx = P.sb("a_kidx", (128, 16 * 8), I32)
            vidx = P.sb("a_vidx", (128, 16 * 4), I32)
            ps_s = P.ps("a_pss", (128, KC))
            ps_t = P.ps("a_pst", (128, KC), BF16)
            ps_av = P.ps("a_psav", (128, 1024))
            ps_q = P.ps("a_psq", (128, KC))

            for k in range(8):
                P.dma("sync", wtmp[:, :], qk_i[k, :, :], ld)
                P.wait("vector", ld)
                P.do("vector", lambda e, k=k: e.tensor_copy(
                    out=wqk[:, D * k:D * (k + 1)], in_=wtmp[:, :]), inc=vv)
                P.wait("sync", vv)
                P.dma("sync", wtmp[:, :], ov_i[k, :, :], ld)
                P.wait("vector", ld)
                P.do("vector", lambda e, k=k: e.tensor_copy(
                    out=wov[:, D * k:D * (k + 1)], in_=wtmp[:, :]), inc=vv)
                P.wait("sync", vv)
            P.dma("sync", qidx[:, :], qidx_i, ld, fence=False)
            P.dma("sync", kio[:, :], kio_i, ld, fence=False)
            P.dma("sync", cst[:, :], cst_i, ld, fence=False)
            P.dma("sync", kidx[:, :], kidx_i, ld, fence=False)
            P.dma("sync", vidx[:, :], vidx_i, ld, fence=False)
            P.fence("sync", ld)
            P.dma("gpsimd", ident[:, :], ident_i, gd)
            P.wait("vector", ld)
            P.do("vector", lambda e: e.memset(m_sb[:, :], -1e30), inc=vv)
            P.do("vector", lambda e: e.memset(l_sb[:, :], 0.0), inc=vv)
            for s in range(NS):
                P.do("vector", lambda e, s=s: e.memset(acc[:, 1024 * s:1024 * (s + 1)], 0.0), inc=vv)
            for qc in range(4):
                P.wait("sync", vv, tt)
                for k in range(8):
                    P.dma("sync", qxb[:, KC * k:KC * (k + 1)], xn1_scr[k, :, KC * qc:KC * (qc + 1)],
                          ld, fence=False)
                P.fence("sync", ld)
                P.wait("tensor", ld, vv)
                for i in range(8):
                    for k in range(8):
                        P.do("tensor", lambda e, i=i, k=k: e.matmul(
                            ps_q[:, :], wqk[:, D * k + 128 * i:D * k + 128 * (i + 1)],
                            qxb[:, KC * k:KC * (k + 1)], start=(k == 0), stop=(k == 7)),
                            inc=(tt if k == 7 else None))
                    P.wait("vector", tt)
                    P.do("vector", lambda e, i=i, qc=qc: e.tensor_copy(
                        out=q_dt[:, NS * 128 * i + KC * qc:NS * 128 * i + KC * (qc + 1)],
                        in_=ps_q[:, :]), inc=vv)
                    P.wait("tensor", vv)
            for j in range(16):
                s_list = [s for s in range(NS) if ncha[s] > j]
                if not s_list:
                    continue
                P.wait("gpsimd", vv, aa, tt, st, ld)
                if j == 0:
                    P.wait("gpsimd", cc3a, cc3b)
                for kd in range(8):
                    P.do("gpsimd", lambda e, j=j, kd=kd: e.indirect_dma_start(
                        out=kb[:, KC * kd:KC * (kd + 1)], out_offset=None,
                        in_=ag3a_out, element_offset=(j % 4) * KC,
                        in_offset=IndirectOffsetOnAxis(kidx[:, j * 8 + kd:j * 8 + kd + 1], 0)),
                        inc=gd, k=16)
                for m4 in range(4):
                    P.do("gpsimd", lambda e, j=j, m4=m4: e.indirect_dma_start(
                        out=vb[:, 1024 * m4:1024 * (m4 + 1)], out_offset=None,
                        in_=ag3b_out,
                        in_offset=IndirectOffsetOnAxis(vidx[:, j * 4 + m4:j * 4 + m4 + 1], 0)),
                        inc=gd, k=16)
                P.fence("gpsimd", gd)
                for s in s_list:
                    P.wait("tensor", gd, aa, vv)
                    for k in range(8):
                        P.do("tensor", lambda e, s=s, k=k: e.matmul(
                            ps_s[:, :], q_dt[:, NS * 128 * k + 128 * s:NS * 128 * k + 128 * (s + 1)],
                            kb[:, KC * k:KC * (k + 1)], start=(k == 0), stop=(k == 7)),
                            inc=(tt if k == 7 else None))
                    P.wait("vector", tt)
                    if j >= s // 4:
                        P.do("vector", lambda e, s=s, j=j: e.tensor_scalar(
                            out=thr[:, :], in0=qidx[:, s:s + 1], scalar1=float(-KC * j),
                            scalar2=None, op0=ALU.add), inc=vv)
                        P.wait("vector", vv)
                        P.do("vector", lambda e: e.tensor_scalar(
                            out=pen[:, :], in0=kio[:, :], scalar1=thr[:, 0:1],
                            scalar2=cst[:, 6:7], op0=ALU.is_gt, op1=ALU.mult), inc=vv)
                        P.wait("vector", vv)
                        P.do("vector", lambda e: e.tensor_add(
                            out=ps_s[:, :], in0=ps_s[:, :], in1=pen[:, :]), inc=vv)
                        P.wait("vector", vv)
                    P.do("vector", lambda e: e.reduce_max(out=mx[:, :], in_=ps_s[:, :], axis=AX.X), inc=vv)
                    P.wait("vector", vv)
                    P.do("vector", lambda e, s=s: e.tensor_sub(
                        out=dif[:, :], in0=m_sb[:, s:s + 1], in1=mx[:, :]), inc=vv)
                    P.wait("vector", vv)
                    P.do("vector", lambda e: e.tensor_scalar(
                        out=dif[:, :], in0=dif[:, :], scalar1=0.0, scalar2=None,
                        op0=ALU.min), inc=vv)
                    P.do("vector", lambda e, s=s: e.tensor_max(
                        out=m_sb[:, s:s + 1], in0=m_sb[:, s:s + 1], in1=mx[:, :]), inc=vv)
                    P.wait("scalar", vv)
                    P.do("scalar", lambda e: e.activation(
                        out=alp[:, :], in_=dif[:, :], func=AF.Exp, scale=cst[:, 4:5]), inc=aa)
                    P.wait("vector", vv)
                    P.do("vector", lambda e, s=s: e.tensor_scalar(
                        out=bcol[:, :], in0=m_sb[:, s:s + 1], scalar1=-1.0 / 32, scalar2=None,
                        op0=ALU.mult), inc=vv)
                    P.wait("scalar", vv, aa)
                    P.do("scalar", lambda e: e.activation(
                        out=p_sb[:, :], in_=ps_s[:, :], func=AF.Exp, scale=cst[:, 4:5],
                        bias=bcol[:, 0:1], accum_out=ls[:, :]), inc=aa)
                    P.wait("vector", aa)
                    P.do("vector", lambda e, s=s: e.scalar_tensor_tensor(
                        out=l_sb[:, s:s + 1], in0=l_sb[:, s:s + 1], scalar=alp[:, 0:1],
                        in1=ls[:, :], op0=ALU.mult, op1=ALU.add), inc=vv)
                    P.wait("tensor", aa, vv, gd)
                    for m4 in range(4):
                        P.do("tensor", lambda e, m4=m4: e.transpose(
                            ps_t[:, 128 * m4:128 * (m4 + 1)], p_sb[:, 128 * m4:128 * (m4 + 1)],
                            ident[:, :]), inc=(tt if m4 == 3 else None))
                    P.wait("vector", tt)
                    P.do("vector", lambda e: e.tensor_copy(out=pt_sb[:, :], in_=ps_t[:, :]), inc=vv)
                    P.wait("tensor", vv)
                    for m4 in range(4):
                        for dh in range(2):
                            P.do("tensor", lambda e, m4=m4, dh=dh: e.matmul(
                                ps_av[:, 512 * dh:512 * (dh + 1)],
                                pt_sb[:, 128 * m4:128 * (m4 + 1)],
                                vb[:, 1024 * m4 + 512 * dh:1024 * m4 + 512 * (dh + 1)],
                                start=(m4 == 0), stop=(m4 == 3)),
                                inc=(tt if (m4 == 3 and dh == 1) else None))
                    P.wait("vector", tt)
                    for dh in range(2):
                        sl = acc[:, 1024 * s + 512 * dh:1024 * s + 512 * (dh + 1)]
                        P.do("vector", lambda e, sl=sl: e.tensor_scalar(
                            out=sl, in0=sl, scalar1=alp[:, 0:1], scalar2=None, op0=ALU.mult), inc=vv)
                    P.wait("vector", vv)
                    for dh in range(2):
                        sl = acc[:, 1024 * s + 512 * dh:1024 * s + 512 * (dh + 1)]
                        P.do("vector", lambda e, sl=sl, dh=dh: e.tensor_add(
                            out=sl, in0=sl, in1=ps_av[:, 512 * dh:512 * (dh + 1)]), inc=vv)
                    P.wait("tensor", vv)
            for s in range(NS):
                P.wait("vector", tt, aa, st)
                P.do("vector", lambda e, s=s: e.reciprocal(out=rcol[:, :], in_=l_sb[:, s:s + 1]), inc=vv)
                P.wait("vector", vv)
                P.do("vector", lambda e, s=s: e.tensor_scalar(
                    out=ax[:, :], in0=acc[:, 1024 * s:1024 * (s + 1)], scalar1=rcol[:, 0:1],
                    scalar2=None, op0=ALU.mult), inc=vv)
                P.wait("tensor", vv)
                for i8 in range(8):
                    P.do("tensor", lambda e, i8=i8: e.transpose(
                        ps_t[:, 128 * (i8 % 4):128 * (i8 % 4 + 1)], ax[:, 128 * i8:128 * (i8 + 1)],
                        ident[:, :]), inc=tt)
                    P.wait("vector", tt)
                    P.do("vector", lambda e, i8=i8: e.tensor_copy(
                        out=axt[:, 128 * i8:128 * (i8 + 1)],
                        in_=ps_t[:, 128 * (i8 % 4):128 * (i8 % 4 + 1)]), inc=vv)
                    P.wait("tensor", vv)
                for i in range(8):
                    for kd in range(8):
                        P.do("tensor", lambda e, i=i, kd=kd: e.matmul(
                            ps_q[:, 0:128], wov[:, D * kd + 128 * i:D * kd + 128 * (i + 1)],
                            axt[:, 128 * kd:128 * (kd + 1)], start=(kd == 0), stop=(kd == 7)),
                            inc=(tt if kd == 7 else None))
                    P.wait("vector", tt)
                    P.do("vector", lambda e, i=i: e.tensor_copy(
                        out=ax[:, 128 * i:128 * (i + 1)], in_=ps_q[:, 0:128]), inc=vv)
                    P.wait("tensor", vv)
                P.wait("sync", vv)
                for i in range(8):
                    P.dma("sync", attn_scr[i, :, 128 * s:128 * (s + 1)], ax[:, 128 * i:128 * (i + 1)],
                          st, fence=False)
                P.fence("sync", st)
            P.wait("sync", st)
            P.barrier(allsems)

    if upto >= 4:
        _phase_4()
    def _phase_5():
        with P.scope():
            NCH5 = T // CH5
            KTOP = 2048
            wg = P.sb("o_wg", (128, 8 * I2), BF16)
            wu = P.sb("o_wu", (128, 8 * I2), BF16)
            wd = P.sb("o_wd", (128, 16 * D), BF16)
            xc5 = P.sb("o_xc", (128, 8 * CH5), BF16)
            hh = P.sb("o_hh", (128, 16 * CH5), BF16)
            gsb = P.sb("o_gsb", (128, CH5), BF16)
            mgrep = P.sb("o_mgrep", (128, CH5), BF16)
            ob = P.sb("o_ob", (128, 8 * CH5), BF16)
            aff2 = P.sb("o_aff2", (128, B * 64), F32)
            affr_sh = P.sb("o_affr", (1, T), F32)
            mg0 = P.sb("o_mg0", (1, T), BF16)
            mg1 = P.sb("o_mg1", (1, T), BF16)
            bis = P.sb("o_bis", (1, 8), I32)
            cnt1 = P.sb("o_cnt1", (128, 1), F32)
            cmp64 = P.sb("o_cmp64", (128, 64), F32)
            cntb = P.sb("o_cntb", (128, 1), BF16)
            ones_bf = P.sb("o_onesbf", (128, 1), BF16)
            oner_f = P.sb("o_onerf", (1, 128), F32)
            oner_b = P.sb("o_onerb", (1, 128), BF16)
            tau = P.sb("o_tau", (1, 2), I32)
            aidx = P.sb("o_aidx", (8, 1), I32)
            affs = P.sb("o_affs", (8, TOKC), F32)
            ps_a = P.ps("o_psa", (128, CH5))
            ps_b = P.ps("o_psb", (128, CH5))
            ps_c = P.ps("o_psc", (128, CH5))
            ps_mr = P.ps("o_psmr", (128, CH5))
            ps_r = P.ps("o_psr", (128, 8))

            P.dma("sync", oner_f[:, :], ones_r, ld, fence=False)
            P.dma("sync", aidx[:, :], aidx_i, ld)
            P.dma("gpsimd", ones_bf[:, :], ones_c, gd, fence=False)
            P.dma("gpsimd", oner_b[:, :], ones_r, gd)
            P.wait("vector", ld, gd)
            P.do("vector", lambda e: e.memset(hh[:, :], 0.0), inc=vv)
            P.wait("sync", vv)
            rs_zv = rs_in.rearrange("(o p) t -> o p t", p=128)
            for o in range(64):
                P.dma("sync", rs_zv[o, :, :], hh[:, 0:TOKC], st, fence=False)
            P.fence("sync", st)
            P.wait("gpsimd", ld, cc3c)
            P.do("gpsimd", lambda e: e.indirect_dma_start(
                out=affs[:, :], out_offset=None, in_=ag3c_out,
                in_offset=IndirectOffsetOnAxis(aidx[:, :], 0)), inc=gd, k=16)
            P.fence("gpsimd", gd)
            P.dma("gpsimd", affg, affs[:, :], gd)
            affg_r = affg.rearrange("q c -> (q c)")
            ONE_F_BITS = 0x3F800000
            for b in range(B):
                P.wait("sync", vv, gd)
                P.dma("sync", affr_sh[:, :],
                      affg_r.rearrange("(b t) -> b t", b=2)[b:b + 1, :], ld)
                P.dma("sync", aff2[:, 64 * b:64 * (b + 1)],
                      affg_r.rearrange("(b p f) -> b p f", b=2, p=128)[b, :, :], ld)
                P.wait("vector", ld, gd)
                P.do("vector", lambda e: e.memset(bis[0:1, 0:1], 0), inc=vv)
                P.do("vector", lambda e: e.memset(bis[0:1, 1:2], ONE_F_BITS), inc=vv)
                a2 = aff2[:, 64 * b:64 * (b + 1)]
                for it in range(31):
                    P.wait("vector", vv, tt)
                    P.do("vector", lambda e: e.tensor_tensor(
                        out=bis[0:1, 2:3], in0=bis[0:1, 0:1], in1=bis[0:1, 1:2], op=ALU.add), inc=vv)
                    P.wait("vector", vv)
                    P.do("vector", lambda e: e.tensor_scalar(
                        out=bis[0:1, 2:3], in0=bis[0:1, 2:3], scalar1=1, scalar2=None,
                        op0=ALU.arith_shift_right), inc=vv)
                    P.wait("tensor", vv, ld)
                    P.do("tensor", lambda e: e.matmul(
                        ps_r[:, 0:1], oner_f[:, :], bis[0:1, 2:3].bitcast(F32),
                        start=True, stop=True), inc=tt)
                    P.wait("vector", tt)
                    P.do("vector", lambda e, a2=a2: e.tensor_scalar(
                        out=cmp64[:, :], in0=a2, scalar1=ps_r[:, 0:1], scalar2=None,
                        op0=ALU.is_ge, op1=ALU.add, accum_out=cnt1[:, 0:1]), inc=vv)
                    P.wait("vector", vv)
                    P.do("vector", lambda e: e.tensor_copy(out=cntb[:, :], in_=cnt1[:, :]), inc=vv)
                    P.wait("tensor", vv, gd)
                    P.do("tensor", lambda e: e.matmul(
                        ps_r[0:1, 1:2], ones_bf[:, :], cntb[:, :], start=True, stop=True), inc=tt)
                    P.wait("vector", tt)
                    P.do("vector", lambda e: e.tensor_scalar(
                        out=bis[0:1, 3:4], in0=ps_r[0:1, 1:2], scalar1=float(KTOP), scalar2=None,
                        op0=ALU.is_ge), inc=vv)
                    P.wait("vector", vv)
                    P.do("vector", lambda e: e.tensor_sub(
                        out=bis[0:1, 4:5], in0=bis[0:1, 2:3], in1=bis[0:1, 0:1]), inc=vv)
                    P.wait("vector", vv)
                    P.do("vector", lambda e: e.tensor_mul(
                        out=bis[0:1, 4:5], in0=bis[0:1, 4:5], in1=bis[0:1, 3:4]), inc=vv)
                    P.wait("vector", vv)
                    P.do("vector", lambda e: e.tensor_add(
                        out=bis[0:1, 0:1], in0=bis[0:1, 0:1], in1=bis[0:1, 4:5]), inc=vv)
                    P.do("vector", lambda e: e.tensor_sub(
                        out=bis[0:1, 5:6], in0=bis[0:1, 1:2], in1=bis[0:1, 2:3]), inc=vv)
                    P.do("vector", lambda e: e.tensor_scalar(
                        out=bis[0:1, 6:7], in0=bis[0:1, 3:4], scalar1=-1, scalar2=None,
                        op0=ALU.add), inc=vv)
                    P.wait("vector", vv)
                    P.do("vector", lambda e: e.tensor_mul(
                        out=bis[0:1, 5:6], in0=bis[0:1, 5:6], in1=bis[0:1, 6:7]), inc=vv)
                    P.wait("vector", vv)
                    P.do("vector", lambda e: e.tensor_add(
                        out=bis[0:1, 1:2], in0=bis[0:1, 1:2], in1=bis[0:1, 5:6]), inc=vv)
                P.wait("vector", vv)
                P.do("vector", lambda e, b=b: e.tensor_copy(out=tau[0:1, b:b + 1], in_=bis[0:1, 0:1]), inc=vv)
                P.wait("vector", vv)
                mgr = (mg0, mg1)[b]
                P.do("vector", lambda e, mgr=mgr, b=b: e.scalar_tensor_tensor(
                    out=mgr[0:1, :], in0=affr_sh[0:1, :], scalar=tau[0:1, b:b + 1].bitcast(F32),
                    in1=affr_sh[0:1, :], op0=ALU.is_ge, op1=ALU.mult), inc=vv)
            for ih in range(2):
                P.wait("gpsimd", tt, aa, vv, st)
                for k in range(8):
                    P.dma("gpsimd", wg[:, I2 * k:I2 * (k + 1)], wg5_i[ih, k, :, :], gd, fence=False)
                    P.dma("gpsimd", wu[:, I2 * k:I2 * (k + 1)], wu5_i[ih, k, :, :], gd, fence=False)
                for k in range(16):
                    P.dma("gpsimd", wd[:, D * k:D * (k + 1)], wd5_i[ih, k, :, :], gd, fence=False)
                P.fence("gpsimd", gd)
                for b in range(B):
                    mgr = (mg0, mg1)[b]
                    for c5 in range(NCH5):
                        lo5 = CH5 * c5
                        gco = b * T + lo5
                        ow = gco // TOKC
                        tloc = gco % TOKC
                        q5 = gco // TOKC
                        P.wait("sync", vv, aa, tt, st)
                        for kd in range(8):
                            P.dma("sync", xc5[:, CH5 * kd:CH5 * (kd + 1)],
                                  ag3aov[q5 * 8 + kd, :, tloc:tloc + CH5], ld, fence=False)
                        P.fence("sync", ld)
                        P.wait("tensor", vv, ld, gd)
                        P.do("tensor", lambda e, mgr=mgr, lo5=lo5: e.matmul(
                            ps_mr[:, :], oner_b[:, :], mgr[0:1, lo5:lo5 + CH5],
                            start=True, stop=True), inc=tt)
                        P.wait("vector", tt)
                        P.do("vector", lambda e: e.tensor_copy(out=mgrep[:, :], in_=ps_mr[:, :]), inc=vv)
                        for ic in range(16):
                            P.wait("tensor", aa, vv)
                            for k in range(8):
                                P.do("tensor", lambda e, ic=ic, k=k: e.matmul(
                                    ps_a[:, :], wg[:, I2 * k + 128 * ic:I2 * k + 128 * (ic + 1)],
                                    xc5[:, CH5 * k:CH5 * (k + 1)], start=(k == 0), stop=(k == 7)),
                                    inc=(tt if k == 7 else None))
                            for k in range(8):
                                P.do("tensor", lambda e, ic=ic, k=k: e.matmul(
                                    ps_b[:, :], wu[:, I2 * k + 128 * ic:I2 * k + 128 * (ic + 1)],
                                    xc5[:, CH5 * k:CH5 * (k + 1)], start=(k == 0), stop=(k == 7)),
                                    inc=(tt if k == 7 else None))
                            P.wait("scalar", tt)
                            P.do("scalar", lambda e: e.activation(
                                out=gsb[:, :], in_=ps_a[:, :], func=AF.Silu), inc=aa)
                            P.wait("vector", aa, tt)
                            P.do("vector", lambda e, ic=ic: e.tensor_mul(
                                out=hh[:, CH5 * ic:CH5 * (ic + 1)], in0=ps_b[:, :],
                                in1=gsb[:, :]), inc=vv)
                        for i in range(8):
                            P.wait("tensor", vv, aa)
                            for k in range(16):
                                P.do("tensor", lambda e, i=i, k=k: e.matmul(
                                    ps_c[:, :], wd[:, D * k + 128 * i:D * k + 128 * (i + 1)],
                                    hh[:, CH5 * k:CH5 * (k + 1)], start=(k == 0), stop=(k == 15)),
                                    inc=(tt if k == 15 else None))
                            P.wait("vector", tt, st, gd)
                            P.do("vector", lambda e, i=i: e.tensor_mul(
                                out=ob[:, CH5 * i:CH5 * (i + 1)], in0=ps_c[:, :],
                                in1=mgrep[:, :]), inc=vv)
                            P.wait("tensor", vv)
                        P.wait("gpsimd", vv)
                        for i in range(8):
                            if ih == 0:
                                P.do("gpsimd", lambda e, i=i, ow=ow, tloc=tloc: e.dma_start(
                                    out=rsv[ow, i, :, tloc:tloc + CH5],
                                    in_=ob[:, CH5 * i:CH5 * (i + 1)], accum_op=ALU.add),
                                    inc=gd, k=16)
                            else:
                                P.do("gpsimd", lambda e, i=i, ow=ow, tloc=tloc: e.dma_start(
                                    out=rsv[ow, i, :, tloc:tloc + CH5],
                                    in_=ob[:, CH5 * i:CH5 * (i + 1)], accum_op=ALU.add),
                                    inc=gd, k=16)
                        P.fence("gpsimd", gd)
            P.wait("sync", st)
            P.wait("gpsimd", gd)
            P.barrier(allsems)

    if upto >= 5:
        _phase_5()
    if upto >= 6:
        P.collective("ReduceScatter", ALU.add, rs_in, rs_out, cc4)

    def _phase_6():
        with P.scope():
            x1t = P.sb("f_x1t", (128, TOKC), F32)
            bt = P.sb("f_bt", (128, TOKC), BF16)
            btf = P.sb("f_btf", (128, TOKC), F32)
            P.wait("sync", cc4)
            for k in range(8):
                P.wait("sync", vv, st)
                P.dma("sync", x1t[:, :], x1_scr[k, :, :], ld)
                for si in range(3):
                    src = (attn_scr[k, :, :], rsov[k, :, :], None)[si] if si < 2 else None
                    if si == 2:
                        break
                    P.wait("sync", vv)
                    P.dma("sync", bt[:, :], src, ld)
                    P.wait("vector", ld)
                    P.do("vector", lambda e: e.tensor_copy(out=btf[:, :], in_=bt[:, :]), inc=vv)
                    P.wait("vector", vv)
                    P.do("vector", lambda e: e.tensor_add(
                        out=x1t[:, :], in0=x1t[:, :], in1=btf[:, :]), inc=vv)
                P.wait("sync", vv)
                P.dma("sync", out_o[k, :, :], x1t[:, :], st)
            P.wait("sync", st)
            P.wait("gpsimd", st)
    if upto >= 6:
        _phase_6()
    else:
        def _stub():
            with P.scope():
                zt = P.sb("z_zt", (128, TOKC), F32)
                P.barrier(allsems)
                P.do("vector", lambda e: e.memset(zt[:, :], 0.0), inc=vv)
                P.wait("sync", vv)
                for k in range(8):
                    P.dma("sync", out_o[k, :, :], zt[:, :], st)
                P.wait("sync", st)
        _stub()
    return P.finish()


def host_fused_inputs(inputs):
    x_dt = np.ascontiguousarray(inputs["L0_attn_x"].reshape(NTOK, D).T)
    inpw_t = np.ascontiguousarray(inputs["L0_attn_in_proj_w"].T.reshape(8, 128, 2 * DI))
    norm_w = np.ascontiguousarray(inputs["L0_attn_norm_w"].reshape(8, 128, 1))
    conv_wa = np.ascontiguousarray(
        inputs["L0_attn_conv_w"].reshape(DI, 4).reshape(16, 128, 4).transpose(1, 0, 2).reshape(128, 64))
    conv_ba = np.ascontiguousarray(inputs["L0_attn_conv_b"].reshape(16, 128).T)
    xpw_t = np.ascontiguousarray(inputs["L0_attn_x_proj_w"].T.reshape(16, 128, 96))
    dtpw_t = np.ascontiguousarray(inputs["L0_attn_dt_proj_w"].T)
    dtb_a = np.ascontiguousarray(inputs["L0_attn_dt_proj_b"].reshape(16, 128).T)
    ones_c = np.ones((128, 1), np.float32)
    ones_r = np.ones((1, 128), np.float32)
    cst_arr = np.tile(np.array([1.0, -1.0, -0.5, LN_EPS, 1.0 / 32, 0, -1e9, 0],
                               np.float32), (128, 1))
    sel = np.zeros((16, 128, 128), np.float32)
    for p in range(16):
        for dh in range(8):
            for n in range(16):
                sel[p, 8 * p + dh, 16 * dh + n] = 1.0
    nselm = np.ascontiguousarray(np.transpose(sel, (0, 2, 1)))
    sbsel = np.zeros((16, 128), np.float32)
    for n in range(16):
        for dh in range(8):
            sbsel[n, 16 * dh + n] = 1.0
    opw_t = np.ascontiguousarray(inputs["L0_attn_out_proj_w"].T.reshape(16, 128, D))
    wg_t = np.ascontiguousarray(inputs["L0_ffn_Wg"].T.reshape(8, 128, 8))
    lnsc_i = np.ascontiguousarray(inputs["L0_ffn_ln_scale"].T.reshape(8, 128, 8))
    lnbi_i = np.ascontiguousarray(inputs["L0_ffn_ln_bias"].T.reshape(8, 128, 8))
    w1_i = np.ascontiguousarray(inputs["L0_ffn_W1"].reshape(8, 8, 128, 256))
    b1_i = np.ascontiguousarray(
        inputs["L0_ffn_b1"].reshape(8, 2, 128).transpose(2, 0, 1).reshape(128, 16))
    w2_i = np.ascontiguousarray(inputs["L0_ffn_W2"].reshape(8, 2, 128, D))
    b2_i = np.ascontiguousarray(inputs["L0_ffn_b2"].reshape(1, 8 * D))
    gw_t = np.ascontiguousarray(inputs["L1_ffn_gate_w"].T.reshape(8, 128, 8))
    ident = np.eye(128, dtype=np.float32)
    qk_i = np.ascontiguousarray(inputs["L1_attn_qk"].reshape(8, 128, D))
    ov_i = np.ascontiguousarray(inputs["L1_attn_ov"].reshape(8, 128, D))
    kio = np.tile(np.arange(512, dtype=np.float32), (128, 1))
    prng = np.arange(128)
    maps = []
    for c in range(NC):
        lo = c * TOKC
        prev = np.zeros((D, 3), np.float32) if lo % T == 0 else x_dt[:, lo - 3:lo]
        beta = c // 4
        ch0 = 256 * c
        alog_a = np.zeros((128, 32), np.float32)
        A_log = inputs["L0_attn_A_log"]
        for g in range(32):
            for dh in range(8):
                for n in range(16):
                    alog_a[16 * dh + n, g] = A_log[ch0 + 8 * g + dh, n]
        d_a = np.ascontiguousarray(inputs["L0_attn_D"][ch0:ch0 + 256].reshape(2, 128).T)
        sidx = np.zeros((128, 32), np.int32)
        for rr in range(2):
            for q in range(8):
                sidx[:, rr * 8 + q] = (q * 33 + 2 * c + rr) * 128 + prng
                sidx[:, 16 + rr * 8 + q] = (q * 33 + 16 + 2 * c + rr) * 128 + prng
        yidx = np.zeros((128, 16), np.int32)
        for g in range(16):
            q, rr = g // 2, g % 2
            yidx[:, g] = q * 2048 + (c * 2 + rr) * 128 + prng
        qidx = np.zeros((128, 16), np.float32)
        for s in range(16):
            qidx[:, s] = (16 * (c % 4) + s) * 128 + prng
        kidx = np.zeros((128, 16 * 8), np.int32)
        for j in range(16):
            gj = beta * 16 + j
            for kd in range(8):
                kidx[:, j * 8 + kd] = (gj // 4) * 1024 + kd * 128 + prng
        vidx = np.zeros((128, 16 * 4), np.int32)
        for j in range(16):
            gj = beta * 16 + j
            for m4 in range(4):
                vidx[:, j * 4 + m4] = gj * 512 + m4 * 128 + prng
        aidx = (np.arange(8, dtype=np.int32) * 8 + c).reshape(8, 1)
        wg5 = np.ascontiguousarray(
            inputs["L1_ffn_w_gate"][c].T.reshape(8, 128, 4096)
            .reshape(8, 128, 2, 2048).transpose(2, 0, 1, 3))
        wu5 = np.ascontiguousarray(
            inputs["L1_ffn_w_up"][c].T.reshape(8, 128, 4096)
            .reshape(8, 128, 2, 2048).transpose(2, 0, 1, 3))
        wd5 = np.ascontiguousarray(inputs["L1_ffn_w_down"][c].T.reshape(2, 16, 128, D))
        maps.append({
            "x_ext": _pad3(x_dt[:, lo:lo + TOKC], prev),
            "inpw_t": inpw_t, "norm_w": norm_w, "conv_wa": conv_wa,
            "conv_ba": conv_ba, "xpw_t": xpw_t, "dtpw_t": dtpw_t, "dtb_a": dtb_a,
            "ones_c": ones_c, "ones_r": ones_r, "cst_i": cst_arr,
            "alog_a": alog_a, "d_i": d_a, "sel_i": sel, "nsel_i": nselm,
            "sbsel_i": sbsel, "sidx_i": sidx, "yidx_i": yidx,
            "opw_t": opw_t, "wg_t": wg_t, "lnsc_i": lnsc_i, "lnbi_i": lnbi_i,
            "w1_i": w1_i, "b1_i": b1_i, "w2_i": w2_i, "b2_i": b2_i, "gw_t": gw_t,
            "ident_i": ident, "qk_i": qk_i, "ov_i": ov_i, "qidx_i": qidx,
            "kio_i": kio, "kidx_i": kidx, "vidx_i": vidx, "aidx_i": aidx,
            "wg5_i": wg5, "wu5_i": wu5, "wd5_i": wd5,
        })
    return maps


def kernel(**inputs):
    inputs = {k: np.asarray(v) for k, v in inputs.items()}
    nc = build_fused()
    maps = host_fused_inputs(inputs)
    res = run_bass_kernel_spmd(nc, maps, core_ids=list(range(NC)))
    out_dt = np.concatenate(
        [res.results[c]["out_o"].reshape(D, TOKC) for c in range(NC)], axis=1)
    return np.ascontiguousarray(out_dt.T.reshape(B, T, D)).astype(np.float32)


def kernel_unfused(**inputs):
    inputs = {k: np.asarray(v) for k, v in inputs.items()}
    fo = run_front(inputs)
    y_full = run_scan(fo, inputs)
    mo = run_mid(inputs, fo["xn_o"], fo["res_o"], y_full)
    attn_full = run_attn(inputs, mo["xn1"])
    dense_list = run_moe(inputs, mo["xn1"], mo["aff"])
    out_dt = run_final(mo["x1"], attn_full, dense_list)
    return np.ascontiguousarray(out_dt.T.reshape(B, T, D)).astype(np.float32)


# version 05
# speedup vs baseline: 1.2484x; 1.2484x over previous
import numpy as np
import ml_dtypes
from contextlib import ExitStack
import concourse.bass as bass
import concourse.mybir as mybir
from concourse.bass_utils import run_bass_kernel_spmd
from concourse.bass_core_rust import IndirectOffsetOnAxis

F32 = mybir.dt.float32
BF16 = mybir.dt.bfloat16
I32 = mybir.dt.int32
AF = mybir.ActivationFunctionType
ALU = mybir.AluOpType
AX = mybir.AxisListType
BF = ml_dtypes.bfloat16

B, T, D, DI, DS, DTR = 2, 8192, 1024, 2048, 16, 64
NTOK = B * T
NC = 8
TOKC = NTOK // NC
CH = 512
LN_EPS = 1e-5

ENGS = ("sync", "vector", "scalar", "tensor", "gpsimd")


class Sem:
    def __init__(self, prog, name):
        self.h = prog.es.enter_context(prog.nc.semaphore(name))
        self.n = 0


class Prog:

    def __init__(self, nc):
        self.nc = nc
        self.es = ExitStack()
        self.q = {e: [] for e in ENGS}
        self._sb = {}

    def sem(self, name):
        return Sem(self, name)

    def sb(self, name, shape, dtype):
        t = self.es.enter_context(self.nc.sbuf_tensor(name, list(shape), dtype))
        self._sb[name] = t
        return t

    def ps(self, name, shape, dtype=F32):
        t = self.es.enter_context(self.nc.psum_tensor(name, list(shape), dtype))
        self._sb[name] = t
        return t

    def do(self, eng, fn, inc=None, k=1):
        if inc is not None:
            inc.n += k
        self.q[eng].append((fn, inc, k))

    def wait(self, eng, *sems):
        for s in sems:
            v = s.n
            if v > 0:
                self.q[eng].append((None, s, v))

    def scope(self):
        from contextlib import contextmanager

        @contextmanager
        def _s():
            saved_sb = self.nc.sbuf_base
            saved_ps = self.nc.psum_base
            inner = ExitStack()
            outer_es = self.es
            self.es = inner
            try:
                yield
            finally:
                self.es = outer_es
                inner.close()
                self.nc.sbuf_base = saved_sb
                self.nc.psum_base = saved_ps
        return _s()

    def barrier(self, sems):
        for e in ENGS:
            self.wait(e, *sems)

    mock_cc = False

    def collective(self, kind, op, in_ap, out_ap, sem):
        if Prog.mock_cc:
            nrow = in_ap.shape[0]
            if kind == "AllGather":
                for r in range(NC):
                    self.do("gpsimd", lambda e, i=in_ap, o=out_ap, n=nrow, r=r:
                            e.dma_start(out=o[r * n:(r + 1) * n, :], in_=i),
                            inc=sem, k=16)
            else:
                self.do("gpsimd", lambda e, i=in_ap, o=out_ap, n=out_ap.shape[0]:
                        e.dma_start(out=o, in_=i[0:n, :]), inc=sem, k=16)
            return
        self.do("gpsimd", lambda e, i=in_ap, o=out_ap, k=kind, p=op:
                e.collective_compute(k, p, replica_groups=[list(range(NC))],
                                     ins=[i.opt()], outs=[o.opt()]), inc=sem, k=1)

    def dma(self, eng, out, in_, sem, fence=True):
        self.do(eng, lambda e, o=out, i=in_: e.dma_start(out=o, in_=i), inc=sem, k=16)
        if fence:
            self.wait(eng, sem)

    def fence(self, eng, sem):
        self.wait(eng, sem)

    def finish(self):
        blk = self.es.enter_context(self.nc.Block())
        for name in ENGS:
            ops = self.q[name]

            def make(ops=ops):
                def run(eng):
                    for fn, sem, k in ops:
                        if fn is None:
                            eng.wait_ge(sem.h, k)
                        else:
                            inst = fn(eng)
                            if sem is not None:
                                inst.then_inc(sem.h, k)
                return run
            getattr(blk, name)(make())
        self.es.close()
        return self.nc


def _nc():
    return bass.Bass("TRN2", target_bir_lowering=False)


def build_front(upto='all', nch=None, upto_sub='all'):
    nc = _nc()
    P = Prog(nc)
    NH = 3
    CE = CH + NH
    NCH = TOKC // CH

    x_ext = nc.dram_tensor("x_ext", [8, 128, NH + TOKC], F32, kind="ExternalInput").ap()
    inpw_t = nc.dram_tensor("inpw_t", [8, 128, 2 * DI], F32, kind="ExternalInput").ap()
    norm_w = nc.dram_tensor("norm_w", [8, 128, 1], F32, kind="ExternalInput").ap()
    conv_wa = nc.dram_tensor("conv_wa", [128, 16 * 4], F32, kind="ExternalInput").ap()
    conv_ba = nc.dram_tensor("conv_ba", [128, 16], F32, kind="ExternalInput").ap()
    xpw_t = nc.dram_tensor("xpw_t", [16, 128, 96], F32, kind="ExternalInput").ap()
    dtpw_t = nc.dram_tensor("dtpw_t", [64, DI], F32, kind="ExternalInput").ap()
    dtb_a = nc.dram_tensor("dtb_a", [128, 16], F32, kind="ExternalInput").ap()
    ones_c = nc.dram_tensor("ones_c", [128, 1], F32, kind="ExternalInput").ap()
    cst_i = nc.dram_tensor("cst_i", [128, 8], F32, kind="ExternalInput").ap()
    ones_r = nc.dram_tensor("ones_r", [1, 128], F32, kind="ExternalInput").ap()

    xn_o = nc.dram_tensor("xn_o", [8, 128, TOKC], F32, kind="ExternalOutput").ap()
    res_o = nc.dram_tensor("res_o", [16, 128, TOKC], BF16, kind="ExternalOutput").ap()
    u_o = nc.dram_tensor("u_o", [16, 128, TOKC], BF16, kind="ExternalOutput").ap()
    dl_o = nc.dram_tensor("dl_o", [16, 128, TOKC], BF16, kind="ExternalOutput").ap()
    bc_o = nc.dram_tensor("bc_o", [32, TOKC], F32, kind="ExternalOutput").ap()

    w_in = P.sb("w_in", (128, 8 * 2 * DI), BF16)
    w_xp = P.sb("w_xp", (128, 16 * 96), BF16)
    w_dt = P.sb("w_dt", (64, DI), BF16)
    wtmp = P.sb("wtmp", (128, 2 * DI), F32)
    cw = P.sb("cw", (128, 64), F32)
    cb = P.sb("cb", (128, 16), F32)
    dtb = P.sb("dtb", (128, 16), F32)
    nwc = P.sb("nwc", (128, 8), F32)
    ones_bf = P.sb("ones_bf", (128, 1), BF16)
    cst = P.sb("cst", (128, 8), F32)
    oner_f = P.sb("oner_f", (1, 128), F32)
    xc = P.sb("xc", (128, 8 * CH), F32)
    xbf = P.sb("xbf", (128, 8 * CH), BF16)
    sq = P.sb("sq", (128, 8 * CH), BF16)
    xnt = P.sb("xnt", (128, 8 * CH), F32)
    ht = P.sb("ht", (128, 8 * CH), BF16)
    srow = P.sb("srow", (1, 6 * CH), F32)
    xs_ext = P.sb("xs_ext", (128, 16 * CE), BF16)
    rbuf = P.sb("rbuf", (128, 2 * CH), BF16)
    convt = P.sb("convt", (128, CH), F32)
    u_full = P.sb("u_full", (128, 16 * CH), BF16)
    din_bf = P.sb("din_bf", (64, CH), BF16)
    bc_sb = P.sb("bc_sb", (32, CH), F32)
    sp1 = P.sb("sp1", (128, CH), F32)
    sp2 = P.sb("sp2", (128, CH), F32)
    dbuf = P.sb("dbuf", (128, 2 * CH), BF16)

    ps_s1 = P.ps("ps_s1", (1, CH))
    ps_s2 = P.ps("ps_s2", (1, CH))
    ps_r1 = P.ps("ps_r1", (128, CH))
    ps_r2 = P.ps("ps_r2", (128, CH))
    ps_a = P.ps("ps_a", (128, CH))
    ps_b = P.ps("ps_b", (128, CH))
    ps_x = P.ps("ps_x", (96, CH))

    ld = P.sem("ld")
    gd = P.sem("gd")
    vv = P.sem("vv")
    aa = P.sem("aa")
    tt = P.sem("tt")
    st = P.sem("st")

    for k in range(8):
        P.dma("sync", wtmp[:, :], inpw_t[k, :, :], ld)
        P.dma("sync", nwc[:, k:k + 1], norm_w[k, :, :], ld)
        P.wait("vector", ld)
        P.do("vector", lambda e, k=k: e.tensor_scalar(
            out=w_in[:, 4096 * k:4096 * (k + 1)], in0=wtmp[:, :],
            scalar1=nwc[:, k:k + 1], scalar2=None, op0=ALU.mult), inc=vv)
        P.wait("sync", vv)
    for k in range(16):
        P.dma("sync", wtmp[:, 0:96], xpw_t[k, :, :], ld)
        P.wait("vector", ld)
        P.do("vector", lambda e, k=k: e.tensor_copy(
            out=w_xp[:, 96 * k:96 * (k + 1)], in_=wtmp[:, 0:96]), inc=vv)
        P.wait("sync", vv)
    P.dma("sync", wtmp[0:64, 0:DI], dtpw_t, ld)
    P.wait("vector", ld)
    P.do("vector", lambda e: e.tensor_copy(out=w_dt[:, :], in_=wtmp[0:64, 0:DI]), inc=vv)
    P.dma("sync", cw[:, :], conv_wa, ld)
    P.dma("sync", cb[:, :], conv_ba, ld)
    P.dma("sync", dtb[:, :], dtb_a, ld)
    P.dma("gpsimd", ones_bf[:, :], ones_c, gd)
    P.dma("sync", cst[:, :], cst_i, ld)
    P.dma("sync", oner_f[:, :], ones_r, ld)

    MU, EX2, VAR, RSTD, MS, RSTD2 = range(6)

    def row(i):
        return srow[0:1, CH * i:CH * i + CH]

    def chunk_barrier():
        P.wait("sync", vv, aa, tt)
        P.wait("vector", tt, aa)
        P.wait("scalar", tt, vv)
        P.wait("tensor", vv, aa)
        P.wait("gpsimd", vv)

    def ln_rms_chunk(n, col0, halo):
        chunk_barrier()
        for k in range(8):
            P.dma("sync", xc[:, CH * k:CH * k + n], x_ext[k, :, col0:col0 + n], ld)
        P.wait("vector", ld)
        P.wait("scalar", ld)
        for k in range(8):
            P.do("vector", lambda e, k=k: e.tensor_copy(
                out=xbf[:, CH * k:CH * k + n], in_=xc[:, CH * k:CH * k + n]), inc=vv)
            P.do("scalar", lambda e, k=k: e.activation(
                out=sq[:, CH * k:CH * k + n], in_=xc[:, CH * k:CH * k + n],
                func=AF.Square), inc=aa)
        P.wait("tensor", vv, aa, gd)
        for k in range(8):
            P.do("tensor", lambda e, k=k: e.matmul(
                ps_s1[0:1, 0:n], ones_bf[:, :], xbf[:, CH * k:CH * k + n],
                start=(k == 0), stop=(k == 7)), inc=(tt if k == 7 else None))
        for k in range(8):
            P.do("tensor", lambda e, k=k: e.matmul(
                ps_s2[0:1, 0:n], ones_bf[:, :], sq[:, CH * k:CH * k + n],
                start=(k == 0), stop=(k == 7)), inc=(tt if k == 7 else None))
        P.wait("vector", tt)
        P.do("vector", lambda e: e.tensor_scalar(
            out=row(MU)[:, 0:n], in0=ps_s1[0:1, 0:n], scalar1=1.0 / D,
            scalar2=None, op0=ALU.mult), inc=vv)
        P.do("vector", lambda e: e.tensor_scalar(
            out=row(EX2)[:, 0:n], in0=ps_s2[0:1, 0:n], scalar1=1.0 / D,
            scalar2=None, op0=ALU.mult), inc=vv)
        P.wait("vector", vv)
        P.do("vector", lambda e: e.tensor_mul(
            out=row(VAR)[:, 0:n], in0=row(MU)[:, 0:n], in1=row(MU)[:, 0:n]), inc=vv)
        P.wait("vector", vv)
        P.do("vector", lambda e: e.tensor_sub(
            out=row(VAR)[:, 0:n], in0=row(EX2)[:, 0:n], in1=row(VAR)[:, 0:n]), inc=vv)
        P.wait("scalar", vv)
        P.do("scalar", lambda e: e.activation(
            out=row(RSTD)[:, 0:n], in_=row(VAR)[:, 0:n], func=AF.Ln, bias=cst[0:1, 3:4]), inc=aa)
        P.wait("scalar", aa)
        P.do("scalar", lambda e: e.activation(
            out=row(RSTD)[:, 0:n], in_=row(RSTD)[:, 0:n], func=AF.Exp, scale=cst[0:1, 2:3]), inc=aa)
        P.wait("tensor", aa)
        P.do("tensor", lambda e: e.matmul(
            ps_r1[:, 0:n], oner_f[:, :], row(MU)[:, 0:n], start=True, stop=True), inc=tt)
        P.do("tensor", lambda e: e.matmul(
            ps_r2[:, 0:n], oner_f[:, :], row(RSTD)[:, 0:n], start=True, stop=True), inc=tt)
        P.wait("vector", tt)
        for k in range(8):
            P.do("vector", lambda e, k=k: e.tensor_sub(
                out=xnt[:, CH * k:CH * k + n], in0=xc[:, CH * k:CH * k + n],
                in1=ps_r1[:, 0:n]), inc=vv)
        P.wait("vector", vv)
        for k in range(8):
            P.do("vector", lambda e, k=k: e.tensor_mul(
                out=xnt[:, CH * k:CH * k + n], in0=xnt[:, CH * k:CH * k + n],
                in1=ps_r2[:, 0:n]), inc=vv)
        if not halo:
            P.wait("sync", vv)
            for k in range(8):
                P.dma("sync", xn_o[k, :, col0 - NH:col0 - NH + n],
                      xnt[:, CH * k:CH * k + n], st)
        P.wait("scalar", vv)
        for k in range(8):
            P.do("scalar", lambda e, k=k: e.activation(
                out=sq[:, CH * k:CH * k + n], in_=xnt[:, CH * k:CH * k + n],
                func=AF.Square), inc=aa)
        P.wait("tensor", aa, vv)
        for k in range(8):
            P.do("tensor", lambda e, k=k: e.matmul(
                ps_s1[0:1, 0:n], ones_bf[:, :], sq[:, CH * k:CH * k + n],
                start=(k == 0), stop=(k == 7)), inc=(tt if k == 7 else None))
        P.wait("vector", tt)
        P.do("vector", lambda e: e.tensor_scalar(
            out=row(MS)[:, 0:n], in0=ps_s1[0:1, 0:n], scalar1=1.0 / D,
            scalar2=None, op0=ALU.mult), inc=vv)
        P.wait("scalar", vv)
        P.do("scalar", lambda e: e.activation(
            out=row(RSTD2)[:, 0:n], in_=row(MS)[:, 0:n], func=AF.Ln, bias=cst[0:1, 3:4]), inc=aa)
        P.wait("scalar", aa)
        P.do("scalar", lambda e: e.activation(
            out=row(RSTD2)[:, 0:n], in_=row(RSTD2)[:, 0:n], func=AF.Exp, scale=cst[0:1, 2:3]), inc=aa)
        P.wait("tensor", aa)
        P.do("tensor", lambda e: e.matmul(
            ps_r1[:, 0:n], oner_f[:, :], row(RSTD2)[:, 0:n], start=True, stop=True), inc=tt)
        P.wait("vector", tt)
        for k in range(8):
            P.do("vector", lambda e, k=k: e.tensor_mul(
                out=ht[:, CH * k:CH * k + n], in0=xnt[:, CH * k:CH * k + n],
                in1=ps_r1[:, 0:n]), inc=vv)

    def in_proj_chunk(n, c, halo):
        irange = range(16) if halo else range(32)
        P.wait("tensor", vv)
        for i in irange:
            pm = (ps_a, ps_b)[i % 2]
            for k in range(8):
                P.do("tensor", lambda e, i=i, k=k, pm=pm: e.matmul(
                    pm[:, 0:n], w_in[:, 4096 * k + 128 * i:4096 * k + 128 * (i + 1)],
                    ht[:, CH * k:CH * k + n], start=(k == 0), stop=(k == 7)),
                    inc=(tt if k == 7 else None))
            P.wait("vector", tt)
            if i < 16:
                off = CE * i + (0 if halo else NH)
                P.do("vector", lambda e, i=i, off=off, pm=pm: e.tensor_copy(
                    out=xs_ext[:, off:off + n], in_=pm[:, 0:n]), inc=vv)
            else:
                g = i - 16
                rb = rbuf[:, CH * (i % 2):CH * (i % 2) + n]
                P.wait("vector", st)
                P.do("vector", lambda e, rb=rb, pm=pm: e.tensor_copy(
                    out=rb, in_=pm[:, 0:n]), inc=vv)
                P.wait("sync", vv)
                P.dma("sync", res_o[g, :, CH * c:CH * c + n], rb, st)
            P.wait("tensor", vv)

    def conv_u_chunk(n, c):
        for g in range(16):
            base = CE * g
            P.do("vector", lambda e, g=g, base=base: e.tensor_scalar(
                out=convt[:, 0:n], in0=xs_ext[:, base:base + n],
                scalar1=cw[:, 4 * g:4 * g + 1], scalar2=None, op0=ALU.mult), inc=vv)
            for m in (1, 2, 3):
                P.wait("vector", vv)
                P.do("vector", lambda e, g=g, base=base, m=m: e.scalar_tensor_tensor(
                    out=convt[:, 0:n], in0=xs_ext[:, base + m:base + m + n],
                    scalar=cw[:, 4 * g + m:4 * g + m + 1], in1=convt[:, 0:n],
                    op0=ALU.mult, op1=ALU.add), inc=vv)
            P.wait("scalar", vv)
            P.do("scalar", lambda e, g=g: e.activation(
                out=u_full[:, CH * g:CH * g + n], in_=convt[:, 0:n],
                func=AF.Silu, bias=cb[:, g:g + 1]), inc=aa)
            P.wait("sync", aa)
            P.dma("sync", u_o[g, :, CH * c:CH * c + n], u_full[:, CH * g:CH * g + n], st)
            P.wait("vector", aa)
            P.do("vector", lambda e, g=g, base=base: e.tensor_copy(
                out=xs_ext[:, base:base + NH],
                in_=xs_ext[:, base + CH:base + CH + NH]), inc=vv)

    def xdbl_chunk(n, c, sub='all'):
        P.wait("tensor", aa, vv)
        for g in range(16):
            P.do("tensor", lambda e, g=g: e.matmul(
                ps_x[:, 0:n], w_xp[:, 96 * g:96 * (g + 1)],
                u_full[:, CH * g:CH * g + n], start=(g == 0), stop=(g == 15)),
                inc=(tt if g == 15 else None))
        P.wait("vector", tt)
        P.do("vector", lambda e: e.tensor_copy(out=bc_sb[:, 0:n], in_=ps_x[64:96, 0:n]), inc=vv)
        P.do("vector", lambda e: e.tensor_copy(out=din_bf[:, 0:n], in_=ps_x[0:64, 0:n]), inc=vv)
        P.wait("sync", vv)
        P.dma("sync", bc_o[:, CH * c:CH * c + n], bc_sb[:, 0:n], st)
        if sub == 'xd':
            return
        P.wait("tensor", vv)
        for i in range(16):
            pm = (ps_a, ps_b)[i % 2]
            P.do("tensor", lambda e, i=i, pm=pm: e.matmul(
                pm[:, 0:n], w_dt[:, 128 * i:128 * (i + 1)], din_bf[:, 0:n],
                start=True, stop=True), inc=tt)
            P.wait("scalar", tt)
            P.wait("vector", tt)
            if sub == 'dtmm':
                P.wait("tensor", tt)
                continue
            P.do("scalar", lambda e, i=i, pm=pm: e.activation(
                out=sp1[:, 0:n], in_=pm[:, 0:n], func=AF.Abs,
                bias=dtb[:, i:i + 1]), inc=aa)
            if sub == 'spa':
                P.wait("tensor", aa)
                continue
            P.wait("scalar", aa)
            P.do("scalar", lambda e: e.activation(
                out=sp1[:, 0:n], in_=sp1[:, 0:n], func=AF.Exp, scale=cst[:, 1:2]), inc=aa)
            P.wait("scalar", aa)
            P.do("scalar", lambda e: e.activation(
                out=sp1[:, 0:n], in_=sp1[:, 0:n], func=AF.Ln, bias=cst[:, 0:1]), inc=aa)
            if sub == 'spc':
                P.wait("tensor", aa)
                continue
            P.wait("vector", aa)
            P.do("vector", lambda e, i=i, pm=pm: e.tensor_scalar(
                out=sp2[:, 0:n], in0=pm[:, 0:n], scalar1=dtb[:, i:i + 1],
                scalar2=cst[:, 5:6], op0=ALU.add, op1=ALU.max), inc=vv)
            if sub == 'spv':
                P.wait("tensor", vv, aa)
                continue
            P.wait("vector", aa, st, vv)
            db = dbuf[:, CH * (i % 2):CH * (i % 2) + n]
            P.do("vector", lambda e, db=db: e.tensor_add(
                out=db, in0=sp1[:, 0:n], in1=sp2[:, 0:n]), inc=vv)
            P.wait("sync", vv)
            P.dma("sync", dl_o[i, :, CH * c:CH * c + n], db, st)
            P.wait("tensor", vv, aa)

    order = ["w", "ln0", "ip0", "ln", "ip", "conv", "xdbl"]
    lim = order.index(upto) if upto in order else len(order)
    if lim >= 1:
        ln_rms_chunk(NH, 0, halo=True)
    if lim >= 2:
        in_proj_chunk(NH, -1, halo=True)
    for c in range(NCH if nch is None else nch):
        if lim >= 3:
            ln_rms_chunk(CH, NH + CH * c, halo=False)
        if lim >= 4:
            in_proj_chunk(CH, c, halo=False)
        if lim >= 5:
            conv_u_chunk(CH, c)
        if lim >= 6:
            xdbl_chunk(CH, c, sub=(upto_sub if lim >= 6 else 'all'))
    P.wait("sync", st)
    return P.finish()


def _pad3(xdt_slice, prev):
    xe = np.concatenate([prev, xdt_slice], axis=1)
    return np.ascontiguousarray(xe.reshape(8, 128, 3 + TOKC))


def host_front_inputs(L0_attn_x, L0_attn_norm_w, L0_attn_in_proj_w, L0_attn_conv_w,
                      L0_attn_conv_b, L0_attn_x_proj_w, L0_attn_dt_proj_w,
                      L0_attn_dt_proj_b):
    x_dt = np.ascontiguousarray(L0_attn_x.reshape(NTOK, D).T)
    inpw_t = np.ascontiguousarray(L0_attn_in_proj_w.T.reshape(8, 128, 2 * DI))
    norm_w = np.ascontiguousarray(L0_attn_norm_w.reshape(8, 128, 1))
    conv_wa = np.ascontiguousarray(
        L0_attn_conv_w.reshape(DI, 4).reshape(16, 128, 4).transpose(1, 0, 2).reshape(128, 64))
    conv_ba = np.ascontiguousarray(L0_attn_conv_b.reshape(16, 128).T)
    xpw_t = np.ascontiguousarray(L0_attn_x_proj_w.T.reshape(16, 128, 96))
    dtpw_t = np.ascontiguousarray(L0_attn_dt_proj_w.T)
    dtb_a = np.ascontiguousarray(L0_attn_dt_proj_b.reshape(16, 128).T)
    ones_c = np.ones((128, 1), np.float32)
    ones_r = np.ones((1, 128), np.float32)
    cst_arr = np.tile(np.array([1.0, -1.0, -0.5, LN_EPS, 1.0 / 32, 0, 0, 0],
                               np.float32), (128, 1))
    maps = []
    for c in range(NC):
        lo = c * TOKC
        if lo % T == 0:
            prev = np.zeros((D, 3), np.float32)
        else:
            prev = x_dt[:, lo - 3:lo]
        maps.append({
            "x_ext": _pad3(x_dt[:, lo:lo + TOKC], prev),
            "inpw_t": inpw_t, "norm_w": norm_w, "conv_wa": conv_wa,
            "conv_ba": conv_ba, "xpw_t": xpw_t, "dtpw_t": dtpw_t, "dtb_a": dtb_a,
            "ones_c": ones_c, "ones_r": ones_r,
            "cst_i": cst_arr,
        })
    return maps


def run_front(inputs):
    nc = build_front()
    maps = host_front_inputs(
        inputs["L0_attn_x"], inputs["L0_attn_norm_w"], inputs["L0_attn_in_proj_w"],
        inputs["L0_attn_conv_w"], inputs["L0_attn_conv_b"], inputs["L0_attn_x_proj_w"],
        inputs["L0_attn_dt_proj_w"], inputs["L0_attn_dt_proj_b"])
    res = run_bass_kernel_spmd(nc, maps, core_ids=list(range(NC)))
    out = {}
    for key, nt, dt in [("xn_o", 8, np.float32), ("res_o", 16, BF), ("u_o", 16, BF),
                        ("dl_o", 16, BF), ("bc_o", None, np.float32)]:
        if nt is None:
            out[key] = np.concatenate([res.results[c][key] for c in range(NC)], axis=1)
        else:
            out[key] = np.concatenate(
                [res.results[c][key].reshape(nt * 128, TOKC) for c in range(NC)], axis=1)
    return out


def kernel(**inputs):
    raise NotImplementedError("in progress")


def build_scan():
    nc = _nc()
    P = Prog(nc)
    TC = 512
    NTC = T // TC

    dl_i = nc.dram_tensor("dl_i", [B, 2, 128, T], BF16, kind="ExternalInput").ap()
    u_i = nc.dram_tensor("u_i", [B, 2, 128, T], BF16, kind="ExternalInput").ap()
    bc_i = nc.dram_tensor("bc_i", [B, 32, T], F32, kind="ExternalInput").ap()
    alog_a = nc.dram_tensor("alog_a", [128, 32], F32, kind="ExternalInput").ap()
    d_i = nc.dram_tensor("d_i", [128, 2], F32, kind="ExternalInput").ap()
    sel_i = nc.dram_tensor("sel_i", [16, 128, 128], F32, kind="ExternalInput").ap()
    nsel_i = nc.dram_tensor("nsel_i", [16, 128, 128], F32, kind="ExternalInput").ap()
    sbsel_i = nc.dram_tensor("sbsel_i", [16, 128], F32, kind="ExternalInput").ap()
    y_o = nc.dram_tensor("y_o", [B, 2, 128, T], BF16, kind="ExternalOutput").ap()

    dl_sb = P.sb("dl_sb", (128, 2 * T), BF16)
    u_sb = P.sb("u_sb", (128, 2 * T), BF16)
    du_sb = P.sb("du_sb", (128, 2 * T), BF16)
    brep = P.sb("brep", (128, T), BF16)
    crep = P.sb("crep", (128, T), BF16)
    brow = P.sb("brow", (16, T), BF16)
    crow = P.sb("crow", (16, T), BF16)
    sel = P.sb("sel", (128, 16 * 128), BF16)
    nsel = P.sb("nsel", (128, 16 * 128), BF16)
    sbsel = P.sb("sbsel", (16, 128), BF16)
    seltmp = P.sb("seltmp", (128, 16 * 128), F32)
    acol = P.sb("acol", (128, 32), F32)
    dcol = P.sb("dcol", (128, 2), F32)
    state = P.sb("state", (128, 32), F32)
    da_sb = P.sb("da_sb", (128, TC), F32)
    dbu_sb = P.sb("dbu_sb", (128, TC), F32)
    h_sb = P.sb("h_sb", (128, TC), F32)
    hc_sb = P.sb("hc_sb", (128, TC), BF16)
    ybuf = P.sb("ybuf", (128, 2 * TC), BF16)

    ps_dr = P.ps("ps_dr", (128, TC))
    ps_ur = P.ps("ps_ur", (128, TC))
    ps_y = P.ps("ps_y", (128, TC))
    ps_br = P.ps("ps_br", (128, TC))

    ld = P.sem("ld")
    gd = P.sem("gd")
    vv = P.sem("vv")
    aa = P.sem("aa")
    tt = P.sem("tt")
    st = P.sem("st")

    for g in range(16):
        P.dma("sync", seltmp[:, 128 * g:128 * (g + 1)], sel_i[g, :, :], ld, fence=False)
    P.fence("sync", ld)
    P.wait("vector", ld)
    P.do("vector", lambda e: e.tensor_copy(out=sel[:, :], in_=seltmp[:, :]), inc=vv)
    P.wait("sync", vv)
    for g in range(16):
        P.dma("sync", seltmp[:, 128 * g:128 * (g + 1)], nsel_i[g, :, :], ld, fence=False)
    P.fence("sync", ld)
    P.wait("vector", ld)
    P.do("vector", lambda e: e.tensor_copy(out=nsel[:, :], in_=seltmp[:, :]), inc=vv)
    P.wait("sync", vv)
    P.dma("sync", seltmp[0:16, 0:128], sbsel_i, ld)
    P.wait("vector", ld)
    P.do("vector", lambda e: e.tensor_copy(out=sbsel[:, :], in_=seltmp[0:16, 0:128]), inc=vv)
    P.dma("sync", acol[:, :], alog_a, ld)
    P.dma("sync", dcol[:, :], d_i, ld)
    P.wait("scalar", ld)
    P.do("scalar", lambda e: e.activation(out=acol[:, :], in_=acol[:, :], func=AF.Exp), inc=aa)
    P.wait("vector", aa)
    P.do("vector", lambda e: e.tensor_scalar(
        out=acol[:, :], in0=acol[:, :], scalar1=-1.0, scalar2=None, op0=ALU.mult), inc=vv)

    for b in range(B):
        P.wait("sync", vv, aa, tt, st)
        for r in range(2):
            P.dma("sync", dl_sb[:, T * r:T * (r + 1)], dl_i[b, r, :, :], ld, fence=False)
            P.dma("sync", u_sb[:, T * r:T * (r + 1)], u_i[b, r, :, :], ld, fence=False)
        P.fence("sync", ld)
        P.wait("gpsimd", tt, vv)
        P.dma("gpsimd", brow[:, :], bc_i[b, 0:16, :], gd, fence=False)
        P.dma("gpsimd", crow[:, :], bc_i[b, 16:32, :], gd)
        P.wait("vector", ld, gd)
        for r in range(2):
            P.do("vector", lambda e, r=r: e.tensor_mul(
                out=du_sb[:, T * r:T * (r + 1)], in0=dl_sb[:, T * r:T * (r + 1)],
                in1=u_sb[:, T * r:T * (r + 1)]), inc=vv)
        P.wait("tensor", vv, ld, gd)
        for cc0 in range(NTC):
            lo = TC * cc0
            P.do("tensor", lambda e, lo=lo: e.matmul(
                ps_br[:, :], sbsel[:, :], brow[0:16, lo:lo + TC],
                start=True, stop=True), inc=tt)
            P.wait("vector", tt)
            P.do("vector", lambda e, lo=lo: e.tensor_copy(
                out=brep[:, lo:lo + TC], in_=ps_br[:, :]), inc=vv)
            P.wait("tensor", vv)
            P.do("tensor", lambda e, lo=lo: e.matmul(
                ps_br[:, :], sbsel[:, :], crow[0:16, lo:lo + TC],
                start=True, stop=True), inc=tt)
            P.wait("vector", tt)
            P.do("vector", lambda e, lo=lo: e.tensor_copy(
                out=crep[:, lo:lo + TC], in_=ps_br[:, :]), inc=vv)
            P.wait("tensor", vv)
        P.do("vector", lambda e: e.memset(state[:, :], 0.0), inc=vv)
        for cc0 in range(NTC):
            lo = TC * cc0
            for r in range(2):
                for gg in range(16):
                    g = 16 * r + gg
                    P.wait("tensor", aa)
                    P.do("tensor", lambda e, r=r, gg=gg, lo=lo: e.matmul(
                        ps_dr[:, :], sel[:, 128 * gg:128 * (gg + 1)],
                        dl_sb[:, T * r + lo:T * r + lo + TC],
                        start=True, stop=True), inc=tt)
                    P.wait("tensor", vv)
                    P.do("tensor", lambda e, r=r, gg=gg, lo=lo: e.matmul(
                        ps_ur[:, :], sel[:, 128 * gg:128 * (gg + 1)],
                        du_sb[:, T * r + lo:T * r + lo + TC],
                        start=True, stop=True), inc=tt)
                    P.wait("scalar", tt, vv)
                    P.do("scalar", lambda e, g=g: e.activation(
                        out=da_sb[:, :], in_=ps_dr[:, :], func=AF.Exp,
                        scale=acol[:, g:g + 1]), inc=aa)
                    P.wait("vector", tt)
                    P.do("vector", lambda e, lo=lo: e.tensor_mul(
                        out=dbu_sb[:, :], in0=ps_ur[:, :], in1=brep[:, lo:lo + TC]), inc=vv)
                    P.wait("vector", aa, vv)
                    P.do("vector", lambda e, g=g: e.tensor_tensor_scan(
                        out=h_sb[:, :], data0=da_sb[:, :], data1=dbu_sb[:, :],
                        initial=state[:, g:g + 1], op0=ALU.mult, op1=ALU.add), inc=vv)
                    P.wait("vector", vv)
                    P.do("vector", lambda e, g=g: e.tensor_copy(
                        out=state[:, g:g + 1], in_=h_sb[:, TC - 1:TC]), inc=vv)
                    P.do("vector", lambda e, lo=lo: e.tensor_mul(
                        out=hc_sb[:, :], in0=h_sb[:, :], in1=crep[:, lo:lo + TC]), inc=vv)
                    P.wait("tensor", vv)
                    P.do("tensor", lambda e, gg=gg: e.matmul(
                        ps_y[:, :], nsel[:, 128 * gg:128 * (gg + 1)], hc_sb[:, :],
                        start=(gg == 0), stop=(gg == 15)),
                        inc=(tt if gg == 15 else None))
                P.wait("vector", tt, st)
                yb = ybuf[:, TC * r:TC * r + TC]
                P.do("vector", lambda e, r=r, lo=lo, yb=yb: e.scalar_tensor_tensor(
                    out=yb, in0=u_sb[:, T * r + lo:T * r + lo + TC],
                    scalar=dcol[:, r:r + 1], in1=ps_y[:, :],
                    op0=ALU.mult, op1=ALU.add), inc=vv)
                P.wait("sync", vv)
                P.dma("sync", y_o[b, r, :, lo:lo + TC], yb, st)
                P.wait("tensor", vv)
    P.wait("sync", st)
    return P.finish()


def host_scan_inputs(dl_full, u_full, bc_full, A_log, Dv):
    sel = np.zeros((16, 128, 128), np.float32)
    for p in range(16):
        for dh in range(8):
            for n in range(16):
                sel[p, 8 * p + dh, 16 * dh + n] = 1.0
    nselm = np.ascontiguousarray(np.transpose(sel, (0, 2, 1)))
    sbsel = np.zeros((16, 128), np.float32)
    for n in range(16):
        for dh in range(8):
            sbsel[n, 16 * dh + n] = 1.0
    maps = []
    for j in range(NC):
        ch0 = 256 * j
        dl_c = np.moveaxis(dl_full[ch0:ch0 + 256].reshape(2, 128, B, T), 2, 0)
        u_c = np.moveaxis(u_full[ch0:ch0 + 256].reshape(2, 128, B, T), 2, 0)
        bc_c = np.moveaxis(bc_full.reshape(32, B, T), 1, 0)
        alog_a = np.zeros((128, 32), np.float32)
        for g in range(32):
            for dh in range(8):
                for n in range(16):
                    alog_a[16 * dh + n, g] = A_log[ch0 + 8 * g + dh, n]
        d_a = Dv[ch0:ch0 + 256].reshape(2, 128).T
        maps.append({
            "dl_i": np.ascontiguousarray(dl_c), "u_i": np.ascontiguousarray(u_c),
            "bc_i": np.ascontiguousarray(bc_c), "alog_a": alog_a,
            "d_i": np.ascontiguousarray(d_a), "sel_i": sel, "nsel_i": nselm,
            "sbsel_i": sbsel,
        })
    return maps


def run_scan(front_out, inputs):
    nc = build_scan()
    maps = host_scan_inputs(front_out["dl_o"], front_out["u_o"], front_out["bc_o"],
                            inputs["L0_attn_A_log"], inputs["L0_attn_D"])
    res = run_bass_kernel_spmd(nc, maps, core_ids=list(range(NC)))
    ys = []
    for j in range(NC):
        y = res.results[j]["y_o"]
        ys.append(np.moveaxis(y, 0, 2).reshape(256, NTOK))
    return np.concatenate(ys, axis=0)


def build_mid():
    nc = _nc()
    P = Prog(nc)
    CHM = 256
    NCH = TOKC // CHM
    E, HH = 8, 256
    NT4 = CHM // 128

    x_i = nc.dram_tensor("x_i", [8, 128, TOKC], F32, kind="ExternalInput").ap()
    xn_i = nc.dram_tensor("xn_i", [8, 128, TOKC], F32, kind="ExternalInput").ap()
    res_i = nc.dram_tensor("res_i", [16, 128, TOKC], BF16, kind="ExternalInput").ap()
    y_i = nc.dram_tensor("y_i", [16, 128, TOKC], BF16, kind="ExternalInput").ap()
    opw_t = nc.dram_tensor("opw_t", [16, 128, D], F32, kind="ExternalInput").ap()
    wg_t = nc.dram_tensor("wg_t", [8, 128, E], F32, kind="ExternalInput").ap()
    lnsc_i = nc.dram_tensor("lnsc_i", [8, 128, E], F32, kind="ExternalInput").ap()
    lnbi_i = nc.dram_tensor("lnbi_i", [8, 128, E], F32, kind="ExternalInput").ap()
    w1_i = nc.dram_tensor("w1_i", [E, 8, 128, HH], F32, kind="ExternalInput").ap()
    b1_i = nc.dram_tensor("b1_i", [128, 16], F32, kind="ExternalInput").ap()
    w2_i = nc.dram_tensor("w2_i", [E, 2, 128, D], F32, kind="ExternalInput").ap()
    b2_i = nc.dram_tensor("b2_i", [1, E * D], F32, kind="ExternalInput").ap()
    gw_t = nc.dram_tensor("gw_t", [8, 128, E], F32, kind="ExternalInput").ap()
    ident_i = nc.dram_tensor("ident_i", [128, 128], F32, kind="ExternalInput").ap()
    ones_c = nc.dram_tensor("ones_c", [128, 1], F32, kind="ExternalInput").ap()
    ones_r = nc.dram_tensor("ones_r", [1, 128], F32, kind="ExternalInput").ap()
    cst_i = nc.dram_tensor("cst_i", [128, 8], F32, kind="ExternalInput").ap()

    x1_o = nc.dram_tensor("x1_o", [8, 128, TOKC], F32, kind="ExternalOutput").ap()
    xn1_o = nc.dram_tensor("xn1_o", [8, 128, TOKC], BF16, kind="ExternalOutput").ap()
    aff_o = nc.dram_tensor("aff_o", [16, 128, E], F32, kind="ExternalOutput").ap()

    w_op = P.sb("w_op", (128, 16 * D), BF16)
    w1 = P.sb("w1", (128, E * 8 * HH), BF16)
    w2 = P.sb("w2", (128, E * 2 * D), BF16)
    b2r = P.sb("b2r", (1, E * D), BF16)
    wgf = P.sb("wgf", (128, 8 * E), F32)
    gwf = P.sb("gwf", (128, 8 * E), F32)
    lnsc = P.sb("lnsc", (128, 8 * E), F32)
    lnbi = P.sb("lnbi", (128, 8 * E), F32)
    b1c = P.sb("b1c", (128, 16), F32)
    ident = P.sb("ident", (128, 128), BF16)
    wtmp = P.sb("wtmp", (128, D), F32)
    ones_bf = P.sb("ones_bf", (128, 1), BF16)
    oner_f = P.sb("oner_f", (1, 128), F32)
    oner_b = P.sb("oner_b", (1, 128), BF16)
    cst = P.sb("cst", (128, 8), F32)

    xc = P.sb("xc", (128, 8 * CHM), F32)
    xnc = P.sb("xnc", (128, 8 * CHM), F32)
    resc = P.sb("resc", (128, 16 * CHM), BF16)
    yc = P.sb("yc", (128, 16 * CHM), BF16)
    xnb = P.sb("xnb", (128, 8 * CHM), BF16)
    sq = P.sb("sq", (128, 8 * CHM), BF16)
    tmpf = P.sb("tmpf", (128, 8 * CHM), F32)
    mrow = P.sb("mrow", (1, E * CHM), BF16)
    mask_sb = P.sb("mask_sb", (128, NT4 * E), BF16)
    nmx = P.sb("nmx", (128, NT4), F32)
    rsum = P.sb("rsum", (128, NT4), F32)
    affc = P.sb("affc", (128, NT4 * E), F32)
    srow = P.sb("srow", (1, 6 * CHM), F32)
    xn1b = P.sb("xn1b", (128, 8 * CHM), BF16)

    ps_a = P.ps("ps_a", (128, CHM))
    ps_b = P.ps("ps_b", (128, CHM))
    ps_r1 = P.ps("ps_r1", (128, CHM))
    ps_r2 = P.ps("ps_r2", (128, CHM))
    ps_l = P.ps("ps_l", (128, E))
    ps_m = P.ps("ps_m", (1, 1024), BF16)
    ps_mr = P.ps("ps_mr", (128, CHM))

    ld = P.sem("ld")
    gd = P.sem("gd")
    vv = P.sem("vv")
    aa = P.sem("aa")
    tt = P.sem("tt")
    st = P.sem("st")

    MU, EX2, VAR, RSTD = range(4)

    def row(i):
        return srow[0:1, CHM * i:CHM * i + CHM]

    for k in range(16):
        P.dma("sync", wtmp[:, 0:D], opw_t[k, :, :], ld)
        P.wait("vector", ld)
        P.do("vector", lambda e, k=k: e.tensor_copy(
            out=w_op[:, D * k:D * (k + 1)], in_=wtmp[:, 0:D]), inc=vv)
        P.wait("sync", vv)
    for e_ in range(E):
        for k in range(8):
            P.dma("sync", wtmp[:, 0:HH], w1_i[e_, k, :, :], ld)
            P.wait("vector", ld)
            P.do("vector", lambda e, e_=e_, k=k: e.tensor_copy(
                out=w1[:, (8 * e_ + k) * HH:(8 * e_ + k + 1) * HH], in_=wtmp[:, 0:HH]), inc=vv)
            P.wait("sync", vv)
        for k in range(2):
            P.dma("sync", wtmp[:, 0:D], w2_i[e_, k, :, :], ld)
            P.wait("vector", ld)
            P.do("vector", lambda e, e_=e_, k=k: e.tensor_copy(
                out=w2[:, (2 * e_ + k) * D:(2 * e_ + k + 1) * D], in_=wtmp[:, 0:D]), inc=vv)
            P.wait("sync", vv)
    P.dma("gpsimd", b2r[:, :], b2_i, gd, fence=False)
    P.dma("gpsimd", ident[:, :], ident_i, gd, fence=False)
    P.dma("gpsimd", ones_bf[:, :], ones_c, gd, fence=False)
    P.dma("gpsimd", oner_b[:, :], ones_r, gd)
    for k in range(8):
        P.dma("sync", wgf[:, E * k:E * (k + 1)], wg_t[k, :, :], ld, fence=False)
        P.dma("sync", gwf[:, E * k:E * (k + 1)], gw_t[k, :, :], ld, fence=False)
        P.dma("sync", lnsc[:, E * k:E * (k + 1)], lnsc_i[k, :, :], ld, fence=False)
        P.dma("sync", lnbi[:, E * k:E * (k + 1)], lnbi_i[k, :, :], ld, fence=False)
    P.dma("sync", b1c[:, :], b1_i, ld, fence=False)
    P.dma("sync", oner_f[:, :], ones_r, ld, fence=False)
    P.dma("sync", cst[:, :], cst_i, ld)

    def stats_ln(src_f32, n, out_bf):
        P.wait("scalar", vv, tt)
        for k in range(8):
            P.do("scalar", lambda e, k=k: e.activation(
                out=sq[:, CHM * k:CHM * k + n], in_=src_f32[:, CHM * k:CHM * k + n],
                func=AF.Square), inc=aa)
        P.wait("vector", vv, tt)
        for k in range(8):
            P.do("vector", lambda e, k=k: e.tensor_copy(
                out=xnb[:, CHM * k:CHM * k + n],
                in_=src_f32[:, CHM * k:CHM * k + n]), inc=vv)
        P.wait("tensor", vv, aa, gd)
        for k in range(8):
            P.do("tensor", lambda e, k=k: e.matmul(
                ps_a[0:1, 0:n], ones_bf[:, :], xnb[:, CHM * k:CHM * k + n],
                start=(k == 0), stop=(k == 7)), inc=(tt if k == 7 else None))
        for k in range(8):
            P.do("tensor", lambda e, k=k: e.matmul(
                ps_b[0:1, 0:n], ones_bf[:, :], sq[:, CHM * k:CHM * k + n],
                start=(k == 0), stop=(k == 7)), inc=(tt if k == 7 else None))
        P.wait("vector", tt)
        P.do("vector", lambda e: e.tensor_scalar(
            out=row(MU)[:, 0:n], in0=ps_a[0:1, 0:n], scalar1=1.0 / D,
            scalar2=None, op0=ALU.mult), inc=vv)
        P.do("vector", lambda e: e.tensor_scalar(
            out=row(EX2)[:, 0:n], in0=ps_b[0:1, 0:n], scalar1=1.0 / D,
            scalar2=None, op0=ALU.mult), inc=vv)
        P.wait("vector", vv)
        P.do("vector", lambda e: e.tensor_mul(
            out=row(VAR)[:, 0:n], in0=row(MU)[:, 0:n], in1=row(MU)[:, 0:n]), inc=vv)
        P.wait("vector", vv)
        P.do("vector", lambda e: e.tensor_sub(
            out=row(VAR)[:, 0:n], in0=row(EX2)[:, 0:n], in1=row(VAR)[:, 0:n]), inc=vv)
        P.wait("scalar", vv)
        P.do("scalar", lambda e: e.activation(
            out=row(RSTD)[:, 0:n], in_=row(VAR)[:, 0:n], func=AF.Ln,
            bias=cst[0:1, 3:4]), inc=aa)
        P.wait("scalar", aa)
        P.do("scalar", lambda e: e.activation(
            out=row(RSTD)[:, 0:n], in_=row(RSTD)[:, 0:n], func=AF.Exp,
            scale=cst[0:1, 2:3]), inc=aa)
        P.wait("tensor", aa, vv)
        P.do("tensor", lambda e: e.matmul(
            ps_r1[:, 0:n], oner_f[:, :], row(MU)[:, 0:n], start=True, stop=True), inc=tt)
        P.do("tensor", lambda e: e.matmul(
            ps_r2[:, 0:n], oner_f[:, :], row(RSTD)[:, 0:n], start=True, stop=True), inc=tt)
        P.wait("vector", tt)
        for k in range(8):
            P.do("vector", lambda e, k=k: e.tensor_sub(
                out=tmpf[:, CHM * k:CHM * k + n],
                in0=src_f32[:, CHM * k:CHM * k + n], in1=ps_r1[:, 0:n]), inc=vv)
        P.wait("vector", vv)
        for k in range(8):
            P.do("vector", lambda e, k=k: e.tensor_mul(
                out=tmpf[:, CHM * k:CHM * k + n], in0=tmpf[:, CHM * k:CHM * k + n],
                in1=ps_r2[:, 0:n]), inc=vv)
        P.wait("vector", vv)
        for k in range(8):
            P.do("vector", lambda e, k=k: e.tensor_copy(
                out=out_bf[:, CHM * k:CHM * k + n], in_=tmpf[:, CHM * k:CHM * k + n]), inc=vv)

    xhat = resc
    h1m = yc
    inpe = xnb
    xn1f = xnc

    for c in range(NCH):
        co = CHM * c
        P.wait("sync", vv, aa, tt, st)
        for k in range(8):
            P.dma("sync", xc[:, CHM * k:CHM * (k + 1)], x_i[k, :, co:co + CHM], ld, fence=False)
            P.dma("sync", xnc[:, CHM * k:CHM * (k + 1)], xn_i[k, :, co:co + CHM], ld, fence=False)
        for g in range(16):
            P.dma("sync", resc[:, CHM * g:CHM * (g + 1)], res_i[g, :, co:co + CHM], ld, fence=False)
            P.dma("sync", yc[:, CHM * g:CHM * (g + 1)], y_i[g, :, co:co + CHM], ld, fence=False)
        P.fence("sync", ld)
        for t4 in range(NT4):
            P.wait("tensor", vv, aa, ld)
            for k in range(8):
                P.do("tensor", lambda e, t4=t4, k=k: e.matmul(
                    ps_l[:, :], xnc[:, CHM * k + 128 * t4:CHM * k + 128 * (t4 + 1)],
                    wgf[:, E * k:E * (k + 1)], start=(k == 0), stop=(k == 7)),
                    inc=(tt if k == 7 else None))
            P.wait("vector", tt)
            P.do("vector", lambda e, t4=t4: e.reduce_max(
                out=nmx[:, t4:t4 + 1], in_=ps_l[:, :], axis=AX.X), inc=vv)
            P.wait("vector", vv)
            P.do("vector", lambda e, t4=t4: e.tensor_scalar(
                out=mask_sb[:, E * t4:E * (t4 + 1)], in0=ps_l[:, :],
                scalar1=nmx[:, t4:t4 + 1], scalar2=None, op0=ALU.is_ge), inc=vv)
            P.wait("tensor", vv, gd)
            for e_ in range(E):
                P.do("tensor", lambda e, t4=t4, e_=e_: e.transpose(
                    ps_m[0:1, 128 * e_:128 * (e_ + 1)],
                    mask_sb[:, E * t4 + e_:E * t4 + e_ + 1], ident[:, :]),
                    inc=(tt if e_ == E - 1 else None))
            P.wait("vector", tt)
            for e_ in range(E):
                P.do("vector", lambda e, t4=t4, e_=e_: e.tensor_copy(
                    out=mrow[0:1, CHM * e_ + 128 * t4:CHM * e_ + 128 * (t4 + 1)],
                    in_=ps_m[0:1, 128 * e_:128 * (e_ + 1)]), inc=vv)
            P.wait("tensor", vv)
        P.wait("vector", ld)
        for k in range(8):
            P.do("vector", lambda e, k=k: e.tensor_add(
                out=xc[:, CHM * k:CHM * (k + 1)], in0=xc[:, CHM * k:CHM * (k + 1)],
                in1=xnc[:, CHM * k:CHM * (k + 1)]), inc=vv)
        stats_ln(xnc, CHM, xn1b)
        P.wait("scalar", ld, vv)
        for g in range(16):
            P.do("scalar", lambda e, g=g: e.activation(
                out=resc[:, CHM * g:CHM * (g + 1)], in_=resc[:, CHM * g:CHM * (g + 1)],
                func=AF.Silu), inc=aa)
        P.wait("vector", aa)
        for g in range(16):
            P.do("vector", lambda e, g=g: e.tensor_mul(
                out=resc[:, CHM * g:CHM * (g + 1)], in0=resc[:, CHM * g:CHM * (g + 1)],
                in1=yc[:, CHM * g:CHM * (g + 1)]), inc=vv)
        P.wait("tensor", vv)
        for i in range(8):
            pm = (ps_a, ps_b)[i % 2]
            for k in range(16):
                P.do("tensor", lambda e, i=i, k=k, pm=pm: e.matmul(
                    pm[:, :], w_op[:, D * k + 128 * i:D * k + 128 * (i + 1)],
                    resc[:, CHM * k:CHM * (k + 1)], start=(k == 0), stop=(k == 15)),
                    inc=(tt if k == 15 else None))
            P.wait("vector", tt)
            P.do("vector", lambda e, i=i, pm=pm: e.tensor_add(
                out=xc[:, CHM * i:CHM * (i + 1)], in0=xc[:, CHM * i:CHM * (i + 1)],
                in1=pm[:, :]), inc=vv)
            P.wait("tensor", vv)
        P.wait("vector", vv)
        for k in range(8):
            P.do("vector", lambda e, k=k: e.tensor_copy(
                out=xhat[:, CHM * k:CHM * (k + 1)], in_=xn1b[:, CHM * k:CHM * (k + 1)]), inc=vv)
        for e_ in range(E):
            P.wait("vector", vv, tt)
            for k in range(8):
                P.do("vector", lambda e, e_=e_, k=k: e.tensor_scalar(
                    out=inpe[:, CHM * k:CHM * (k + 1)], in0=xhat[:, CHM * k:CHM * (k + 1)],
                    scalar1=lnsc[:, E * k + e_:E * k + e_ + 1],
                    scalar2=lnbi[:, E * k + e_:E * k + e_ + 1],
                    op0=ALU.mult, op1=ALU.add), inc=vv)
            P.wait("tensor", vv, aa)
            P.do("tensor", lambda e, e_=e_: e.matmul(
                ps_mr[:, :], oner_b[:, :], mrow[0:1, CHM * e_:CHM * (e_ + 1)],
                start=True, stop=True), inc=tt)
            for hi in range(2):
                pm = (ps_a, ps_b)[hi]
                for k in range(8):
                    P.do("tensor", lambda e, e_=e_, hi=hi, k=k, pm=pm: e.matmul(
                        pm[:, :], w1[:, (8 * e_ + k) * HH + 128 * hi:(8 * e_ + k) * HH + 128 * (hi + 1)],
                        inpe[:, CHM * k:CHM * (k + 1)], start=(k == 0), stop=(k == 7)),
                        inc=(tt if k == 7 else None))
                P.wait("scalar", tt)
                P.do("scalar", lambda e, e_=e_, hi=hi, pm=pm: e.activation(
                    out=h1m[:, CHM * (2 * e_ + hi):CHM * (2 * e_ + hi + 1)], in_=pm[:, :],
                    func=AF.Gelu, bias=b1c[:, 2 * e_ + hi:2 * e_ + hi + 1]), inc=aa)
            P.wait("vector", tt, aa)
            for hi in range(2):
                P.do("vector", lambda e, e_=e_, hi=hi: e.tensor_mul(
                    out=h1m[:, CHM * (2 * e_ + hi):CHM * (2 * e_ + hi + 1)],
                    in0=h1m[:, CHM * (2 * e_ + hi):CHM * (2 * e_ + hi + 1)],
                    in1=ps_mr[:, :]), inc=vv)
            P.wait("tensor", vv)
        for i in range(8):
            pm = (ps_a, ps_b)[i % 2]
            P.wait("tensor", vv, aa)
            nmm = 3 * E
            cnt = 0
            for e_ in range(E):
                for hi in range(2):
                    cnt += 1
                    s0, p0 = (cnt == 1), (cnt == nmm)
                    P.do("tensor", lambda e, i=i, e_=e_, hi=hi, pm=pm, s=s0, p=p0: e.matmul(
                        pm[:, :], w2[:, (2 * e_ + hi) * D + 128 * i:(2 * e_ + hi) * D + 128 * (i + 1)],
                        h1m[:, CHM * (2 * e_ + hi):CHM * (2 * e_ + hi + 1)],
                        start=s, stop=p), inc=(tt if p0 else None))
                cnt += 1
                p0 = (cnt == nmm)
                P.do("tensor", lambda e, i=i, e_=e_, pm=pm, p=p0: e.matmul(
                    pm[:, :], b2r[0:1, D * e_ + 128 * i:D * e_ + 128 * (i + 1)],
                    mrow[0:1, CHM * e_:CHM * (e_ + 1)], start=False, stop=p),
                    inc=(tt if p0 else None))
            P.wait("vector", tt)
            P.do("vector", lambda e, i=i, pm=pm: e.tensor_add(
                out=xc[:, CHM * i:CHM * (i + 1)], in0=xc[:, CHM * i:CHM * (i + 1)],
                in1=pm[:, :]), inc=vv)
            P.wait("tensor", vv)
        P.wait("sync", vv)
        for k in range(8):
            P.dma("sync", x1_o[k, :, co:co + CHM], xc[:, CHM * k:CHM * (k + 1)], st, fence=False)
        P.fence("sync", st)
        stats_ln(xc, CHM, xn1b)
        P.wait("vector", vv)
        for k in range(8):
            P.do("vector", lambda e, k=k: e.tensor_copy(
                out=xn1f[:, CHM * k:CHM * (k + 1)], in_=tmpf[:, CHM * k:CHM * (k + 1)]), inc=vv)
        P.wait("sync", vv)
        for k in range(8):
            P.dma("sync", xn1_o[k, :, co:co + CHM], xn1b[:, CHM * k:CHM * (k + 1)], st, fence=False)
        P.fence("sync", st)
        for t4 in range(NT4):
            P.wait("tensor", vv, aa)
            for k in range(8):
                P.do("tensor", lambda e, t4=t4, k=k: e.matmul(
                    ps_l[:, :], xn1f[:, CHM * k + 128 * t4:CHM * k + 128 * (t4 + 1)],
                    gwf[:, E * k:E * (k + 1)], start=(k == 0), stop=(k == 7)),
                    inc=(tt if k == 7 else None))
            P.wait("vector", tt)
            P.do("vector", lambda e, t4=t4: e.reduce_max(
                out=nmx[:, t4:t4 + 1], in_=ps_l[:, :], axis=AX.X, negate=True), inc=vv)
            P.wait("scalar", vv)
            P.do("scalar", lambda e, t4=t4: e.activation(
                out=affc[:, E * t4:E * (t4 + 1)], in_=ps_l[:, :], func=AF.Exp,
                bias=nmx[:, t4:t4 + 1], accum_out=rsum[:, t4:t4 + 1]), inc=aa)
            P.wait("vector", aa)
            P.do("vector", lambda e, t4=t4: e.reciprocal(
                out=rsum[:, t4:t4 + 1], in_=rsum[:, t4:t4 + 1]), inc=vv)
            P.wait("vector", vv)
            P.do("vector", lambda e, t4=t4: e.tensor_scalar(
                out=affc[:, E * t4:E * (t4 + 1)], in0=affc[:, E * t4:E * (t4 + 1)],
                scalar1=rsum[:, t4:t4 + 1], scalar2=None, op0=ALU.mult), inc=vv)
            P.wait("sync", vv)
            P.dma("sync", aff_o[NT4 * c + t4, :, :], affc[:, E * t4:E * (t4 + 1)], st)
            P.wait("tensor", aa, vv)
    P.wait("sync", st)
    return P.finish()


def host_mid_inputs(inputs, xn_full, res_full, y_full):
    x_dt = np.ascontiguousarray(inputs["L0_attn_x"].reshape(NTOK, D).T)
    opw_t = np.ascontiguousarray(inputs["L0_attn_out_proj_w"].T.reshape(16, 128, D))
    wg_t = np.ascontiguousarray(inputs["L0_ffn_Wg"].T.reshape(8, 128, 8))
    lnsc_i = np.ascontiguousarray(inputs["L0_ffn_ln_scale"].T.reshape(8, 128, 8))
    lnbi_i = np.ascontiguousarray(inputs["L0_ffn_ln_bias"].T.reshape(8, 128, 8))
    w1_i = np.ascontiguousarray(inputs["L0_ffn_W1"].reshape(8, 8, 128, 256))
    b1_i = np.ascontiguousarray(
        inputs["L0_ffn_b1"].reshape(8, 2, 128).transpose(2, 0, 1).reshape(128, 16))
    w2_i = np.ascontiguousarray(inputs["L0_ffn_W2"].reshape(8, 2, 128, D))
    b2_i = np.ascontiguousarray(inputs["L0_ffn_b2"].reshape(1, 8 * D))
    gw_t = np.ascontiguousarray(inputs["L1_ffn_gate_w"].T.reshape(8, 128, 8))
    ident = np.eye(128, dtype=np.float32)
    ones_c = np.ones((128, 1), np.float32)
    ones_r = np.ones((1, 128), np.float32)
    cst_arr = np.tile(np.array([1.0, -1.0, -0.5, LN_EPS, 1.0 / 32, 0, 0, 0],
                               np.float32), (128, 1))
    maps = []
    for c in range(NC):
        lo = c * TOKC
        maps.append({
            "x_i": np.ascontiguousarray(x_dt[:, lo:lo + TOKC].reshape(8, 128, TOKC)),
            "xn_i": np.ascontiguousarray(xn_full[:, lo:lo + TOKC].reshape(8, 128, TOKC)),
            "res_i": np.ascontiguousarray(res_full[:, lo:lo + TOKC].reshape(16, 128, TOKC)),
            "y_i": np.ascontiguousarray(y_full[:, lo:lo + TOKC].reshape(16, 128, TOKC)),
            "opw_t": opw_t, "wg_t": wg_t, "lnsc_i": lnsc_i, "lnbi_i": lnbi_i,
            "w1_i": w1_i, "b1_i": b1_i, "w2_i": w2_i, "b2_i": b2_i, "gw_t": gw_t,
            "ident_i": ident, "ones_c": ones_c, "ones_r": ones_r, "cst_i": cst_arr,
        })
    return maps


def run_mid(inputs, xn_full, res_full, y_full):
    nc = build_mid()
    maps = host_mid_inputs(inputs, xn_full, res_full, y_full)
    res = run_bass_kernel_spmd(nc, maps, core_ids=list(range(NC)))
    out = {}
    out["x1"] = np.concatenate(
        [res.results[c]["x1_o"].reshape(D, TOKC) for c in range(NC)], axis=1)
    out["xn1"] = np.concatenate(
        [res.results[c]["xn1_o"].reshape(D, TOKC) for c in range(NC)], axis=1)
    out["aff"] = np.concatenate(
        [res.results[c]["aff_o"].reshape(TOKC, 8) for c in range(NC)], axis=0)
    return out


def build_attn():
    nc = _nc()
    P = Prog(nc)
    NS = 16
    KC = 512
    NKC = T // KC

    qx_i = nc.dram_tensor("qx_i", [8, 128, NS * 128], BF16, kind="ExternalInput").ap()
    keys_i = nc.dram_tensor("keys_i", [8, 128, T], BF16, kind="ExternalInput").ap()
    qk_i = nc.dram_tensor("qk_i", [8, 128, D], F32, kind="ExternalInput").ap()
    ov_i = nc.dram_tensor("ov_i", [8, 128, D], F32, kind="ExternalInput").ap()
    qidx_i = nc.dram_tensor("qidx_i", [128, NS], F32, kind="ExternalInput").ap()
    kio_i = nc.dram_tensor("kio_i", [128, KC], F32, kind="ExternalInput").ap()
    ident_i = nc.dram_tensor("ident_i", [128, 128], F32, kind="ExternalInput").ap()
    cst_i = nc.dram_tensor("cst_i", [128, 8], F32, kind="ExternalInput").ap()
    attn_o = nc.dram_tensor("attn_o", [8, 128, NS * 128], BF16, kind="ExternalOutput").ap()

    q_dt = P.sb("q_dt", (128, 8 * NS * 128), BF16)
    acc = P.sb("acc", (128, NS * 1024), F32)
    wqk = P.sb("wqk", (128, 8 * D), BF16)
    wov = P.sb("wov", (128, 8 * D), BF16)
    kb = P.sb("kb", (128, 8 * KC), BF16)
    vb = P.sb("vb", (128, 4 * 1024), BF16)
    qxb = P.sb("qxb", (128, 8 * KC), BF16)
    wtmp = P.sb("wtmp", (128, D), F32)
    p_sb = P.sb("p_sb", (128, KC), BF16)
    pt_sb = P.sb("pt_sb", (128, KC), BF16)
    pen = P.sb("pen", (128, KC), F32)
    m_sb = P.sb("m_sb", (128, NS), F32)
    l_sb = P.sb("l_sb", (128, NS), F32)
    mx = P.sb("mx", (128, 1), F32)
    dif = P.sb("dif", (128, 1), F32)
    alp = P.sb("alp", (128, 1), F32)
    bcol = P.sb("bcol", (128, 1), F32)
    thr = P.sb("thr", (128, 1), F32)
    ls = P.sb("ls", (128, 1), F32)
    rcol = P.sb("rcol", (128, 1), F32)
    qidx = P.sb("qidx", (128, NS), F32)
    kio = P.sb("kio", (128, KC), F32)
    ident = P.sb("ident", (128, 128), BF16)
    cst = P.sb("cst", (128, 8), F32)
    ax = P.sb("ax", (128, 1024), BF16)
    axt = P.sb("axt", (128, 8 * 128), BF16)

    ps_s = P.ps("ps_s", (128, KC))
    ps_t = P.ps("ps_t", (128, KC), BF16)
    ps_av = P.ps("ps_av", (128, 1024))
    ps_q = P.ps("ps_q", (128, KC))

    ld = P.sem("ld")
    gd = P.sem("gd")
    vv = P.sem("vv")
    aa = P.sem("aa")
    tt = P.sem("tt")
    st = P.sem("st")

    for k in range(8):
        P.dma("sync", wtmp[:, :], qk_i[k, :, :], ld)
        P.wait("vector", ld)
        P.do("vector", lambda e, k=k: e.tensor_copy(
            out=wqk[:, D * k:D * (k + 1)], in_=wtmp[:, :]), inc=vv)
        P.wait("sync", vv)
        P.dma("sync", wtmp[:, :], ov_i[k, :, :], ld)
        P.wait("vector", ld)
        P.do("vector", lambda e, k=k: e.tensor_copy(
            out=wov[:, D * k:D * (k + 1)], in_=wtmp[:, :]), inc=vv)
        P.wait("sync", vv)
    P.dma("sync", qidx[:, :], qidx_i, ld, fence=False)
    P.dma("sync", kio[:, :], kio_i, ld, fence=False)
    P.dma("sync", cst[:, :], cst_i, ld, fence=False)
    P.fence("sync", ld)
    P.dma("gpsimd", ident[:, :], ident_i, gd)
    P.wait("vector", ld)
    P.do("vector", lambda e: e.memset(m_sb[:, :], -1e30), inc=vv)
    P.do("vector", lambda e: e.memset(l_sb[:, :], 0.0), inc=vv)
    for s in range(NS):
        P.do("vector", lambda e, s=s: e.memset(acc[:, 1024 * s:1024 * (s + 1)], 0.0), inc=vv)
    for qc in range(4):
        P.wait("sync", vv, tt)
        for k in range(8):
            P.dma("sync", qxb[:, KC * k:KC * (k + 1)], qx_i[k, :, KC * qc:KC * (qc + 1)],
                  ld, fence=False)
        P.fence("sync", ld)
        P.wait("tensor", ld, vv)
        for i in range(8):
            for k in range(8):
                P.do("tensor", lambda e, i=i, k=k: e.matmul(
                    ps_q[:, :], wqk[:, D * k + 128 * i:D * k + 128 * (i + 1)],
                    qxb[:, KC * k:KC * (k + 1)], start=(k == 0), stop=(k == 7)),
                    inc=(tt if k == 7 else None))
            P.wait("vector", tt)
            P.do("vector", lambda e, i=i, qc=qc: e.tensor_copy(
                out=q_dt[:, NS * 128 * i + KC * qc:NS * 128 * i + KC * (qc + 1)],
                in_=ps_q[:, :]), inc=vv)
            P.wait("tensor", vv)
    for j in range(NKC):
        P.wait("sync", vv, aa, tt, st)
        for k in range(8):
            P.dma("sync", kb[:, KC * k:KC * (k + 1)], keys_i[k, :, KC * j:KC * (j + 1)],
                  ld, fence=False)
        P.fence("sync", ld)
        for m4 in range(4):
            for kd in range(8):
                P.do("sync", lambda e, m4=m4, kd=kd, j=j: e.dma_start_transpose(
                    out=vb[:, 1024 * m4 + 128 * kd:1024 * m4 + 128 * (kd + 1)],
                    in_=keys_i[kd, :, KC * j + 128 * m4:KC * j + 128 * (m4 + 1)]),
                    inc=ld, k=16)
        P.fence("sync", ld)
        for s in range(j, NS):
            P.wait("tensor", ld, aa, vv)
            for k in range(8):
                P.do("tensor", lambda e, s=s, k=k: e.matmul(
                    ps_s[:, :], q_dt[:, NS * 128 * k + 128 * s:NS * 128 * k + 128 * (s + 1)],
                    kb[:, KC * k:KC * (k + 1)], start=(k == 0), stop=(k == 7)),
                    inc=(tt if k == 7 else None))
            P.wait("vector", tt)
            if s == j:
                P.do("vector", lambda e, s=s, j=j: e.tensor_scalar(
                    out=thr[:, :], in0=qidx[:, s:s + 1], scalar1=float(-KC * j),
                    scalar2=None, op0=ALU.add), inc=vv)
                P.wait("vector", vv)
                P.do("vector", lambda e: e.tensor_scalar(
                    out=pen[:, :], in0=kio[:, :], scalar1=thr[:, 0:1],
                    scalar2=cst[:, 6:7], op0=ALU.is_gt, op1=ALU.mult), inc=vv)
                P.wait("vector", vv)
                P.do("vector", lambda e: e.tensor_add(
                    out=ps_s[:, :], in0=ps_s[:, :], in1=pen[:, :]), inc=vv)
                P.wait("vector", vv)
            P.do("vector", lambda e: e.reduce_max(out=mx[:, :], in_=ps_s[:, :], axis=AX.X), inc=vv)
            P.wait("vector", vv)
            P.do("vector", lambda e, s=s: e.tensor_sub(
                out=dif[:, :], in0=m_sb[:, s:s + 1], in1=mx[:, :]), inc=vv)
            P.wait("vector", vv)
            P.do("vector", lambda e: e.tensor_scalar(
                out=dif[:, :], in0=dif[:, :], scalar1=0.0, scalar2=None,
                op0=ALU.min), inc=vv)
            P.do("vector", lambda e, s=s: e.tensor_max(
                out=m_sb[:, s:s + 1], in0=m_sb[:, s:s + 1], in1=mx[:, :]), inc=vv)
            P.wait("scalar", vv)
            P.do("scalar", lambda e: e.activation(
                out=alp[:, :], in_=dif[:, :], func=AF.Exp, scale=cst[:, 4:5]), inc=aa)
            P.wait("vector", vv)
            P.do("vector", lambda e, s=s: e.tensor_scalar(
                out=bcol[:, :], in0=m_sb[:, s:s + 1], scalar1=-1.0 / 32, scalar2=None,
                op0=ALU.mult), inc=vv)
            P.wait("scalar", vv, aa)
            P.do("scalar", lambda e: e.activation(
                out=p_sb[:, :], in_=ps_s[:, :], func=AF.Exp, scale=cst[:, 4:5],
                bias=bcol[:, 0:1], accum_out=ls[:, :]), inc=aa)
            P.wait("vector", aa)
            P.do("vector", lambda e, s=s: e.scalar_tensor_tensor(
                out=l_sb[:, s:s + 1], in0=l_sb[:, s:s + 1], scalar=alp[:, 0:1],
                in1=ls[:, :], op0=ALU.mult, op1=ALU.add), inc=vv)
            P.wait("tensor", aa, vv, gd)
            for m4 in range(4):
                P.do("tensor", lambda e, m4=m4: e.transpose(
                    ps_t[:, 128 * m4:128 * (m4 + 1)], p_sb[:, 128 * m4:128 * (m4 + 1)],
                    ident[:, :]), inc=(tt if m4 == 3 else None))
            P.wait("vector", tt)
            P.do("vector", lambda e: e.tensor_copy(out=pt_sb[:, :], in_=ps_t[:, :]), inc=vv)
            P.wait("tensor", vv)
            for m4 in range(4):
                for dh in range(2):
                    p0 = (m4 == 3)
                    P.do("tensor", lambda e, m4=m4, dh=dh: e.matmul(
                        ps_av[:, 512 * dh:512 * (dh + 1)],
                        pt_sb[:, 128 * m4:128 * (m4 + 1)],
                        vb[:, 1024 * m4 + 512 * dh:1024 * m4 + 512 * (dh + 1)],
                        start=(m4 == 0), stop=(m4 == 3)),
                        inc=(tt if (p0 and dh == 1) else None))
            P.wait("vector", tt)
            for dh in range(2):
                sl = acc[:, 1024 * s + 512 * dh:1024 * s + 512 * (dh + 1)]
                P.do("vector", lambda e, sl=sl: e.tensor_scalar(
                    out=sl, in0=sl, scalar1=alp[:, 0:1], scalar2=None, op0=ALU.mult), inc=vv)
            P.wait("vector", vv)
            for dh in range(2):
                sl = acc[:, 1024 * s + 512 * dh:1024 * s + 512 * (dh + 1)]
                P.do("vector", lambda e, sl=sl, dh=dh: e.tensor_add(
                    out=sl, in0=sl, in1=ps_av[:, 512 * dh:512 * (dh + 1)]), inc=vv)
            P.wait("tensor", vv)
    for s in range(NS):
        P.wait("vector", tt, aa, st)
        P.do("vector", lambda e, s=s: e.reciprocal(out=rcol[:, :], in_=l_sb[:, s:s + 1]), inc=vv)
        P.wait("vector", vv)
        P.do("vector", lambda e, s=s: e.tensor_scalar(
            out=ax[:, :], in0=acc[:, 1024 * s:1024 * (s + 1)], scalar1=rcol[:, 0:1],
            scalar2=None, op0=ALU.mult), inc=vv)
        P.wait("tensor", vv)
        for i8 in range(8):
            P.do("tensor", lambda e, i8=i8: e.transpose(
                ps_t[:, 128 * (i8 % 4):128 * (i8 % 4 + 1)], ax[:, 128 * i8:128 * (i8 + 1)],
                ident[:, :]), inc=tt)
            P.wait("vector", tt)
            P.do("vector", lambda e, i8=i8: e.tensor_copy(
                out=axt[:, 128 * i8:128 * (i8 + 1)],
                in_=ps_t[:, 128 * (i8 % 4):128 * (i8 % 4 + 1)]), inc=vv)
            P.wait("tensor", vv)
        for i in range(8):
            for kd in range(8):
                P.do("tensor", lambda e, i=i, kd=kd: e.matmul(
                    ps_q[:, 0:128], wov[:, D * kd + 128 * i:D * kd + 128 * (i + 1)],
                    axt[:, 128 * kd:128 * (kd + 1)], start=(kd == 0), stop=(kd == 7)),
                    inc=(tt if kd == 7 else None))
            P.wait("vector", tt)
            P.do("vector", lambda e, i=i: e.tensor_copy(
                out=ax[:, 128 * i:128 * (i + 1)], in_=ps_q[:, 0:128]), inc=vv)
            P.wait("tensor", vv)
        P.wait("sync", vv)
        for i in range(8):
            P.dma("sync", attn_o[i, :, 128 * s:128 * (s + 1)], ax[:, 128 * i:128 * (i + 1)],
                  st, fence=False)
        P.fence("sync", st)
    P.wait("sync", st)
    return P.finish()


def host_attn_inputs(inputs, xn1_full_bf):
    qk_i = np.ascontiguousarray(inputs["L1_attn_qk"].reshape(8, 128, D))
    ov_i = np.ascontiguousarray(inputs["L1_attn_ov"].reshape(8, 128, D))
    kio = np.tile(np.arange(512, dtype=np.float32), (128, 1))
    ident = np.eye(128, dtype=np.float32)
    cst_arr = np.tile(np.array([1.0, -1.0, -0.5, LN_EPS, 1.0 / 32, 0, -1e9, 0],
                               np.float32), (128, 1))
    maps = []
    for c in range(NC):
        beta = c // 4
        xb = xn1_full_bf[:, beta * T:(beta + 1) * T]
        slots = [4 * s + (c % 4) for s in range(16)]
        qcols = np.concatenate([np.arange(128 * i, 128 * (i + 1)) for i in slots])
        qx = np.ascontiguousarray(xb[:, qcols].reshape(8, 128, 16 * 128))
        keys = np.ascontiguousarray(xb.reshape(8, 128, T))
        qidx = np.zeros((128, 16), np.float32)
        for s, i in enumerate(slots):
            qidx[:, s] = 128 * i + np.arange(128)
        maps.append({
            "qx_i": qx, "keys_i": keys, "qk_i": qk_i, "ov_i": ov_i,
            "qidx_i": qidx, "kio_i": kio, "ident_i": ident, "cst_i": cst_arr,
        })
    return maps


def run_attn(inputs, xn1_full_bf):
    nc = build_attn()
    maps = host_attn_inputs(inputs, xn1_full_bf)
    res = run_bass_kernel_spmd(nc, maps, core_ids=list(range(NC)))
    attn_full = np.zeros((D, NTOK), dtype=BF)
    for c in range(NC):
        beta = c // 4
        a = res.results[c]["attn_o"].reshape(D, 16 * 128)
        for s in range(16):
            i = 4 * s + (c % 4)
            attn_full[:, beta * T + 128 * i: beta * T + 128 * (i + 1)] = a[:, 128 * s:128 * (s + 1)]
    return attn_full


def build_moe(nch5=None):
    nc = _nc()
    P = Prog(nc)
    I2 = 2048
    CH5 = 512
    NCH5 = (T // CH5) if nch5 is None else nch5
    KTOP = 2048

    xn1_i = nc.dram_tensor("xn1_i", [8, 128, NTOK], BF16, kind="ExternalInput").ap()
    affe_i = nc.dram_tensor("affe_i", [B, 128, 64], F32, kind="ExternalInput").ap()
    affr_i = nc.dram_tensor("affr_i", [B, 1, T], F32, kind="ExternalInput").ap()
    wg_i = nc.dram_tensor("wg_i", [2, 8, 128, I2], F32, kind="ExternalInput").ap()
    wu_i = nc.dram_tensor("wu_i", [2, 8, 128, I2], F32, kind="ExternalInput").ap()
    wd_i = nc.dram_tensor("wd_i", [2, 16, 128, D], F32, kind="ExternalInput").ap()
    ones_c = nc.dram_tensor("ones_c", [128, 1], F32, kind="ExternalInput").ap()
    ones_r = nc.dram_tensor("ones_r", [1, 128], F32, kind="ExternalInput").ap()
    dense_o = nc.dram_tensor("dense_o", [8, 128, NTOK], BF16, kind="ExternalOutput").ap()

    wg = P.sb("wg", (128, 8 * I2), BF16)
    wu = P.sb("wu", (128, 8 * I2), BF16)
    wd = P.sb("wd", (128, 16 * D), BF16)
    xc = P.sb("xc", (128, 8 * CH5), BF16)
    hh = P.sb("hh", (128, 16 * CH5), BF16)
    gsb = P.sb("gsb", (128, CH5), BF16)
    mgrep = P.sb("mgrep", (128, CH5), BF16)
    ob = P.sb("ob", (128, 8 * CH5), BF16)
    aff2 = P.sb("aff2", (128, B * 64), F32)
    affr_sh = P.sb("affr_sh", (1, T), F32)
    mg0 = P.sb("mg0", (1, T), BF16)
    mg1 = P.sb("mg1", (1, T), BF16)
    bis = P.sb("bis", (1, 8), I32)
    cnt1 = P.sb("cnt1", (128, 1), F32)
    cmp64 = P.sb("cmp64", (128, 64), F32)
    cntb = P.sb("cntb", (128, 1), BF16)
    ones_bf = P.sb("ones_bf", (128, 1), BF16)
    oner_f = P.sb("oner_f", (1, 128), F32)
    oner_b = P.sb("oner_b", (1, 128), BF16)
    tau = P.sb("tau", (1, 2), I32)

    ps_a = P.ps("ps_a", (128, CH5))
    ps_b = P.ps("ps_b", (128, CH5))
    ps_c = P.ps("ps_c", (128, CH5))
    ps_mr = P.ps("ps_mr", (128, CH5))
    ps_r = P.ps("ps_r", (128, 8))

    ld = P.sem("ld")
    gd = P.sem("gd")
    vv = P.sem("vv")
    aa = P.sem("aa")
    tt = P.sem("tt")
    st = P.sem("st")

    P.dma("sync", aff2[:, 0:64], affe_i[0, :, :], ld, fence=False)
    P.dma("sync", aff2[:, 64:128], affe_i[1, :, :], ld, fence=False)
    P.dma("sync", oner_f[:, :], ones_r, ld)
    P.dma("gpsimd", ones_bf[:, :], ones_c, gd, fence=False)
    P.dma("gpsimd", oner_b[:, :], ones_r, gd)

    ONE_F_BITS = 0x3F800000
    for b in range(B):
        P.wait("sync", vv)
        P.dma("sync", affr_sh[:, :], affr_i[b, :, :], ld)
        P.wait("vector", ld, gd)
        P.do("vector", lambda e: e.memset(bis[0:1, 0:1], 0), inc=vv)
        P.do("vector", lambda e: e.memset(bis[0:1, 1:2], ONE_F_BITS), inc=vv)
        arow = affr_sh
        a2 = aff2[:, 64 * b:64 * (b + 1)]
        for it in range(31):
            P.wait("vector", vv, tt)
            P.do("vector", lambda e: e.tensor_tensor(
                out=bis[0:1, 2:3], in0=bis[0:1, 0:1], in1=bis[0:1, 1:2], op=ALU.add), inc=vv)
            P.wait("vector", vv)
            P.do("vector", lambda e: e.tensor_scalar(
                out=bis[0:1, 2:3], in0=bis[0:1, 2:3], scalar1=1, scalar2=None,
                op0=ALU.arith_shift_right), inc=vv)
            P.wait("tensor", vv, ld)
            P.do("tensor", lambda e: e.matmul(
                ps_r[:, 0:1], oner_f[:, :], bis[0:1, 2:3].bitcast(F32),
                start=True, stop=True), inc=tt)
            P.wait("vector", tt)
            P.do("vector", lambda e, a2=a2: e.tensor_scalar(
                out=cmp64[:, :], in0=a2, scalar1=ps_r[:, 0:1], scalar2=None,
                op0=ALU.is_ge, op1=ALU.add, accum_out=cnt1[:, 0:1]), inc=vv)
            P.wait("vector", vv)
            P.do("vector", lambda e: e.tensor_copy(out=cntb[:, :], in_=cnt1[:, :]), inc=vv)
            P.wait("tensor", vv, gd)
            P.do("tensor", lambda e: e.matmul(
                ps_r[0:1, 1:2], ones_bf[:, :], cntb[:, :], start=True, stop=True), inc=tt)
            P.wait("vector", tt)
            P.do("vector", lambda e: e.tensor_scalar(
                out=bis[0:1, 3:4], in0=ps_r[0:1, 1:2], scalar1=float(KTOP), scalar2=None,
                op0=ALU.is_ge), inc=vv)
            P.wait("vector", vv)
            P.do("vector", lambda e: e.tensor_sub(
                out=bis[0:1, 4:5], in0=bis[0:1, 2:3], in1=bis[0:1, 0:1]), inc=vv)
            P.wait("vector", vv)
            P.do("vector", lambda e: e.tensor_mul(
                out=bis[0:1, 4:5], in0=bis[0:1, 4:5], in1=bis[0:1, 3:4]), inc=vv)
            P.wait("vector", vv)
            P.do("vector", lambda e: e.tensor_add(
                out=bis[0:1, 0:1], in0=bis[0:1, 0:1], in1=bis[0:1, 4:5]), inc=vv)
            P.do("vector", lambda e: e.tensor_sub(
                out=bis[0:1, 5:6], in0=bis[0:1, 1:2], in1=bis[0:1, 2:3]), inc=vv)
            P.do("vector", lambda e: e.tensor_scalar(
                out=bis[0:1, 6:7], in0=bis[0:1, 3:4], scalar1=-1, scalar2=None,
                op0=ALU.add), inc=vv)
            P.wait("vector", vv)
            P.do("vector", lambda e: e.tensor_mul(
                out=bis[0:1, 5:6], in0=bis[0:1, 5:6], in1=bis[0:1, 6:7]), inc=vv)
            P.wait("vector", vv)
            P.do("vector", lambda e: e.tensor_add(
                out=bis[0:1, 1:2], in0=bis[0:1, 1:2], in1=bis[0:1, 5:6]), inc=vv)
        P.wait("vector", vv)
        P.do("vector", lambda e, b=b: e.tensor_copy(out=tau[0:1, b:b + 1], in_=bis[0:1, 0:1]), inc=vv)
        P.wait("vector", vv)
        mgr = (mg0, mg1)[b]
        P.do("vector", lambda e, arow=arow, mgr=mgr, b=b: e.scalar_tensor_tensor(
            out=mgr[0:1, :], in0=arow[0:1, :], scalar=tau[0:1, b:b + 1].bitcast(F32),
            in1=arow[0:1, :], op0=ALU.is_ge, op1=ALU.mult), inc=vv)

    for ih in range(2):
        P.wait("gpsimd", tt, aa, vv)
        for k in range(8):
            P.dma("gpsimd", wg[:, I2 * k:I2 * (k + 1)], wg_i[ih, k, :, :], gd, fence=False)
            P.dma("gpsimd", wu[:, I2 * k:I2 * (k + 1)], wu_i[ih, k, :, :], gd, fence=False)
        for k in range(16):
            P.dma("gpsimd", wd[:, D * k:D * (k + 1)], wd_i[ih, k, :, :], gd, fence=False)
        P.fence("gpsimd", gd)
        for b in range(B):
            mgr = (mg0, mg1)[b]
            for c5 in range(NCH5):
                lo5 = CH5 * c5
                gco = b * T + lo5
                P.wait("sync", vv, aa, tt, st)
                for k in range(8):
                    P.dma("sync", xc[:, CH5 * k:CH5 * (k + 1)],
                          xn1_i[k, :, gco:gco + CH5], ld, fence=False)
                P.fence("sync", ld)
                P.wait("tensor", vv, ld, gd)
                P.do("tensor", lambda e, mgr=mgr, lo5=lo5: e.matmul(
                    ps_mr[:, :], oner_b[:, :], mgr[0:1, lo5:lo5 + CH5],
                    start=True, stop=True), inc=tt)
                P.wait("vector", tt)
                P.do("vector", lambda e: e.tensor_copy(out=mgrep[:, :], in_=ps_mr[:, :]), inc=vv)
                for ic in range(16):
                    P.wait("tensor", aa, vv)
                    for k in range(8):
                        P.do("tensor", lambda e, ic=ic, k=k: e.matmul(
                            ps_a[:, :], wg[:, I2 * k + 128 * ic:I2 * k + 128 * (ic + 1)],
                            xc[:, CH5 * k:CH5 * (k + 1)], start=(k == 0), stop=(k == 7)),
                            inc=(tt if k == 7 else None))
                    for k in range(8):
                        P.do("tensor", lambda e, ic=ic, k=k: e.matmul(
                            ps_b[:, :], wu[:, I2 * k + 128 * ic:I2 * k + 128 * (ic + 1)],
                            xc[:, CH5 * k:CH5 * (k + 1)], start=(k == 0), stop=(k == 7)),
                            inc=(tt if k == 7 else None))
                    P.wait("scalar", tt)
                    P.do("scalar", lambda e: e.activation(
                        out=gsb[:, :], in_=ps_a[:, :], func=AF.Silu), inc=aa)
                    P.wait("vector", aa, tt)
                    P.do("vector", lambda e, ic=ic: e.tensor_mul(
                        out=hh[:, CH5 * ic:CH5 * (ic + 1)], in0=ps_b[:, :],
                        in1=gsb[:, :]), inc=vv)
                for i in range(8):
                    P.wait("tensor", vv, aa)
                    for k in range(16):
                        P.do("tensor", lambda e, i=i, k=k: e.matmul(
                            ps_c[:, :], wd[:, D * k + 128 * i:D * k + 128 * (i + 1)],
                            hh[:, CH5 * k:CH5 * (k + 1)], start=(k == 0), stop=(k == 15)),
                            inc=(tt if k == 15 else None))
                    P.wait("vector", tt, st)
                    P.do("vector", lambda e, i=i: e.tensor_mul(
                        out=ob[:, CH5 * i:CH5 * (i + 1)], in0=ps_c[:, :],
                        in1=mgrep[:, :]), inc=vv)
                    P.wait("tensor", vv)
                P.wait("gpsimd", vv)
                for i in range(8):
                    if ih == 0:
                        P.dma("gpsimd", dense_o[i, :, gco:gco + CH5],
                              ob[:, CH5 * i:CH5 * (i + 1)], gd, fence=False)
                    else:
                        P.do("gpsimd", lambda e, i=i, gco=gco: e.dma_start(
                            out=dense_o[i, :, gco:gco + CH5],
                            in_=ob[:, CH5 * i:CH5 * (i + 1)],
                            accum_op=ALU.add), inc=gd, k=16)
                P.fence("gpsimd", gd)
    P.wait("sync", st)
    P.wait("gpsimd", gd)
    return P.finish()


def host_moe_inputs(inputs, xn1_full_bf, aff_full):
    ones_c = np.ones((128, 1), np.float32)
    ones_r = np.ones((1, 128), np.float32)
    xn1_a = np.ascontiguousarray(xn1_full_bf.reshape(8, 128, NTOK))
    maps = []
    for e_ in range(NC):
        affe = np.ascontiguousarray(
            aff_full[:, e_].reshape(B, 128, 64).astype(np.float32))
        affr = np.ascontiguousarray(aff_full[:, e_].reshape(B, 1, T).astype(np.float32))
        wg = np.ascontiguousarray(
            inputs["L1_ffn_w_gate"][e_].T.reshape(8, 128, 4096)
            .reshape(8, 128, 2, 2048).transpose(2, 0, 1, 3))
        wu = np.ascontiguousarray(
            inputs["L1_ffn_w_up"][e_].T.reshape(8, 128, 4096)
            .reshape(8, 128, 2, 2048).transpose(2, 0, 1, 3))
        wd = np.ascontiguousarray(inputs["L1_ffn_w_down"][e_].T.reshape(2, 16, 128, D))
        maps.append({
            "xn1_i": xn1_a, "affe_i": affe, "affr_i": affr,
            "wg_i": wg, "wu_i": wu, "wd_i": wd,
            "ones_c": ones_c, "ones_r": ones_r,
        })
    return maps


def run_moe(inputs, xn1_full_bf, aff_full):
    nc = build_moe()
    maps = host_moe_inputs(inputs, xn1_full_bf, aff_full)
    res = run_bass_kernel_spmd(nc, maps, core_ids=list(range(NC)))
    return [res.results[e_]["dense_o"].reshape(D, NTOK) for e_ in range(NC)]


def build_final():
    nc = _nc()
    P = Prog(nc)
    x1_i = nc.dram_tensor("x1_i", [8, 128, TOKC], F32, kind="ExternalInput").ap()
    attn_i = nc.dram_tensor("attn_i", [8, 128, TOKC], BF16, kind="ExternalInput").ap()
    dense_i = nc.dram_tensor("dense_i", [8, 8, 128, TOKC], BF16, kind="ExternalInput").ap()
    out_o = nc.dram_tensor("out_o", [8, 128, TOKC], F32, kind="ExternalOutput").ap()

    x1t = P.sb("x1t", (128, TOKC), F32)
    bt = P.sb("bt", (128, TOKC), BF16)
    btf = P.sb("btf", (128, TOKC), F32)
    ld = P.sem("ld")
    vv = P.sem("vv")
    st = P.sem("st")
    for k in range(8):
        P.wait("sync", vv, st)
        P.dma("sync", x1t[:, :], x1_i[k, :, :], ld)
        for si in range(9):
            src = attn_i[k, :, :] if si == 0 else dense_i[si - 1, k, :, :]
            P.wait("sync", vv)
            P.dma("sync", bt[:, :], src, ld)
            P.wait("vector", ld)
            P.do("vector", lambda e: e.tensor_copy(out=btf[:, :], in_=bt[:, :]), inc=vv)
            P.wait("vector", vv)
            P.do("vector", lambda e: e.tensor_add(
                out=x1t[:, :], in0=x1t[:, :], in1=btf[:, :]), inc=vv)
        P.wait("sync", vv)
        P.dma("sync", out_o[k, :, :], x1t[:, :], st)
    P.wait("sync", st)
    return P.finish()


def run_final(x1_full, attn_full_bf, dense_list):
    nc = build_final()
    maps = []
    for c in range(NC):
        lo = c * TOKC
        dense = np.stack([d[:, lo:lo + TOKC].reshape(8, 128, TOKC) for d in dense_list])
        maps.append({
            "x1_i": np.ascontiguousarray(x1_full[:, lo:lo + TOKC].reshape(8, 128, TOKC)),
            "attn_i": np.ascontiguousarray(attn_full_bf[:, lo:lo + TOKC].reshape(8, 128, TOKC)),
            "dense_i": np.ascontiguousarray(dense),
        })
    res = run_bass_kernel_spmd(nc, maps, core_ids=list(range(NC)))
    return np.concatenate(
        [res.results[c]["out_o"].reshape(D, TOKC) for c in range(NC)], axis=1)




import math


KTOP_G = 2048
SHARED_AG = True


def build_fused(upto=6, mid_skip=()):
    nc = _nc()
    P = Prog(nc)
    NH = 3
    CE = CH + NH
    E, HH = 8, 256
    CHM = 256
    I2 = 2048
    CH5 = 512
    KC = 512
    NS = 16
    TC = 512

    def EIN(name, shape, dt=F32):
        return nc.dram_tensor(name, list(shape), dt, kind="ExternalInput").ap()

    x_ext = EIN("x_ext", [8, 128, NH + TOKC])
    inpw_t = EIN("inpw_t", [8, 128, 2 * DI])
    norm_w = EIN("norm_w", [8, 128, 1])
    conv_wa = EIN("conv_wa", [128, 64])
    conv_ba = EIN("conv_ba", [128, 16])
    xpw_t = EIN("xpw_t", [16, 128, 96])
    dtpw_t = EIN("dtpw_t", [64, DI])
    dtb_a = EIN("dtb_a", [128, 16])
    ones_c = EIN("ones_c", [128, 1])
    ones_r = EIN("ones_r", [1, 128])
    cst_i = EIN("cst_i", [128, 8])
    alog_a = EIN("alog_a", [128, 32])
    d_i = EIN("d_i", [128, 2])
    sel_i = EIN("sel_i", [16, 128, 128])
    nsel_i = EIN("nsel_i", [16, 128, 128])
    sbsel_i = EIN("sbsel_i", [16, 128])
    sidx_i = EIN("sidx_i", [128, 32], I32)
    yidx_i = EIN("yidx_i", [128, 16], I32)
    opw_t = EIN("opw_t", [16, 128, D])
    wg_t = EIN("wg_t", [8, 128, E])
    lnsc_i = EIN("lnsc_i", [8, 128, E])
    lnbi_i = EIN("lnbi_i", [8, 128, E])
    w1_i = EIN("w1_i", [E, 8, 128, HH])
    b1_i = EIN("b1_i", [128, 16])
    w2_i = EIN("w2_i", [E, 2, 128, D])
    b2_i = EIN("b2_i", [1, E * D])
    gw_t = EIN("gw_t", [8, 128, E])
    ident_i = EIN("ident_i", [128, 128])
    qk_i = EIN("qk_i", [8, 128, D])
    ov_i = EIN("ov_i", [8, 128, D])
    qidx_i = EIN("qidx_i", [128, NS])
    kio_i = EIN("kio_i", [128, KC])
    kidx_i = EIN("kidx_i", [128, 16 * 8], I32)
    vidx_i = EIN("vidx_i", [128, 16 * 4], I32)
    aidx_i = EIN("aidx_i", [8, 1], I32)
    wg5_i = EIN("wg5_i", [2, 8, 128, I2])
    wu5_i = EIN("wu5_i", [2, 8, 128, I2])
    wd5_i = EIN("wd5_i", [2, 16, 128, D])

    def IDR(name, shape, dt, shared=False):
        return nc.dram_tensor(name, list(shape), dt, kind="Internal",
                              addr_space=("Shared" if (shared and SHARED_AG) else "Local")).ap()

    xn_scr = IDR("xn_scr", [8, 128, TOKC], F32)
    res_scr = IDR("res_scr", [16, 128, TOKC], BF16)
    x1_scr = IDR("x1_scr", [8, 128, TOKC], F32)
    xn1_scr = IDR("xn1_scr", [8, 128, TOKC], BF16)
    attn_scr = IDR("attn_scr", [8, 128, TOKC], BF16)
    ag1_in = IDR("ag1_in", [33 * 128, TOKC], BF16)
    ag1_out = IDR("ag1_out", [8 * 33 * 128, TOKC], BF16, shared=True)
    ag2_in = IDR("ag2_in", [2048, TOKC], BF16)
    ag2_out = IDR("ag2_out", [8 * 2048, TOKC], BF16, shared=True)
    ag3a_in = IDR("ag3a_in", [1024, TOKC], BF16)
    ag3a_out = IDR("ag3a_out", [8 * 1024, TOKC], BF16, shared=True)
    ag3b_in = IDR("ag3b_in", [TOKC, 1024], BF16)
    ag3b_out = IDR("ag3b_out", [NTOK, 1024], BF16, shared=True)
    ag3c_in = IDR("ag3c_in", [8, TOKC], F32)
    ag3c_out = IDR("ag3c_out", [64, TOKC], F32, shared=True)
    affg = IDR("affg", [8, TOKC], F32)
    rs_in = IDR("rs_in", [NTOK, D], BF16)
    rs_out = IDR("rs_out", [TOKC, D], BF16)
    posd = IDR("posd", [B, T], F32)
    xin_st = IDR("xin_st", [B * KTOP_G, D], BF16)
    out_st = IDR("out_st", [B * KTOP_G, D], BF16)
    out_o = nc.dram_tensor("out_o", [8, 128, TOKC], F32, kind="ExternalOutput").ap()

    ld = P.sem("ld")
    gd = P.sem("gd")
    vv = P.sem("vv")
    aa = P.sem("aa")
    tt = P.sem("tt")
    st = P.sem("st")
    cc1 = P.sem("cc1")
    cc2 = P.sem("cc2")
    cc3a = P.sem("cc3a")
    cc3b = P.sem("cc3b")
    cc3c = P.sem("cc3c")
    cc4 = P.sem("cc4")
    allsems = [ld, gd, vv, aa, tt, st]

    ag1v = ag1_in.rearrange("(a p) t -> a p t", p=128)
    ag1ov = ag1_out.rearrange("(q a p) t -> q a p t", q=8, p=128)
    ag2v = ag2_in.rearrange("(a p) t -> a p t", p=128)
    ag3av = ag3a_in.rearrange("(a p) t -> a p t", p=128)
    ag3aov = ag3a_out.rearrange("(a p) t -> a p t", p=128)


    def _phase_1():
        with P.scope():
            w_in = P.sb("w_in", (128, 8 * 2 * DI), BF16)
            w_xp = P.sb("w_xp", (128, 16 * 96), BF16)
            w_dt = P.sb("w_dt", (64, DI), BF16)
            wtmp = P.sb("wtmp", (128, 2 * DI), F32)
            cw = P.sb("cw", (128, 64), F32)
            cb = P.sb("cb", (128, 16), F32)
            dtb = P.sb("dtb", (128, 16), F32)
            nwc = P.sb("nwc", (128, 8), F32)
            ones_bf = P.sb("ones_bf", (128, 1), BF16)
            cst = P.sb("cst", (128, 8), F32)
            oner_f = P.sb("oner_f", (1, 128), F32)
            xc = P.sb("xc", (128, 8 * CH), F32)
            xbf = P.sb("xbf", (128, 8 * CH), BF16)
            sq = P.sb("sq", (128, 8 * CH), BF16)
            xnt = P.sb("xnt", (128, 8 * CH), F32)
            ht = P.sb("ht", (128, 8 * CH), BF16)
            srow = P.sb("srow", (1, 6 * CH), F32)
            xs_ext = P.sb("xs_ext", (128, 16 * CE), BF16)
            rbuf = P.sb("rbuf", (128, 2 * CH), BF16)
            convt = P.sb("convt", (128, CH), F32)
            u_full = P.sb("u_full", (128, 16 * CH), BF16)
            din_bf = P.sb("din_bf", (64, CH), BF16)
            bc_sb = P.sb("bc_sb", (32, CH), BF16)
            sp1 = P.sb("sp1", (128, CH), F32)
            sp2 = P.sb("sp2", (128, CH), F32)
            dbuf = P.sb("dbuf", (128, 2 * CH), BF16)
            ps_s1 = P.ps("ps_s1", (1, CH))
            ps_s2 = P.ps("ps_s2", (1, CH))
            ps_r1 = P.ps("ps_r1", (128, CH))
            ps_r2 = P.ps("ps_r2", (128, CH))
            ps_a = P.ps("ps_a", (128, CH))
            ps_b = P.ps("ps_b", (128, CH))
            ps_x = P.ps("ps_x", (96, CH))

            for k in range(8):
                P.dma("sync", wtmp[:, :], inpw_t[k, :, :], ld)
                P.dma("sync", nwc[:, k:k + 1], norm_w[k, :, :], ld)
                P.wait("vector", ld)
                P.do("vector", lambda e, k=k: e.tensor_scalar(
                    out=w_in[:, 4096 * k:4096 * (k + 1)], in0=wtmp[:, :],
                    scalar1=nwc[:, k:k + 1], scalar2=None, op0=ALU.mult), inc=vv)
                P.wait("sync", vv)
            for k in range(16):
                P.dma("sync", wtmp[:, 0:96], xpw_t[k, :, :], ld)
                P.wait("vector", ld)
                P.do("vector", lambda e, k=k: e.tensor_copy(
                    out=w_xp[:, 96 * k:96 * (k + 1)], in_=wtmp[:, 0:96]), inc=vv)
                P.wait("sync", vv)
            P.dma("sync", wtmp[0:64, 0:DI], dtpw_t, ld)
            P.wait("vector", ld)
            P.do("vector", lambda e: e.tensor_copy(out=w_dt[:, :], in_=wtmp[0:64, 0:DI]), inc=vv)
            P.dma("sync", cw[:, :], conv_wa, ld)
            P.dma("sync", cb[:, :], conv_ba, ld)
            P.dma("sync", dtb[:, :], dtb_a, ld)
            P.dma("gpsimd", ones_bf[:, :], ones_c, gd)
            P.dma("sync", cst[:, :], cst_i, ld)
            P.dma("sync", oner_f[:, :], ones_r, ld)

            MU, EX2, VAR, RSTD, MS, RSTD2 = range(6)

            def row(i):
                return srow[0:1, CH * i:CH * i + CH]

            def chunk_barrier():
                P.wait("sync", vv, aa, tt)
                P.wait("vector", tt, aa)
                P.wait("scalar", tt, vv)
                P.wait("tensor", vv, aa)
                P.wait("gpsimd", vv)

            def ln_rms_chunk(n, col0, halo):
                chunk_barrier()
                for k in range(8):
                    P.dma("sync", xc[:, CH * k:CH * k + n], x_ext[k, :, col0:col0 + n],
                          ld, fence=False)
                P.fence("sync", ld)
                P.wait("vector", ld)
                P.wait("scalar", ld)
                for k in range(8):
                    P.do("vector", lambda e, k=k: e.tensor_copy(
                        out=xbf[:, CH * k:CH * k + n], in_=xc[:, CH * k:CH * k + n]), inc=vv)
                    P.do("scalar", lambda e, k=k: e.activation(
                        out=sq[:, CH * k:CH * k + n], in_=xc[:, CH * k:CH * k + n],
                        func=AF.Square), inc=aa)
                P.wait("tensor", vv, aa, gd)
                for k in range(8):
                    P.do("tensor", lambda e, k=k: e.matmul(
                        ps_s1[0:1, 0:n], ones_bf[:, :], xbf[:, CH * k:CH * k + n],
                        start=(k == 0), stop=(k == 7)), inc=(tt if k == 7 else None))
                for k in range(8):
                    P.do("tensor", lambda e, k=k: e.matmul(
                        ps_s2[0:1, 0:n], ones_bf[:, :], sq[:, CH * k:CH * k + n],
                        start=(k == 0), stop=(k == 7)), inc=(tt if k == 7 else None))
                P.wait("vector", tt)
                P.do("vector", lambda e: e.tensor_scalar(
                    out=row(MU)[:, 0:n], in0=ps_s1[0:1, 0:n], scalar1=1.0 / D,
                    scalar2=None, op0=ALU.mult), inc=vv)
                P.do("vector", lambda e: e.tensor_scalar(
                    out=row(EX2)[:, 0:n], in0=ps_s2[0:1, 0:n], scalar1=1.0 / D,
                    scalar2=None, op0=ALU.mult), inc=vv)
                P.wait("vector", vv)
                P.do("vector", lambda e: e.tensor_mul(
                    out=row(VAR)[:, 0:n], in0=row(MU)[:, 0:n], in1=row(MU)[:, 0:n]), inc=vv)
                P.wait("vector", vv)
                P.do("vector", lambda e: e.tensor_sub(
                    out=row(VAR)[:, 0:n], in0=row(EX2)[:, 0:n], in1=row(VAR)[:, 0:n]), inc=vv)
                P.wait("scalar", vv)
                P.do("scalar", lambda e: e.activation(
                    out=row(RSTD)[:, 0:n], in_=row(VAR)[:, 0:n], func=AF.Ln,
                    bias=cst[0:1, 3:4]), inc=aa)
                P.wait("scalar", aa)
                P.do("scalar", lambda e: e.activation(
                    out=row(RSTD)[:, 0:n], in_=row(RSTD)[:, 0:n], func=AF.Exp,
                    scale=cst[0:1, 2:3]), inc=aa)
                P.wait("tensor", aa)
                P.do("tensor", lambda e: e.matmul(
                    ps_r1[:, 0:n], oner_f[:, :], row(MU)[:, 0:n], start=True, stop=True), inc=tt)
                P.do("tensor", lambda e: e.matmul(
                    ps_r2[:, 0:n], oner_f[:, :], row(RSTD)[:, 0:n], start=True, stop=True), inc=tt)
                P.wait("vector", tt)
                for k in range(8):
                    P.do("vector", lambda e, k=k: e.tensor_sub(
                        out=xnt[:, CH * k:CH * k + n], in0=xc[:, CH * k:CH * k + n],
                        in1=ps_r1[:, 0:n]), inc=vv)
                P.wait("vector", vv)
                for k in range(8):
                    P.do("vector", lambda e, k=k: e.tensor_mul(
                        out=xnt[:, CH * k:CH * k + n], in0=xnt[:, CH * k:CH * k + n],
                        in1=ps_r2[:, 0:n]), inc=vv)
                if not halo:
                    P.wait("sync", vv)
                    for k in range(8):
                        P.dma("sync", xn_scr[k, :, col0 - NH:col0 - NH + n],
                              xnt[:, CH * k:CH * k + n], st, fence=False)
                    P.fence("sync", st)
                P.wait("scalar", vv)
                for k in range(8):
                    P.do("scalar", lambda e, k=k: e.activation(
                        out=sq[:, CH * k:CH * k + n], in_=xnt[:, CH * k:CH * k + n],
                        func=AF.Square), inc=aa)
                P.wait("tensor", aa, vv)
                for k in range(8):
                    P.do("tensor", lambda e, k=k: e.matmul(
                        ps_s1[0:1, 0:n], ones_bf[:, :], sq[:, CH * k:CH * k + n],
                        start=(k == 0), stop=(k == 7)), inc=(tt if k == 7 else None))
                P.wait("vector", tt)
                P.do("vector", lambda e: e.tensor_scalar(
                    out=row(MS)[:, 0:n], in0=ps_s1[0:1, 0:n], scalar1=1.0 / D,
                    scalar2=None, op0=ALU.mult), inc=vv)
                P.wait("scalar", vv)
                P.do("scalar", lambda e: e.activation(
                    out=row(RSTD2)[:, 0:n], in_=row(MS)[:, 0:n], func=AF.Ln,
                    bias=cst[0:1, 3:4]), inc=aa)
                P.wait("scalar", aa)
                P.do("scalar", lambda e: e.activation(
                    out=row(RSTD2)[:, 0:n], in_=row(RSTD2)[:, 0:n], func=AF.Exp,
                    scale=cst[0:1, 2:3]), inc=aa)
                P.wait("tensor", aa)
                P.do("tensor", lambda e: e.matmul(
                    ps_r1[:, 0:n], oner_f[:, :], row(RSTD2)[:, 0:n], start=True, stop=True), inc=tt)
                P.wait("vector", tt)
                for k in range(8):
                    P.do("vector", lambda e, k=k: e.tensor_mul(
                        out=ht[:, CH * k:CH * k + n], in0=xnt[:, CH * k:CH * k + n],
                        in1=ps_r1[:, 0:n]), inc=vv)

            def in_proj_chunk(n, c, halo):
                irange = range(16) if halo else range(32)
                P.wait("tensor", vv)
                for i in irange:
                    pm = (ps_a, ps_b)[i % 2]
                    for k in range(8):
                        P.do("tensor", lambda e, i=i, k=k, pm=pm: e.matmul(
                            pm[:, 0:n], w_in[:, 4096 * k + 128 * i:4096 * k + 128 * (i + 1)],
                            ht[:, CH * k:CH * k + n], start=(k == 0), stop=(k == 7)),
                            inc=(tt if k == 7 else None))
                    P.wait("vector", tt)
                    if i < 16:
                        off = CE * i + (0 if halo else NH)
                        P.do("vector", lambda e, i=i, off=off, pm=pm: e.tensor_copy(
                            out=xs_ext[:, off:off + n], in_=pm[:, 0:n]), inc=vv)
                    else:
                        g = i - 16
                        rb = rbuf[:, CH * (i % 2):CH * (i % 2) + n]
                        P.wait("vector", st)
                        P.do("vector", lambda e, rb=rb, pm=pm: e.tensor_copy(
                            out=rb, in_=pm[:, 0:n]), inc=vv)
                        P.wait("sync", vv)
                        P.dma("sync", res_scr[g, :, CH * c:CH * c + n], rb, st)
                    P.wait("tensor", vv)

            def conv_u_chunk(n, c):
                for g in range(16):
                    base = CE * g
                    P.do("vector", lambda e, g=g, base=base: e.tensor_scalar(
                        out=convt[:, 0:n], in0=xs_ext[:, base:base + n],
                        scalar1=cw[:, 4 * g:4 * g + 1], scalar2=None, op0=ALU.mult), inc=vv)
                    for m in (1, 2, 3):
                        P.wait("vector", vv)
                        P.do("vector", lambda e, g=g, base=base, m=m: e.scalar_tensor_tensor(
                            out=convt[:, 0:n], in0=xs_ext[:, base + m:base + m + n],
                            scalar=cw[:, 4 * g + m:4 * g + m + 1], in1=convt[:, 0:n],
                            op0=ALU.mult, op1=ALU.add), inc=vv)
                    P.wait("scalar", vv)
                    P.do("scalar", lambda e, g=g: e.activation(
                        out=u_full[:, CH * g:CH * g + n], in_=convt[:, 0:n],
                        func=AF.Silu, bias=cb[:, g:g + 1]), inc=aa)
                    P.wait("sync", aa)
                    P.dma("sync", ag1v[16 + g, :, CH * c:CH * c + n],
                          u_full[:, CH * g:CH * g + n], st)
                    P.wait("vector", aa)
                    P.do("vector", lambda e, g=g, base=base: e.tensor_copy(
                        out=xs_ext[:, base:base + NH],
                        in_=xs_ext[:, base + CH:base + CH + NH]), inc=vv)

            def xdbl_chunk(n, c):
                P.wait("tensor", aa, vv)
                for g in range(16):
                    P.do("tensor", lambda e, g=g: e.matmul(
                        ps_x[:, 0:n], w_xp[:, 96 * g:96 * (g + 1)],
                        u_full[:, CH * g:CH * g + n], start=(g == 0), stop=(g == 15)),
                        inc=(tt if g == 15 else None))
                P.wait("vector", tt)
                P.do("vector", lambda e: e.tensor_copy(out=bc_sb[:, 0:n], in_=ps_x[64:96, 0:n]), inc=vv)
                P.do("vector", lambda e: e.tensor_copy(out=din_bf[:, 0:n], in_=ps_x[0:64, 0:n]), inc=vv)
                P.wait("sync", vv)
                P.dma("sync", ag1v[32, 0:32, CH * c:CH * c + n], bc_sb[:, 0:n], st)
                P.wait("tensor", vv)
                for i in range(16):
                    pm = (ps_a, ps_b)[i % 2]
                    P.do("tensor", lambda e, i=i, pm=pm: e.matmul(
                        pm[:, 0:n], w_dt[:, 128 * i:128 * (i + 1)], din_bf[:, 0:n],
                        start=True, stop=True), inc=tt)
                    P.wait("scalar", tt)
                    P.wait("vector", tt)
                    P.do("scalar", lambda e, i=i, pm=pm: e.activation(
                        out=sp1[:, 0:n], in_=pm[:, 0:n], func=AF.Abs,
                        bias=dtb[:, i:i + 1]), inc=aa)
                    P.wait("scalar", aa)
                    P.do("scalar", lambda e: e.activation(
                        out=sp1[:, 0:n], in_=sp1[:, 0:n], func=AF.Exp, scale=cst[:, 1:2]), inc=aa)
                    P.wait("scalar", aa)
                    P.do("scalar", lambda e: e.activation(
                        out=sp1[:, 0:n], in_=sp1[:, 0:n], func=AF.Ln, bias=cst[:, 0:1]), inc=aa)
                    P.wait("vector", aa)
                    P.do("vector", lambda e, i=i, pm=pm: e.tensor_scalar(
                        out=sp2[:, 0:n], in0=pm[:, 0:n], scalar1=dtb[:, i:i + 1],
                        scalar2=cst[:, 5:6], op0=ALU.add, op1=ALU.max), inc=vv)
                    P.wait("vector", aa, st, vv)
                    db = dbuf[:, CH * (i % 2):CH * (i % 2) + n]
                    P.do("vector", lambda e, db=db: e.tensor_add(
                        out=db, in0=sp1[:, 0:n], in1=sp2[:, 0:n]), inc=vv)
                    P.wait("sync", vv)
                    P.dma("sync", ag1v[i, :, CH * c:CH * c + n], db, st)
                    P.wait("tensor", vv, aa)

            ln_rms_chunk(NH, 0, halo=True)
            in_proj_chunk(NH, -1, halo=True)
            for c in range(TOKC // CH):
                ln_rms_chunk(CH, NH + CH * c, halo=False)
                in_proj_chunk(CH, c, halo=False)
                conv_u_chunk(CH, c)
                xdbl_chunk(CH, c)
            P.wait("sync", st)
            P.barrier(allsems)

    _phase_1()
    if upto >= 2:
        P.wait("gpsimd", st)
        P.collective("AllGather", ALU.bypass, ag1_in, ag1_out, cc1)

    def _phase_2():
        with P.scope():
            NTC = T // TC
            dl_sb = P.sb("s_dl", (128, 2 * T), BF16)
            u_sb = P.sb("s_u", (128, 2 * T), BF16)
            du_sb = P.sb("s_du", (128, 2 * T), BF16)
            brep = P.sb("s_brep", (128, T), BF16)
            crep = P.sb("s_crep", (128, T), BF16)
            brow = P.sb("s_brow", (16, T), BF16)
            crow = P.sb("s_crow", (16, T), BF16)
            sel = P.sb("s_sel", (128, 16 * 128), BF16)
            nsel = P.sb("s_nsel", (128, 16 * 128), BF16)
            sbsel = P.sb("s_sbsel", (16, 128), BF16)
            seltmp = P.sb("s_seltmp", (128, 16 * 128), F32)
            acol = P.sb("s_acol", (128, 32), F32)
            dcol = P.sb("s_dcol", (128, 2), F32)
            state = P.sb("s_state", (128, 32), F32)
            sidx = P.sb("s_sidx", (128, 32), I32)
            da_sb = P.sb("s_da", (128, TC), F32)
            dbu_sb = P.sb("s_dbu", (128, TC), F32)
            h_sb = P.sb("s_h", (128, TC), F32)
            hc_sb = P.sb("s_hc", (128, TC), BF16)
            ybuf = P.sb("s_ybuf", (128, 2 * TC), BF16)
            ps_dr = P.ps("s_psdr", (128, TC))
            ps_ur = P.ps("s_psur", (128, TC))
            ps_y = P.ps("s_psy", (128, TC))
            ps_br = P.ps("s_psbr", (128, TC))

            for g in range(16):
                P.dma("sync", seltmp[:, 128 * g:128 * (g + 1)], sel_i[g, :, :], ld, fence=False)
            P.fence("sync", ld)
            P.wait("vector", ld)
            P.do("vector", lambda e: e.tensor_copy(out=sel[:, :], in_=seltmp[:, :]), inc=vv)
            P.wait("sync", vv)
            for g in range(16):
                P.dma("sync", seltmp[:, 128 * g:128 * (g + 1)], nsel_i[g, :, :], ld, fence=False)
            P.fence("sync", ld)
            P.wait("vector", ld)
            P.do("vector", lambda e: e.tensor_copy(out=nsel[:, :], in_=seltmp[:, :]), inc=vv)
            P.wait("sync", vv)
            P.dma("sync", seltmp[0:16, 0:128], sbsel_i, ld)
            P.wait("vector", ld)
            P.do("vector", lambda e: e.tensor_copy(out=sbsel[:, :], in_=seltmp[0:16, 0:128]), inc=vv)
            P.dma("sync", acol[:, :], alog_a, ld)
            P.dma("sync", dcol[:, :], d_i, ld)
            P.dma("sync", sidx[:, :], sidx_i, ld)
            P.wait("scalar", ld)
            P.do("scalar", lambda e: e.activation(out=acol[:, :], in_=acol[:, :], func=AF.Exp), inc=aa)
            P.wait("vector", aa)
            P.do("vector", lambda e: e.tensor_scalar(
                out=acol[:, :], in0=acol[:, :], scalar1=-1.0, scalar2=None, op0=ALU.mult), inc=vv)

            for b in range(B):
                P.wait("sync", vv, aa, tt, st)
                P.wait("gpsimd", vv, aa, tt, st, ld)
                if b == 0:
                    P.wait("gpsimd", cc1)
                    P.wait("sync", cc1)
                for rr in range(2):
                    for qq in range(4):
                        q = 4 * b + qq
                        P.do("gpsimd", lambda e, rr=rr, q=q, qq=qq: e.indirect_dma_start(
                            out=dl_sb[:, T * rr + TOKC * qq:T * rr + TOKC * (qq + 1)],
                            out_offset=None, in_=ag1_out,
                            in_offset=IndirectOffsetOnAxis(sidx[:, rr * 8 + q:rr * 8 + q + 1], 0)),
                            inc=gd, k=16)
                        P.do("gpsimd", lambda e, rr=rr, q=q, qq=qq: e.indirect_dma_start(
                            out=u_sb[:, T * rr + TOKC * qq:T * rr + TOKC * (qq + 1)],
                            out_offset=None, in_=ag1_out,
                            in_offset=IndirectOffsetOnAxis(sidx[:, 16 + rr * 8 + q:16 + rr * 8 + q + 1], 0)),
                            inc=gd, k=16)
                P.fence("gpsimd", gd)
                for qq in range(4):
                    q = 4 * b + qq
                    P.dma("sync", brow[:, TOKC * qq:TOKC * (qq + 1)],
                          ag1ov[q, 32, 0:16, :], ld, fence=False)
                    P.dma("sync", crow[:, TOKC * qq:TOKC * (qq + 1)],
                          ag1ov[q, 32, 16:32, :], ld, fence=False)
                P.fence("sync", ld)
                P.wait("vector", ld, gd)
                for rr in range(2):
                    P.do("vector", lambda e, rr=rr: e.tensor_mul(
                        out=du_sb[:, T * rr:T * (rr + 1)], in0=dl_sb[:, T * rr:T * (rr + 1)],
                        in1=u_sb[:, T * rr:T * (rr + 1)]), inc=vv)
                P.wait("tensor", vv, ld, gd)
                for cc0 in range(NTC):
                    lo = TC * cc0
                    P.do("tensor", lambda e, lo=lo: e.matmul(
                        ps_br[:, :], sbsel[:, :], brow[0:16, lo:lo + TC],
                        start=True, stop=True), inc=tt)
                    P.wait("vector", tt)
                    P.do("vector", lambda e, lo=lo: e.tensor_copy(
                        out=brep[:, lo:lo + TC], in_=ps_br[:, :]), inc=vv)
                    P.wait("tensor", vv)
                    P.do("tensor", lambda e, lo=lo: e.matmul(
                        ps_br[:, :], sbsel[:, :], crow[0:16, lo:lo + TC],
                        start=True, stop=True), inc=tt)
                    P.wait("vector", tt)
                    P.do("vector", lambda e, lo=lo: e.tensor_copy(
                        out=crep[:, lo:lo + TC], in_=ps_br[:, :]), inc=vv)
                    P.wait("tensor", vv)
                P.do("vector", lambda e: e.memset(state[:, :], 0.0), inc=vv)
                for cc0 in range(NTC):
                    lo = TC * cc0
                    for r in range(2):
                        for gg in range(16):
                            g = 16 * r + gg
                            P.wait("tensor", aa)
                            P.do("tensor", lambda e, r=r, gg=gg, lo=lo: e.matmul(
                                ps_dr[:, :], sel[:, 128 * gg:128 * (gg + 1)],
                                dl_sb[:, T * r + lo:T * r + lo + TC],
                                start=True, stop=True), inc=tt)
                            P.wait("tensor", vv)
                            P.do("tensor", lambda e, r=r, gg=gg, lo=lo: e.matmul(
                                ps_ur[:, :], sel[:, 128 * gg:128 * (gg + 1)],
                                du_sb[:, T * r + lo:T * r + lo + TC],
                                start=True, stop=True), inc=tt)
                            P.wait("scalar", tt, vv)
                            P.do("scalar", lambda e, g=g: e.activation(
                                out=da_sb[:, :], in_=ps_dr[:, :], func=AF.Exp,
                                scale=acol[:, g:g + 1]), inc=aa)
                            P.wait("vector", tt)
                            P.do("vector", lambda e, lo=lo: e.tensor_mul(
                                out=dbu_sb[:, :], in0=ps_ur[:, :], in1=brep[:, lo:lo + TC]), inc=vv)
                            P.wait("vector", aa, vv)
                            P.do("vector", lambda e, g=g: e.tensor_tensor_scan(
                                out=h_sb[:, :], data0=da_sb[:, :], data1=dbu_sb[:, :],
                                initial=state[:, g:g + 1], op0=ALU.mult, op1=ALU.add), inc=vv)
                            P.wait("vector", vv)
                            P.do("vector", lambda e, g=g: e.tensor_copy(
                                out=state[:, g:g + 1], in_=h_sb[:, TC - 1:TC]), inc=vv)
                            P.do("vector", lambda e, lo=lo: e.tensor_mul(
                                out=hc_sb[:, :], in0=h_sb[:, :], in1=crep[:, lo:lo + TC]), inc=vv)
                            P.wait("tensor", vv)
                            P.do("tensor", lambda e, gg=gg: e.matmul(
                                ps_y[:, :], nsel[:, 128 * gg:128 * (gg + 1)], hc_sb[:, :],
                                start=(gg == 0), stop=(gg == 15)),
                                inc=(tt if gg == 15 else None))
                        P.wait("vector", tt, st)
                        yb = ybuf[:, TC * r:TC * r + TC]
                        P.do("vector", lambda e, r=r, lo=lo, yb=yb: e.scalar_tensor_tensor(
                            out=yb, in0=u_sb[:, T * r + lo:T * r + lo + TC],
                            scalar=dcol[:, r:r + 1], in1=ps_y[:, :],
                            op0=ALU.mult, op1=ALU.add), inc=vv)
                        P.wait("sync", vv)
                        cpr = 4 * b + lo // 2048
                        P.dma("sync", ag2v[2 * cpr + r, :, lo % 2048:lo % 2048 + TC], yb, st)
                        P.wait("tensor", vv)
            P.wait("sync", st)
            P.barrier(allsems)

    if upto >= 2:
        _phase_2()
    if upto >= 3:
        P.wait("gpsimd", st)
        P.collective("AllGather", ALU.bypass, ag2_in, ag2_out, cc2)
        P.wait("sync", cc2)
        P.wait("gpsimd", cc2)
    def _phase_3():
        with P.scope():
            CHMN = TOKC // CHM
            NT4 = CHM // 128
            w_op = P.sb("m_wop", (128, 16 * D), BF16)
            w1 = P.sb("m_w1", (128, E * 8 * HH), BF16)
            w2 = P.sb("m_w2", (128, E * 2 * D), BF16)
            b2r = P.sb("m_b2r", (1, E * D), BF16)
            wgf = P.sb("m_wgf", (128, 8 * E), F32)
            gwf = P.sb("m_gwf", (128, 8 * E), F32)
            lnsc = P.sb("m_lnsc", (128, 8 * E), F32)
            lnbi = P.sb("m_lnbi", (128, 8 * E), F32)
            b1c = P.sb("m_b1c", (128, 16), F32)
            ident = P.sb("m_ident", (128, 128), BF16)
            identf = P.sb("m_identf", (128, 128), F32)
            wtmp = P.sb("m_wtmp", (128, D), F32)
            ones_bf = P.sb("m_onesbf", (128, 1), BF16)
            oner_f = P.sb("m_onerf", (1, 128), F32)
            oner_b = P.sb("m_onerb", (1, 128), BF16)
            cst = P.sb("m_cst", (128, 8), F32)
            xc = P.sb("m_xc", (128, 8 * CHM), F32)
            xnc = P.sb("m_xnc", (128, 8 * CHM), F32)
            resc = P.sb("m_resc", (128, 16 * CHM), BF16)
            yc = P.sb("m_yc", (128, 16 * CHM), BF16)
            xnb = P.sb("m_xnb", (128, 8 * CHM), BF16)
            sq = P.sb("m_sq", (128, 8 * CHM), BF16)
            tmpf = P.sb("m_tmpf", (128, 8 * CHM), F32)
            mrow = P.sb("m_mrow", (1, E * CHM), BF16)
            mask_sb = P.sb("m_mask", (128, NT4 * E), BF16)
            nmx = P.sb("m_nmx", (128, NT4), F32)
            rsum = P.sb("m_rsum", (128, NT4), F32)
            affc = P.sb("m_affc", (128, NT4 * E), F32)
            srow = P.sb("m_srow", (1, 6 * CHM), F32)
            xn1b = P.sb("m_xn1b", (128, 8 * CHM), BF16)
            tmrow = P.sb("m_tmrow", (8, 128), F32)
            yidx = P.sb("m_yidx", (128, 16), I32)
            tbuf = P.sb("m_tbuf", (128, 1024), BF16)
            ps_a = P.ps("m_psa", (128, CHM))
            ps_b = P.ps("m_psb", (128, CHM))
            ps_r1 = P.ps("m_psr1", (128, CHM))
            ps_r2 = P.ps("m_psr2", (128, CHM))
            ps_l = P.ps("m_psl", (128, E))
            ps_m = P.ps("m_psm", (1, 1024), BF16)
            ps_mr = P.ps("m_psmr", (128, CHM))
            ps_tb = P.ps("m_pstb", (128, 128), BF16)

            MU, EX2, VAR, RSTD = range(4)

            def row(i):
                return srow[0:1, CHM * i:CHM * i + CHM]

            for k in range(16):
                P.dma("sync", wtmp[:, 0:D], opw_t[k, :, :], ld)
                P.wait("vector", ld)
                P.do("vector", lambda e, k=k: e.tensor_copy(
                    out=w_op[:, D * k:D * (k + 1)], in_=wtmp[:, 0:D]), inc=vv)
                P.wait("sync", vv)
            for e_ in range(E):
                for k in range(8):
                    P.dma("sync", wtmp[:, 0:HH], w1_i[e_, k, :, :], ld)
                    P.wait("vector", ld)
                    P.do("vector", lambda e, e_=e_, k=k: e.tensor_copy(
                        out=w1[:, (8 * e_ + k) * HH:(8 * e_ + k + 1) * HH], in_=wtmp[:, 0:HH]), inc=vv)
                    P.wait("sync", vv)
                for k in range(2):
                    P.dma("sync", wtmp[:, 0:D], w2_i[e_, k, :, :], ld)
                    P.wait("vector", ld)
                    P.do("vector", lambda e, e_=e_, k=k: e.tensor_copy(
                        out=w2[:, (2 * e_ + k) * D:(2 * e_ + k + 1) * D], in_=wtmp[:, 0:D]), inc=vv)
                    P.wait("sync", vv)
            P.dma("gpsimd", b2r[:, :], b2_i, gd, fence=False)
            P.dma("gpsimd", ident[:, :], ident_i, gd, fence=False)
            P.dma("gpsimd", ones_bf[:, :], ones_c, gd, fence=False)
            P.dma("gpsimd", oner_b[:, :], ones_r, gd)
            for k in range(8):
                P.dma("sync", wgf[:, E * k:E * (k + 1)], wg_t[k, :, :], ld, fence=False)
                P.dma("sync", gwf[:, E * k:E * (k + 1)], gw_t[k, :, :], ld, fence=False)
                P.dma("sync", lnsc[:, E * k:E * (k + 1)], lnsc_i[k, :, :], ld, fence=False)
                P.dma("sync", lnbi[:, E * k:E * (k + 1)], lnbi_i[k, :, :], ld, fence=False)
            P.dma("sync", b1c[:, :], b1_i, ld, fence=False)
            P.dma("sync", oner_f[:, :], ones_r, ld, fence=False)
            P.dma("sync", identf[:, :], ident_i, ld, fence=False)
            P.dma("sync", yidx[:, :], yidx_i, ld, fence=False)
            P.dma("sync", cst[:, :], cst_i, ld)

            def stats_ln(src_f32, n, out_bf):
                P.wait("scalar", vv, tt)
                for k in range(8):
                    P.do("scalar", lambda e, k=k: e.activation(
                        out=sq[:, CHM * k:CHM * k + n], in_=src_f32[:, CHM * k:CHM * k + n],
                        func=AF.Square), inc=aa)
                P.wait("vector", vv, tt)
                for k in range(8):
                    P.do("vector", lambda e, k=k: e.tensor_copy(
                        out=xnb[:, CHM * k:CHM * k + n],
                        in_=src_f32[:, CHM * k:CHM * k + n]), inc=vv)
                P.wait("tensor", vv, aa, gd)
                for k in range(8):
                    P.do("tensor", lambda e, k=k: e.matmul(
                        ps_a[0:1, 0:n], ones_bf[:, :], xnb[:, CHM * k:CHM * k + n],
                        start=(k == 0), stop=(k == 7)), inc=(tt if k == 7 else None))
                for k in range(8):
                    P.do("tensor", lambda e, k=k: e.matmul(
                        ps_b[0:1, 0:n], ones_bf[:, :], sq[:, CHM * k:CHM * k + n],
                        start=(k == 0), stop=(k == 7)), inc=(tt if k == 7 else None))
                P.wait("vector", tt)
                P.do("vector", lambda e: e.tensor_scalar(
                    out=row(MU)[:, 0:n], in0=ps_a[0:1, 0:n], scalar1=1.0 / D,
                    scalar2=None, op0=ALU.mult), inc=vv)
                P.do("vector", lambda e: e.tensor_scalar(
                    out=row(EX2)[:, 0:n], in0=ps_b[0:1, 0:n], scalar1=1.0 / D,
                    scalar2=None, op0=ALU.mult), inc=vv)
                P.wait("vector", vv)
                P.do("vector", lambda e: e.tensor_mul(
                    out=row(VAR)[:, 0:n], in0=row(MU)[:, 0:n], in1=row(MU)[:, 0:n]), inc=vv)
                P.wait("vector", vv)
                P.do("vector", lambda e: e.tensor_sub(
                    out=row(VAR)[:, 0:n], in0=row(EX2)[:, 0:n], in1=row(VAR)[:, 0:n]), inc=vv)
                P.wait("scalar", vv)
                P.do("scalar", lambda e: e.activation(
                    out=row(RSTD)[:, 0:n], in_=row(VAR)[:, 0:n], func=AF.Ln,
                    bias=cst[0:1, 3:4]), inc=aa)
                P.wait("scalar", aa)
                P.do("scalar", lambda e: e.activation(
                    out=row(RSTD)[:, 0:n], in_=row(RSTD)[:, 0:n], func=AF.Exp,
                    scale=cst[0:1, 2:3]), inc=aa)
                P.wait("tensor", aa, vv)
                P.do("tensor", lambda e: e.matmul(
                    ps_r1[:, 0:n], oner_f[:, :], row(MU)[:, 0:n], start=True, stop=True), inc=tt)
                P.do("tensor", lambda e: e.matmul(
                    ps_r2[:, 0:n], oner_f[:, :], row(RSTD)[:, 0:n], start=True, stop=True), inc=tt)
                P.wait("vector", tt)
                for k in range(8):
                    P.do("vector", lambda e, k=k: e.tensor_sub(
                        out=tmpf[:, CHM * k:CHM * k + n],
                        in0=src_f32[:, CHM * k:CHM * k + n], in1=ps_r1[:, 0:n]), inc=vv)
                P.wait("vector", vv)
                for k in range(8):
                    P.do("vector", lambda e, k=k: e.tensor_mul(
                        out=tmpf[:, CHM * k:CHM * k + n], in0=tmpf[:, CHM * k:CHM * k + n],
                        in1=ps_r2[:, 0:n]), inc=vv)
                P.wait("vector", vv)
                for k in range(8):
                    P.do("vector", lambda e, k=k: e.tensor_copy(
                        out=out_bf[:, CHM * k:CHM * k + n], in_=tmpf[:, CHM * k:CHM * k + n]), inc=vv)

            xhat = resc
            h1m = yc
            inpe = xnb
            xn1f = xnc

            for c in range(CHMN):
                co = CHM * c
                P.wait("sync", vv, aa, tt, st)
                for k in range(8):
                    P.dma("sync", xc[:, CHM * k:CHM * (k + 1)],
                          x_ext[k, :, NH + co:NH + co + CHM], ld, fence=False)
                    P.dma("sync", xnc[:, CHM * k:CHM * (k + 1)], xn_scr[k, :, co:co + CHM],
                          ld, fence=False)
                for g in range(16):
                    P.dma("sync", resc[:, CHM * g:CHM * (g + 1)], res_scr[g, :, co:co + CHM],
                          ld, fence=False)
                P.fence("sync", ld)
                P.wait("gpsimd", vv, tt, st, ld)
                if c == 0:
                    P.wait("gpsimd", cc2)
                for g in range(16):
                    P.do("gpsimd", lambda e, g=g, co=co: e.indirect_dma_start(
                        out=yc[:, CHM * g:CHM * (g + 1)], out_offset=None,
                        in_=ag2_out, element_offset=co,
                        in_offset=IndirectOffsetOnAxis(yidx[:, g:g + 1], 0)),
                        inc=gd, k=16)
                P.fence("gpsimd", gd)
                for t4 in range(NT4):
                    P.wait("tensor", vv, aa, ld, gd)
                    for k in range(8):
                        P.do("tensor", lambda e, t4=t4, k=k: e.matmul(
                            ps_l[:, :], xnc[:, CHM * k + 128 * t4:CHM * k + 128 * (t4 + 1)],
                            wgf[:, E * k:E * (k + 1)], start=(k == 0), stop=(k == 7)),
                            inc=(tt if k == 7 else None))
                    P.wait("vector", tt)
                    P.do("vector", lambda e, t4=t4: e.reduce_max(
                        out=nmx[:, t4:t4 + 1], in_=ps_l[:, :], axis=AX.X), inc=vv)
                    P.wait("vector", vv)
                    P.do("vector", lambda e, t4=t4: e.tensor_scalar(
                        out=mask_sb[:, E * t4:E * (t4 + 1)], in0=ps_l[:, :],
                        scalar1=nmx[:, t4:t4 + 1], scalar2=None, op0=ALU.is_ge), inc=vv)
                    P.wait("tensor", vv)
                    for e_ in range(E):
                        P.do("tensor", lambda e, t4=t4, e_=e_: e.transpose(
                            ps_m[0:1, 128 * e_:128 * (e_ + 1)],
                            mask_sb[:, E * t4 + e_:E * t4 + e_ + 1], ident[:, :]),
                            inc=(tt if e_ == E - 1 else None))
                    P.wait("vector", tt)
                    for e_ in range(E):
                        P.do("vector", lambda e, t4=t4, e_=e_: e.tensor_copy(
                            out=mrow[0:1, CHM * e_ + 128 * t4:CHM * e_ + 128 * (t4 + 1)],
                            in_=ps_m[0:1, 128 * e_:128 * (e_ + 1)]), inc=vv)
                    P.wait("tensor", vv)
                P.wait("vector", ld)
                for k in range(8):
                    P.do("vector", lambda e, k=k: e.tensor_add(
                        out=xc[:, CHM * k:CHM * (k + 1)], in0=xc[:, CHM * k:CHM * (k + 1)],
                        in1=xnc[:, CHM * k:CHM * (k + 1)]), inc=vv)
                stats_ln(xnc, CHM, xn1b)
                P.wait("scalar", ld, vv, gd)
                for g in range(16):
                    P.do("scalar", lambda e, g=g: e.activation(
                        out=resc[:, CHM * g:CHM * (g + 1)], in_=resc[:, CHM * g:CHM * (g + 1)],
                        func=AF.Silu), inc=aa)
                P.wait("vector", aa, gd)
                for g in range(16):
                    P.do("vector", lambda e, g=g: e.tensor_mul(
                        out=resc[:, CHM * g:CHM * (g + 1)], in0=resc[:, CHM * g:CHM * (g + 1)],
                        in1=yc[:, CHM * g:CHM * (g + 1)]), inc=vv)
                P.wait("tensor", vv)
                for i in range(8):
                    pm = (ps_a, ps_b)[i % 2]
                    for k in range(16):
                        P.do("tensor", lambda e, i=i, k=k, pm=pm: e.matmul(
                            pm[:, :], w_op[:, D * k + 128 * i:D * k + 128 * (i + 1)],
                            resc[:, CHM * k:CHM * (k + 1)], start=(k == 0), stop=(k == 15)),
                            inc=(tt if k == 15 else None))
                    P.wait("vector", tt)
                    P.do("vector", lambda e, i=i, pm=pm: e.tensor_add(
                        out=xc[:, CHM * i:CHM * (i + 1)], in0=xc[:, CHM * i:CHM * (i + 1)],
                        in1=pm[:, :]), inc=vv)
                    P.wait("tensor", vv)
                P.wait("vector", vv)
                for k in range(8):
                    P.do("vector", lambda e, k=k: e.tensor_copy(
                        out=xhat[:, CHM * k:CHM * (k + 1)], in_=xn1b[:, CHM * k:CHM * (k + 1)]), inc=vv)
                for e_ in range(E):
                    P.wait("vector", vv, tt)
                    for k in range(8):
                        P.do("vector", lambda e, e_=e_, k=k: e.tensor_scalar(
                            out=inpe[:, CHM * k:CHM * (k + 1)], in0=xhat[:, CHM * k:CHM * (k + 1)],
                            scalar1=lnsc[:, E * k + e_:E * k + e_ + 1],
                            scalar2=lnbi[:, E * k + e_:E * k + e_ + 1],
                            op0=ALU.mult, op1=ALU.add), inc=vv)
                    P.wait("tensor", vv, aa, gd)
                    P.do("tensor", lambda e, e_=e_: e.matmul(
                        ps_mr[:, :], oner_b[:, :], mrow[0:1, CHM * e_:CHM * (e_ + 1)],
                        start=True, stop=True), inc=tt)
                    for hi in range(2):
                        pm = (ps_a, ps_b)[hi]
                        for k in range(8):
                            P.do("tensor", lambda e, e_=e_, hi=hi, k=k, pm=pm: e.matmul(
                                pm[:, :], w1[:, (8 * e_ + k) * HH + 128 * hi:(8 * e_ + k) * HH + 128 * (hi + 1)],
                                inpe[:, CHM * k:CHM * (k + 1)], start=(k == 0), stop=(k == 7)),
                                inc=(tt if k == 7 else None))
                        P.wait("scalar", tt)
                        P.do("scalar", lambda e, e_=e_, hi=hi, pm=pm: e.activation(
                            out=h1m[:, CHM * (2 * e_ + hi):CHM * (2 * e_ + hi + 1)], in_=pm[:, :],
                            func=AF.Gelu, bias=b1c[:, 2 * e_ + hi:2 * e_ + hi + 1]), inc=aa)
                    P.wait("vector", tt, aa)
                    for hi in range(2):
                        P.do("vector", lambda e, e_=e_, hi=hi: e.tensor_mul(
                            out=h1m[:, CHM * (2 * e_ + hi):CHM * (2 * e_ + hi + 1)],
                            in0=h1m[:, CHM * (2 * e_ + hi):CHM * (2 * e_ + hi + 1)],
                            in1=ps_mr[:, :]), inc=vv)
                    P.wait("tensor", vv)
                for i in range(8):
                    pm = (ps_a, ps_b)[i % 2]
                    P.wait("tensor", vv, aa)
                    nmm = 3 * E
                    cnt = 0
                    for e_ in range(E):
                        for hi in range(2):
                            cnt += 1
                            s0, p0 = (cnt == 1), (cnt == nmm)
                            P.do("tensor", lambda e, i=i, e_=e_, hi=hi, pm=pm, s=s0, p=p0: e.matmul(
                                pm[:, :], w2[:, (2 * e_ + hi) * D + 128 * i:(2 * e_ + hi) * D + 128 * (i + 1)],
                                h1m[:, CHM * (2 * e_ + hi):CHM * (2 * e_ + hi + 1)],
                                start=s, stop=p), inc=(tt if p0 else None))
                        cnt += 1
                        p0 = (cnt == nmm)
                        P.do("tensor", lambda e, i=i, e_=e_, pm=pm, p=p0: e.matmul(
                            pm[:, :], b2r[0:1, D * e_ + 128 * i:D * e_ + 128 * (i + 1)],
                            mrow[0:1, CHM * e_:CHM * (e_ + 1)], start=False, stop=p),
                            inc=(tt if p0 else None))
                    P.wait("vector", tt)
                    P.do("vector", lambda e, i=i, pm=pm: e.tensor_add(
                        out=xc[:, CHM * i:CHM * (i + 1)], in0=xc[:, CHM * i:CHM * (i + 1)],
                        in1=pm[:, :]), inc=vv)
                    P.wait("tensor", vv)
                P.wait("sync", vv)
                for k in range(8):
                    P.dma("sync", x1_scr[k, :, co:co + CHM], xc[:, CHM * k:CHM * (k + 1)], st, fence=False)
                P.fence("sync", st)
                stats_ln(xc, CHM, xn1b)
                P.wait("vector", vv)
                for k in range(8):
                    P.do("vector", lambda e, k=k: e.tensor_copy(
                        out=xn1f[:, CHM * k:CHM * (k + 1)], in_=tmpf[:, CHM * k:CHM * (k + 1)]), inc=vv)
                P.wait("sync", vv)
                for k in range(8):
                    P.dma("sync", xn1_scr[k, :, co:co + CHM], xn1b[:, CHM * k:CHM * (k + 1)], st, fence=False)
                    P.dma("sync", ag3av[k, :, co:co + CHM], xn1b[:, CHM * k:CHM * (k + 1)], st, fence=False)
                P.fence("sync", st)
                for t4 in (() if "agb" in mid_skip else range(NT4)):
                    P.wait("tensor", vv, st, aa)
                    for k in range(8):
                        P.do("tensor", lambda e, t4=t4, k=k: e.transpose(
                            ps_tb[:, :], xn1b[:, CHM * k + 128 * t4:CHM * k + 128 * (t4 + 1)],
                            ident[:, :]), inc=tt)
                        P.wait("vector", tt)
                        P.do("vector", lambda e, k=k: e.tensor_copy(
                            out=tbuf[:, 128 * k:128 * (k + 1)], in_=ps_tb[:, :]), inc=vv)
                        P.wait("tensor", vv)
                    P.wait("sync", vv)
                    P.dma("sync", ag3b_in[co + 128 * t4:co + 128 * (t4 + 1), :], tbuf[:, :], st)
                for t4 in range(NT4):
                    P.wait("tensor", vv, aa)
                    for k in range(8):
                        P.do("tensor", lambda e, t4=t4, k=k: e.matmul(
                            ps_l[:, :], xn1f[:, CHM * k + 128 * t4:CHM * k + 128 * (t4 + 1)],
                            gwf[:, E * k:E * (k + 1)], start=(k == 0), stop=(k == 7)),
                            inc=(tt if k == 7 else None))
                    P.wait("vector", tt)
                    P.do("vector", lambda e, t4=t4: e.reduce_max(
                        out=nmx[:, t4:t4 + 1], in_=ps_l[:, :], axis=AX.X, negate=True), inc=vv)
                    P.wait("scalar", vv)
                    P.do("scalar", lambda e, t4=t4: e.activation(
                        out=affc[:, E * t4:E * (t4 + 1)], in_=ps_l[:, :], func=AF.Exp,
                        bias=nmx[:, t4:t4 + 1], accum_out=rsum[:, t4:t4 + 1]), inc=aa)
                    P.wait("vector", aa)
                    P.do("vector", lambda e, t4=t4: e.reciprocal(
                        out=rsum[:, t4:t4 + 1], in_=rsum[:, t4:t4 + 1]), inc=vv)
                    P.wait("vector", vv)
                    P.do("vector", lambda e, t4=t4: e.tensor_scalar(
                        out=affc[:, E * t4:E * (t4 + 1)], in0=affc[:, E * t4:E * (t4 + 1)],
                        scalar1=rsum[:, t4:t4 + 1], scalar2=None, op0=ALU.mult), inc=vv)
                    if "afft" not in mid_skip:
                        P.wait("tensor", vv, aa)
                        P.do("tensor", lambda e, t4=t4: e.transpose(
                            ps_mr[0:8, 0:128], affc[:, E * t4:E * (t4 + 1)], identf[:, :]), inc=tt)
                        P.wait("vector", tt, st)
                        P.do("vector", lambda e: e.tensor_copy(
                            out=tmrow[:, :], in_=ps_mr[0:8, 0:128]), inc=vv)
                        P.wait("sync", vv)
                        P.dma("sync", ag3c_in[:, co + 128 * t4:co + 128 * (t4 + 1)], tmrow[:, :], st)
                    P.wait("tensor", aa, vv)
            P.wait("sync", st)
            P.barrier(allsems)

    if upto >= 3:
        _phase_3()
    if upto >= 4:
        P.wait("gpsimd", st)
        P.collective("AllGather", ALU.bypass, ag3a_in, ag3a_out, cc3a)
        P.collective("AllGather", ALU.bypass, ag3b_in, ag3b_out, cc3b)
        P.collective("AllGather", ALU.bypass, ag3c_in, ag3c_out, cc3c)
    def _phase_4():
        with P.scope():
            ncha = [math.ceil((48 + s + 1) / 4) for s in range(NS)]
            q_dt = P.sb("a_qdt", (128, 8 * NS * 128), BF16)
            acc = P.sb("a_acc", (128, NS * 1024), F32)
            wqk = P.sb("a_wqk", (128, 8 * D), BF16)
            wov = P.sb("a_wov", (128, 8 * D), BF16)
            kb = P.sb("a_kb", (128, 8 * KC), BF16)
            vb = P.sb("a_vb", (128, 4 * 1024), BF16)
            qxb = P.sb("a_qxb", (128, 8 * KC), BF16)
            wtmp = P.sb("a_wtmp", (128, D), F32)
            p_sb = P.sb("a_p", (128, KC), BF16)
            pt_sb = P.sb("a_pt", (128, KC), BF16)
            pen = P.sb("a_pen", (128, KC), F32)
            m_sb = P.sb("a_m", (128, NS), F32)
            l_sb = P.sb("a_l", (128, NS), F32)
            mx = P.sb("a_mx", (128, 1), F32)
            dif = P.sb("a_dif", (128, 1), F32)
            alp = P.sb("a_alp", (128, 1), F32)
            bcol = P.sb("a_bcol", (128, 1), F32)
            thr = P.sb("a_thr", (128, 1), F32)
            ls = P.sb("a_ls", (128, 1), F32)
            rcol = P.sb("a_rcol", (128, 1), F32)
            qidx = P.sb("a_qidx", (128, NS), F32)
            kio = P.sb("a_kio", (128, KC), F32)
            ident = P.sb("a_ident", (128, 128), BF16)
            cst = P.sb("a_cst", (128, 8), F32)
            ax = P.sb("a_ax", (128, 1024), BF16)
            axt = P.sb("a_axt", (128, 8 * 128), BF16)
            kidx = P.sb("a_kidx", (128, 16 * 8), I32)
            vidx = P.sb("a_vidx", (128, 16 * 4), I32)
            ps_s = P.ps("a_pss", (128, KC))
            ps_t = P.ps("a_pst", (128, KC), BF16)
            ps_av = P.ps("a_psav", (128, 1024))
            ps_q = P.ps("a_psq", (128, KC))

            for k in range(8):
                P.dma("sync", wtmp[:, :], qk_i[k, :, :], ld)
                P.wait("vector", ld)
                P.do("vector", lambda e, k=k: e.tensor_copy(
                    out=wqk[:, D * k:D * (k + 1)], in_=wtmp[:, :]), inc=vv)
                P.wait("sync", vv)
                P.dma("sync", wtmp[:, :], ov_i[k, :, :], ld)
                P.wait("vector", ld)
                P.do("vector", lambda e, k=k: e.tensor_copy(
                    out=wov[:, D * k:D * (k + 1)], in_=wtmp[:, :]), inc=vv)
                P.wait("sync", vv)
            P.dma("sync", qidx[:, :], qidx_i, ld, fence=False)
            P.dma("sync", kio[:, :], kio_i, ld, fence=False)
            P.dma("sync", cst[:, :], cst_i, ld, fence=False)
            P.dma("sync", kidx[:, :], kidx_i, ld, fence=False)
            P.dma("sync", vidx[:, :], vidx_i, ld, fence=False)
            P.fence("sync", ld)
            P.dma("gpsimd", ident[:, :], ident_i, gd)
            P.wait("vector", ld)
            P.do("vector", lambda e: e.memset(m_sb[:, :], -1e30), inc=vv)
            P.do("vector", lambda e: e.memset(l_sb[:, :], 0.0), inc=vv)
            for s in range(NS):
                P.do("vector", lambda e, s=s: e.memset(acc[:, 1024 * s:1024 * (s + 1)], 0.0), inc=vv)
            for qc in range(4):
                P.wait("sync", vv, tt)
                for k in range(8):
                    P.dma("sync", qxb[:, KC * k:KC * (k + 1)], xn1_scr[k, :, KC * qc:KC * (qc + 1)],
                          ld, fence=False)
                P.fence("sync", ld)
                P.wait("tensor", ld, vv)
                for i in range(8):
                    for k in range(8):
                        P.do("tensor", lambda e, i=i, k=k: e.matmul(
                            ps_q[:, :], wqk[:, D * k + 128 * i:D * k + 128 * (i + 1)],
                            qxb[:, KC * k:KC * (k + 1)], start=(k == 0), stop=(k == 7)),
                            inc=(tt if k == 7 else None))
                    P.wait("vector", tt)
                    P.do("vector", lambda e, i=i, qc=qc: e.tensor_copy(
                        out=q_dt[:, NS * 128 * i + KC * qc:NS * 128 * i + KC * (qc + 1)],
                        in_=ps_q[:, :]), inc=vv)
                    P.wait("tensor", vv)
            for j in range(16):
                s_list = [s for s in range(NS) if ncha[s] > j]
                if not s_list:
                    continue
                P.wait("gpsimd", vv, aa, tt, st, ld)
                if j == 0:
                    P.wait("gpsimd", cc3a, cc3b)
                for kd in range(8):
                    P.do("gpsimd", lambda e, j=j, kd=kd: e.indirect_dma_start(
                        out=kb[:, KC * kd:KC * (kd + 1)], out_offset=None,
                        in_=ag3a_out, element_offset=(j % 4) * KC,
                        in_offset=IndirectOffsetOnAxis(kidx[:, j * 8 + kd:j * 8 + kd + 1], 0)),
                        inc=gd, k=16)
                for m4 in range(4):
                    P.do("gpsimd", lambda e, j=j, m4=m4: e.indirect_dma_start(
                        out=vb[:, 1024 * m4:1024 * (m4 + 1)], out_offset=None,
                        in_=ag3b_out,
                        in_offset=IndirectOffsetOnAxis(vidx[:, j * 4 + m4:j * 4 + m4 + 1], 0)),
                        inc=gd, k=16)
                P.fence("gpsimd", gd)
                for s in s_list:
                    P.wait("tensor", gd, aa, vv)
                    for k in range(8):
                        P.do("tensor", lambda e, s=s, k=k: e.matmul(
                            ps_s[:, :], q_dt[:, NS * 128 * k + 128 * s:NS * 128 * k + 128 * (s + 1)],
                            kb[:, KC * k:KC * (k + 1)], start=(k == 0), stop=(k == 7)),
                            inc=(tt if k == 7 else None))
                    P.wait("vector", tt)
                    if j >= s // 4:
                        P.do("vector", lambda e, s=s, j=j: e.tensor_scalar(
                            out=thr[:, :], in0=qidx[:, s:s + 1], scalar1=float(-KC * j),
                            scalar2=None, op0=ALU.add), inc=vv)
                        P.wait("vector", vv)
                        P.do("vector", lambda e: e.tensor_scalar(
                            out=pen[:, :], in0=kio[:, :], scalar1=thr[:, 0:1],
                            scalar2=cst[:, 6:7], op0=ALU.is_gt, op1=ALU.mult), inc=vv)
                        P.wait("vector", vv)
                        P.do("vector", lambda e: e.tensor_add(
                            out=ps_s[:, :], in0=ps_s[:, :], in1=pen[:, :]), inc=vv)
                        P.wait("vector", vv)
                    P.do("vector", lambda e: e.reduce_max(out=mx[:, :], in_=ps_s[:, :], axis=AX.X), inc=vv)
                    P.wait("vector", vv)
                    P.do("vector", lambda e, s=s: e.tensor_sub(
                        out=dif[:, :], in0=m_sb[:, s:s + 1], in1=mx[:, :]), inc=vv)
                    P.wait("vector", vv)
                    P.do("vector", lambda e: e.tensor_scalar(
                        out=dif[:, :], in0=dif[:, :], scalar1=0.0, scalar2=None,
                        op0=ALU.min), inc=vv)
                    P.do("vector", lambda e, s=s: e.tensor_max(
                        out=m_sb[:, s:s + 1], in0=m_sb[:, s:s + 1], in1=mx[:, :]), inc=vv)
                    P.wait("scalar", vv)
                    P.do("scalar", lambda e: e.activation(
                        out=alp[:, :], in_=dif[:, :], func=AF.Exp, scale=cst[:, 4:5]), inc=aa)
                    P.wait("vector", vv)
                    P.do("vector", lambda e, s=s: e.tensor_scalar(
                        out=bcol[:, :], in0=m_sb[:, s:s + 1], scalar1=-1.0 / 32, scalar2=None,
                        op0=ALU.mult), inc=vv)
                    P.wait("scalar", vv, aa)
                    P.do("scalar", lambda e: e.activation(
                        out=p_sb[:, :], in_=ps_s[:, :], func=AF.Exp, scale=cst[:, 4:5],
                        bias=bcol[:, 0:1], accum_out=ls[:, :]), inc=aa)
                    P.wait("vector", aa)
                    P.do("vector", lambda e, s=s: e.scalar_tensor_tensor(
                        out=l_sb[:, s:s + 1], in0=l_sb[:, s:s + 1], scalar=alp[:, 0:1],
                        in1=ls[:, :], op0=ALU.mult, op1=ALU.add), inc=vv)
                    P.wait("tensor", aa, vv, gd)
                    for m4 in range(4):
                        P.do("tensor", lambda e, m4=m4: e.transpose(
                            ps_t[:, 128 * m4:128 * (m4 + 1)], p_sb[:, 128 * m4:128 * (m4 + 1)],
                            ident[:, :]), inc=(tt if m4 == 3 else None))
                    P.wait("vector", tt)
                    P.do("vector", lambda e: e.tensor_copy(out=pt_sb[:, :], in_=ps_t[:, :]), inc=vv)
                    P.wait("tensor", vv)
                    for m4 in range(4):
                        for dh in range(2):
                            P.do("tensor", lambda e, m4=m4, dh=dh: e.matmul(
                                ps_av[:, 512 * dh:512 * (dh + 1)],
                                pt_sb[:, 128 * m4:128 * (m4 + 1)],
                                vb[:, 1024 * m4 + 512 * dh:1024 * m4 + 512 * (dh + 1)],
                                start=(m4 == 0), stop=(m4 == 3)),
                                inc=(tt if (m4 == 3 and dh == 1) else None))
                    P.wait("vector", tt)
                    for dh in range(2):
                        sl = acc[:, 1024 * s + 512 * dh:1024 * s + 512 * (dh + 1)]
                        P.do("vector", lambda e, sl=sl: e.tensor_scalar(
                            out=sl, in0=sl, scalar1=alp[:, 0:1], scalar2=None, op0=ALU.mult), inc=vv)
                    P.wait("vector", vv)
                    for dh in range(2):
                        sl = acc[:, 1024 * s + 512 * dh:1024 * s + 512 * (dh + 1)]
                        P.do("vector", lambda e, sl=sl, dh=dh: e.tensor_add(
                            out=sl, in0=sl, in1=ps_av[:, 512 * dh:512 * (dh + 1)]), inc=vv)
                    P.wait("tensor", vv)
            for s in range(NS):
                P.wait("vector", tt, aa, st)
                P.do("vector", lambda e, s=s: e.reciprocal(out=rcol[:, :], in_=l_sb[:, s:s + 1]), inc=vv)
                P.wait("vector", vv)
                P.do("vector", lambda e, s=s: e.tensor_scalar(
                    out=ax[:, :], in0=acc[:, 1024 * s:1024 * (s + 1)], scalar1=rcol[:, 0:1],
                    scalar2=None, op0=ALU.mult), inc=vv)
                P.wait("tensor", vv)
                for i8 in range(8):
                    P.do("tensor", lambda e, i8=i8: e.transpose(
                        ps_t[:, 128 * (i8 % 4):128 * (i8 % 4 + 1)], ax[:, 128 * i8:128 * (i8 + 1)],
                        ident[:, :]), inc=tt)
                    P.wait("vector", tt)
                    P.do("vector", lambda e, i8=i8: e.tensor_copy(
                        out=axt[:, 128 * i8:128 * (i8 + 1)],
                        in_=ps_t[:, 128 * (i8 % 4):128 * (i8 % 4 + 1)]), inc=vv)
                    P.wait("tensor", vv)
                for i in range(8):
                    for kd in range(8):
                        P.do("tensor", lambda e, i=i, kd=kd: e.matmul(
                            ps_q[:, 0:128], wov[:, D * kd + 128 * i:D * kd + 128 * (i + 1)],
                            axt[:, 128 * kd:128 * (kd + 1)], start=(kd == 0), stop=(kd == 7)),
                            inc=(tt if kd == 7 else None))
                    P.wait("vector", tt)
                    P.do("vector", lambda e, i=i: e.tensor_copy(
                        out=ax[:, 128 * i:128 * (i + 1)], in_=ps_q[:, 0:128]), inc=vv)
                    P.wait("tensor", vv)
                P.wait("sync", vv)
                for i in range(8):
                    P.dma("sync", attn_scr[i, :, 128 * s:128 * (s + 1)], ax[:, 128 * i:128 * (i + 1)],
                          st, fence=False)
                P.fence("sync", st)
            P.wait("sync", st)
            P.barrier(allsems)

    if upto >= 4:
        _phase_4()
    def _phase_5():
        with P.scope():
            KTOP = KTOP_G
            BIG = 60000.0
            aidx = P.sb("o_aidx", (8, 1), I32)
            affs = P.sb("o_affs", (8, TOKC), F32)
            aff2 = P.sb("o_aff2", (128, B * 64), F32)
            affT = P.sb("o_affT", (128, B * 64), F32)
            bis = P.sb("o_bis", (1, 8), I32)
            cnt1 = P.sb("o_cnt1", (128, 1), F32)
            cmp64 = P.sb("o_cmp64", (128, 64), F32)
            cntb = P.sb("o_cntb", (128, 1), BF16)
            ones_bf = P.sb("o_onesbf", (128, 1), BF16)
            oner_f = P.sb("o_onerf", (1, 128), F32)
            oner_b = P.sb("o_onerb", (1, 128), BF16)
            tau = P.sb("o_tau", (1, 2), I32)
            posi2 = P.sb("o_posi2", (128, B * 64), I32)
            posf2 = P.sb("o_posf2", (128, 64), F32)
            posjr = P.sb("o_posjr", (64, 128), F32)
            identf5 = P.sb("o_identf5", (128, 128), F32)
            ps_p = P.ps("o_psp", (128, 64))
            xrow = P.sb("o_xrow", (128, D), BF16)
            pcol = P.sb("o_pcol", (128, 1), I32)
            ident5 = P.sb("o_ident5", (128, 128), BF16)

            breg = nc.gpsimd.alloc_register("bchk")
            P.do("gpsimd", lambda e: e.reg_mov(breg, B * KTOP_G))
            P.dma("sync", oner_f[:, :], ones_r, ld, fence=False)
            P.dma("sync", identf5[:, :], ident_i, ld, fence=False)
            P.dma("sync", aidx[:, :], aidx_i, ld)
            P.dma("gpsimd", ones_bf[:, :], ones_c, gd, fence=False)
            P.dma("gpsimd", ident5[:, :], ident_i, gd, fence=False)
            P.dma("gpsimd", oner_b[:, :], ones_r, gd)
            P.wait("gpsimd", ld, cc3c)
            P.do("gpsimd", lambda e: e.indirect_dma_start(
                out=affs[:, :], out_offset=None, in_=ag3c_out,
                in_offset=IndirectOffsetOnAxis(aidx[:, :], 0)), inc=gd, k=16)
            P.fence("gpsimd", gd)
            P.dma("gpsimd", affg, affs[:, :], gd)
            affg_r = affg.rearrange("q c -> (q c)")
            ONE_F_BITS = 0x3F800000

            def _compact():
                with P.scope():
                    affr_sh = P.sb("o_affr", (1, T), F32)
                    m01 = P.sb("o_m01", (1, T), F32)
                    rnk = P.sb("o_rnk", (1, T), F32)
                    ps_r = P.ps("o_psr", (128, 8))
                    for b in range(B):
                        P.wait("sync", vv, gd)
                        P.dma("sync", affr_sh[:, :],
                              affg_r.rearrange("(b t) -> b t", b=2)[b:b + 1, :], ld)
                        P.dma("sync", aff2[:, 64 * b:64 * (b + 1)],
                              affg_r.rearrange("(b p f) -> b p f", b=2, p=128)[b, :, :], ld)
                        P.wait("vector", ld, gd)
                        P.do("vector", lambda e: e.memset(bis[0:1, 0:1], 0), inc=vv)
                        P.do("vector", lambda e: e.memset(bis[0:1, 1:2], ONE_F_BITS), inc=vv)
                        a2 = aff2[:, 64 * b:64 * (b + 1)]
                        for it in range(31):
                            P.wait("vector", vv, tt)
                            P.do("vector", lambda e: e.tensor_tensor(
                                out=bis[0:1, 2:3], in0=bis[0:1, 0:1], in1=bis[0:1, 1:2], op=ALU.add), inc=vv)
                            P.wait("vector", vv)
                            P.do("vector", lambda e: e.tensor_scalar(
                                out=bis[0:1, 2:3], in0=bis[0:1, 2:3], scalar1=1, scalar2=None,
                                op0=ALU.arith_shift_right), inc=vv)
                            P.wait("tensor", vv, ld)
                            P.do("tensor", lambda e: e.matmul(
                                ps_r[:, 0:1], oner_f[:, :], bis[0:1, 2:3].bitcast(F32),
                                start=True, stop=True), inc=tt)
                            P.wait("vector", tt)
                            P.do("vector", lambda e, a2=a2: e.tensor_scalar(
                                out=cmp64[:, :], in0=a2, scalar1=ps_r[:, 0:1], scalar2=None,
                                op0=ALU.is_ge, op1=ALU.add, accum_out=cnt1[:, 0:1]), inc=vv)
                            P.wait("vector", vv)
                            P.do("vector", lambda e: e.tensor_copy(out=cntb[:, :], in_=cnt1[:, :]), inc=vv)
                            P.wait("tensor", vv, gd)
                            P.do("tensor", lambda e: e.matmul(
                                ps_r[0:1, 1:2], ones_bf[:, :], cntb[:, :], start=True, stop=True), inc=tt)
                            P.wait("vector", tt)
                            P.do("vector", lambda e: e.tensor_scalar(
                                out=bis[0:1, 3:4], in0=ps_r[0:1, 1:2], scalar1=float(KTOP), scalar2=None,
                                op0=ALU.is_ge), inc=vv)
                            P.wait("vector", vv)
                            P.do("vector", lambda e: e.tensor_sub(
                                out=bis[0:1, 4:5], in0=bis[0:1, 2:3], in1=bis[0:1, 0:1]), inc=vv)
                            P.wait("vector", vv)
                            P.do("vector", lambda e: e.tensor_mul(
                                out=bis[0:1, 4:5], in0=bis[0:1, 4:5], in1=bis[0:1, 3:4]), inc=vv)
                            P.wait("vector", vv)
                            P.do("vector", lambda e: e.tensor_add(
                                out=bis[0:1, 0:1], in0=bis[0:1, 0:1], in1=bis[0:1, 4:5]), inc=vv)
                            P.do("vector", lambda e: e.tensor_sub(
                                out=bis[0:1, 5:6], in0=bis[0:1, 1:2], in1=bis[0:1, 2:3]), inc=vv)
                            P.do("vector", lambda e: e.tensor_scalar(
                                out=bis[0:1, 6:7], in0=bis[0:1, 3:4], scalar1=-1, scalar2=None,
                                op0=ALU.add), inc=vv)
                            P.wait("vector", vv)
                            P.do("vector", lambda e: e.tensor_mul(
                                out=bis[0:1, 5:6], in0=bis[0:1, 5:6], in1=bis[0:1, 6:7]), inc=vv)
                            P.wait("vector", vv)
                            P.do("vector", lambda e: e.tensor_add(
                                out=bis[0:1, 1:2], in0=bis[0:1, 1:2], in1=bis[0:1, 5:6]), inc=vv)
                        P.wait("vector", vv)
                        P.do("vector", lambda e, b=b: e.tensor_copy(out=tau[0:1, b:b + 1], in_=bis[0:1, 0:1]), inc=vv)
                        P.wait("vector", vv)
                        P.do("vector", lambda e, b=b: e.tensor_scalar(
                            out=m01[0:1, :], in0=affr_sh[0:1, :],
                            scalar1=tau[0:1, b:b + 1].bitcast(F32), scalar2=None,
                            op0=ALU.is_ge), inc=vv)
                        P.wait("vector", vv)
                        P.do("vector", lambda e: e.tensor_tensor_scan(
                            out=rnk[0:1, :], data0=m01[0:1, :], data1=m01[0:1, :],
                            initial=0.0, op0=ALU.add, op1=ALU.max), inc=vv)
                        P.wait("sync", vv)
                        P.dma("sync", posd[b:b + 1, :], rnk[0:1, :], st)
                        P.dma("sync", posjr[:, :],
                              posd.rearrange("b (j p) -> b j p", p=128)[b, :, :], ld)
                        P.wait("tensor", ld, vv)
                        P.do("tensor", lambda e: e.transpose(
                            ps_p[:, :], posjr[:, :], identf5[0:64, 0:64]), inc=tt)
                        P.wait("vector", tt)
                        P.do("vector", lambda e: e.tensor_copy(out=posf2[:, :], in_=ps_p[:, :]), inc=vv)
                        P.wait("sync", vv)
                        P.dma("sync", posjr[:, :],
                              affg_r.rearrange("(b j p) -> b j p", b=2, p=128)[b, :, :], ld)
                        P.wait("tensor", ld, vv)
                        P.do("tensor", lambda e: e.transpose(
                            ps_p[:, :], posjr[:, :], identf5[0:64, 0:64]), inc=tt)
                        P.wait("vector", tt)
                        P.do("vector", lambda e, b=b: e.tensor_copy(
                            out=affT[:, 64 * b:64 * (b + 1)], in_=ps_p[:, :]), inc=vv)
                        P.wait("tensor", vv)
                        P.do("tensor", lambda e, b=b: e.matmul(
                            ps_r[:, 2:3], oner_f[:, :], tau[0:1, b:b + 1].bitcast(F32),
                            start=True, stop=True), inc=tt)
                        P.wait("vector", tt)
                        P.do("vector", lambda e, b=b: e.tensor_scalar(
                            out=cmp64[:, :], in0=affT[:, 64 * b:64 * (b + 1)],
                            scalar1=ps_r[:, 2:3],
                            scalar2=None, op0=ALU.is_ge), inc=vv)
                        P.wait("vector", vv)
                        P.do("vector", lambda e: e.tensor_mul(
                            out=posf2[:, :], in0=posf2[:, :], in1=cmp64[:, :]), inc=vv)
                        P.do("vector", lambda e, b=b: e.tensor_scalar(
                            out=cmp64[:, :], in0=cmp64[:, :], scalar1=-BIG,
                            scalar2=BIG - 1.0 + b * KTOP,
                            op0=ALU.mult, op1=ALU.add), inc=vv)
                        P.wait("vector", vv)
                        P.do("vector", lambda e: e.tensor_add(
                            out=posf2[:, :], in0=posf2[:, :], in1=cmp64[:, :]), inc=vv)
                        P.wait("vector", vv)
                        P.do("vector", lambda e, b=b: e.tensor_copy(
                            out=posi2[:, 64 * b:64 * (b + 1)], in_=posf2[:, :]), inc=vv)
                        P.wait("gpsimd", ld, vv, st)
                        if b == 0:
                            P.wait("gpsimd", cc3b)
                        for j in range(64):
                            P.wait("vector", gd, vv)
                            P.do("vector", lambda e, b=b, j=j: e.tensor_copy(
                                out=pcol[:, :], in_=posi2[:, 64 * b + j:64 * b + j + 1]), inc=vv)
                            P.wait("gpsimd", vv)
                            P.do("gpsimd", lambda e, b=b, j=j: e.dma_start(
                                out=xrow[:, :], in_=ag3b_out[b * T + 128 * j:b * T + 128 * (j + 1), :]),
                                inc=gd, k=16)
                            P.fence("gpsimd", gd)
                            P.do("gpsimd", lambda e, b=b, j=j: e.indirect_dma_start(
                                out=xin_st, out_offset=IndirectOffsetOnAxis(pcol[:, :], 0),
                                in_=xrow[:, :], in_offset=None,
                                bounds_check=breg, oob_is_err=False), inc=gd, k=16)
                            P.fence("gpsimd", gd)
            _compact()

            def _expert():
                with P.scope():
                    I2L = 2048
                    wg = P.sb("o_wg", (128, 8 * I2L), BF16)
                    wu = P.sb("o_wu", (128, 8 * I2L), BF16)
                    wd = P.sb("o_wd", (128, 16 * D), BF16)
                    xc5 = P.sb("o_xc", (128, 8 * CH5), BF16)
                    hh = P.sb("o_hh", (128, 16 * CH5), BF16)
                    gsb = P.sb("o_gsb", (128, CH5), BF16)
                    ob = P.sb("o_ob", (128, 8 * CH5), BF16)
                    obtm = P.sb("o_obtm", (128, D), BF16)
                    ps_a = P.ps("o_psa", (128, CH5))
                    ps_b = P.ps("o_psb", (128, CH5))
                    ps_c = P.ps("o_psc", (128, CH5))
                    ps_tb = P.ps("o_pstb", (128, 128), BF16)
                    NCH4 = KTOP_G // CH5
                    for ih in range(2):
                        P.wait("gpsimd", tt, aa, vv, st)
                        for k in range(8):
                            P.dma("gpsimd", wg[:, I2L * k:I2L * (k + 1)], wg5_i[ih, k, :, :], gd, fence=False)
                            P.dma("gpsimd", wu[:, I2L * k:I2L * (k + 1)], wu5_i[ih, k, :, :], gd, fence=False)
                        for k in range(16):
                            P.dma("gpsimd", wd[:, D * k:D * (k + 1)], wd5_i[ih, k, :, :], gd, fence=False)
                        P.fence("gpsimd", gd)
                        for b in range(B):
                            for c5 in range(NCH4):
                                lo5 = CH5 * c5
                                P.wait("sync", vv, aa, tt, st, gd)
                                for m4 in range(4):
                                    for kd in range(8):
                                        P.do("sync", lambda e, b=b, lo5=lo5, m4=m4, kd=kd: e.dma_start_transpose(
                                            out=xc5[:, CH5 * kd + 128 * m4:CH5 * kd + 128 * (m4 + 1)],
                                            in_=xin_st[b * KTOP_G + lo5 + 128 * m4:b * KTOP_G + lo5 + 128 * (m4 + 1),
                                                       128 * kd:128 * (kd + 1)]),
                                            inc=ld, k=16)
                                P.fence("sync", ld)
                                P.wait("tensor", ld, gd)
                                for ic in range(16):
                                    P.wait("tensor", aa, vv)
                                    for k in range(8):
                                        P.do("tensor", lambda e, ic=ic, k=k: e.matmul(
                                            ps_a[:, :], wg[:, I2L * k + 128 * ic:I2L * k + 128 * (ic + 1)],
                                            xc5[:, CH5 * k:CH5 * (k + 1)], start=(k == 0), stop=(k == 7)),
                                            inc=(tt if k == 7 else None))
                                    for k in range(8):
                                        P.do("tensor", lambda e, ic=ic, k=k: e.matmul(
                                            ps_b[:, :], wu[:, I2L * k + 128 * ic:I2L * k + 128 * (ic + 1)],
                                            xc5[:, CH5 * k:CH5 * (k + 1)], start=(k == 0), stop=(k == 7)),
                                            inc=(tt if k == 7 else None))
                                    P.wait("scalar", tt)
                                    P.do("scalar", lambda e: e.activation(
                                        out=gsb[:, :], in_=ps_a[:, :], func=AF.Silu), inc=aa)
                                    P.wait("vector", aa, tt)
                                    P.do("vector", lambda e, ic=ic: e.tensor_mul(
                                        out=hh[:, CH5 * ic:CH5 * (ic + 1)], in0=ps_b[:, :],
                                        in1=gsb[:, :]), inc=vv)
                                for i in range(8):
                                    P.wait("tensor", vv, aa)
                                    for k in range(16):
                                        P.do("tensor", lambda e, i=i, k=k: e.matmul(
                                            ps_c[:, :], wd[:, D * k + 128 * i:D * k + 128 * (i + 1)],
                                            hh[:, CH5 * k:CH5 * (k + 1)], start=(k == 0), stop=(k == 15)),
                                            inc=(tt if k == 15 else None))
                                    P.wait("vector", tt, st, gd)
                                    P.do("vector", lambda e, i=i: e.tensor_copy(
                                        out=ob[:, CH5 * i:CH5 * (i + 1)], in_=ps_c[:, :]), inc=vv)
                                    P.wait("tensor", vv)
                                for m4 in range(4):
                                    P.wait("tensor", vv, gd)
                                    for i in range(8):
                                        P.do("tensor", lambda e, i=i, m4=m4: e.transpose(
                                            ps_tb[:, :], ob[:, CH5 * i + 128 * m4:CH5 * i + 128 * (m4 + 1)],
                                            ident5[:, :]), inc=tt)
                                        P.wait("vector", tt)
                                        P.do("vector", lambda e, i=i: e.tensor_copy(
                                            out=obtm[:, 128 * i:128 * (i + 1)], in_=ps_tb[:, :]), inc=vv)
                                        P.wait("tensor", vv)
                                    P.wait("gpsimd", vv)
                                    if ih == 0:
                                        P.do("gpsimd", lambda e, b=b, lo5=lo5, m4=m4: e.dma_start(
                                            out=out_st[b * KTOP_G + lo5 + 128 * m4:b * KTOP_G + lo5 + 128 * (m4 + 1), :],
                                            in_=obtm[:, :]), inc=gd, k=16)
                                    else:
                                        P.do("gpsimd", lambda e, b=b, lo5=lo5, m4=m4: e.dma_start(
                                            out=out_st[b * KTOP_G + lo5 + 128 * m4:b * KTOP_G + lo5 + 128 * (m4 + 1), :],
                                            in_=obtm[:, :], accum_op=ALU.add), inc=gd, k=16)
                                    P.fence("gpsimd", gd)
                                    P.wait("vector", gd)
            _expert()

            for b in range(B):
                for j in range(64):
                    P.wait("vector", gd, vv)
                    P.do("vector", lambda e: e.memset(xrow[:, :], 0.0), inc=vv)
                    P.do("vector", lambda e, b=b, j=j: e.tensor_copy(
                        out=pcol[:, :], in_=posi2[:, 64 * b + j:64 * b + j + 1]), inc=vv)
                    P.wait("gpsimd", vv, st)
                    P.do("gpsimd", lambda e, b=b, j=j: e.indirect_dma_start(
                        out=xrow[:, :], out_offset=None,
                        in_=out_st, in_offset=IndirectOffsetOnAxis(pcol[:, :], 0),
                        bounds_check=breg, oob_is_err=False), inc=gd, k=16)
                    P.fence("gpsimd", gd)
                    P.wait("vector", gd)
                    P.do("vector", lambda e, b=b, j=j: e.tensor_scalar(
                        out=xrow[:, :], in0=xrow[:, :],
                        scalar1=affT[:, 64 * b + j:64 * b + j + 1], scalar2=None,
                        op0=ALU.mult), inc=vv)
                    P.wait("gpsimd", vv)
                    P.dma("gpsimd", rs_in[b * T + 128 * j:b * T + 128 * (j + 1), :],
                          xrow[:, :], gd)
            P.wait("sync", st)
            P.wait("gpsimd", gd)
            P.barrier(allsems)
    if upto >= 5:
        _phase_5()
    if upto >= 6:
        P.collective("ReduceScatter", ALU.add, rs_in, rs_out, cc4)

    def _phase_6():
        with P.scope():
            x1t = P.sb("f_x1t", (128, TOKC), F32)
            bt = P.sb("f_bt", (128, TOKC), BF16)
            btf = P.sb("f_btf", (128, TOKC), F32)
            rst = P.sb("f_rst", (128, TOKC), BF16)
            P.wait("sync", cc4)
            for k in range(8):
                P.wait("sync", vv, st)
                P.dma("sync", x1t[:, :], x1_scr[k, :, :], ld, fence=False)
                P.dma("sync", bt[:, :], attn_scr[k, :, :], ld)
                for t16 in range(16):
                    P.do("sync", lambda e, k=k, t16=t16: e.dma_start_transpose(
                        out=rst[:, 128 * t16:128 * (t16 + 1)],
                        in_=rs_out[128 * t16:128 * (t16 + 1), 128 * k:128 * (k + 1)]),
                        inc=ld, k=16)
                P.fence("sync", ld)
                P.wait("vector", ld)
                P.do("vector", lambda e: e.tensor_copy(out=btf[:, :], in_=bt[:, :]), inc=vv)
                P.wait("vector", vv)
                P.do("vector", lambda e: e.tensor_add(
                    out=x1t[:, :], in0=x1t[:, :], in1=btf[:, :]), inc=vv)
                P.wait("vector", vv)
                P.do("vector", lambda e: e.tensor_copy(out=btf[:, :], in_=rst[:, :]), inc=vv)
                P.wait("vector", vv)
                P.do("vector", lambda e: e.tensor_add(
                    out=x1t[:, :], in0=x1t[:, :], in1=btf[:, :]), inc=vv)
                P.wait("sync", vv)
                P.dma("sync", out_o[k, :, :], x1t[:, :], st)
            P.wait("sync", st)
            P.wait("gpsimd", st)
    if upto >= 6:
        _phase_6()
    else:
        def _stub():
            with P.scope():
                zt = P.sb("z_zt", (128, TOKC), F32)
                P.barrier(allsems)
                P.do("vector", lambda e: e.memset(zt[:, :], 0.0), inc=vv)
                P.wait("sync", vv)
                for k in range(8):
                    P.dma("sync", out_o[k, :, :], zt[:, :], st)
                P.wait("sync", st)
        _stub()
    return P.finish()


def host_fused_inputs(inputs):
    x_dt = np.ascontiguousarray(inputs["L0_attn_x"].reshape(NTOK, D).T)
    inpw_t = np.ascontiguousarray(inputs["L0_attn_in_proj_w"].T.reshape(8, 128, 2 * DI))
    norm_w = np.ascontiguousarray(inputs["L0_attn_norm_w"].reshape(8, 128, 1))
    conv_wa = np.ascontiguousarray(
        inputs["L0_attn_conv_w"].reshape(DI, 4).reshape(16, 128, 4).transpose(1, 0, 2).reshape(128, 64))
    conv_ba = np.ascontiguousarray(inputs["L0_attn_conv_b"].reshape(16, 128).T)
    xpw_t = np.ascontiguousarray(inputs["L0_attn_x_proj_w"].T.reshape(16, 128, 96))
    dtpw_t = np.ascontiguousarray(inputs["L0_attn_dt_proj_w"].T)
    dtb_a = np.ascontiguousarray(inputs["L0_attn_dt_proj_b"].reshape(16, 128).T)
    ones_c = np.ones((128, 1), np.float32)
    ones_r = np.ones((1, 128), np.float32)
    cst_arr = np.tile(np.array([1.0, -1.0, -0.5, LN_EPS, 1.0 / 32, 0, -1e9, 0],
                               np.float32), (128, 1))
    sel = np.zeros((16, 128, 128), np.float32)
    for p in range(16):
        for dh in range(8):
            for n in range(16):
                sel[p, 8 * p + dh, 16 * dh + n] = 1.0
    nselm = np.ascontiguousarray(np.transpose(sel, (0, 2, 1)))
    sbsel = np.zeros((16, 128), np.float32)
    for n in range(16):
        for dh in range(8):
            sbsel[n, 16 * dh + n] = 1.0
    opw_t = np.ascontiguousarray(inputs["L0_attn_out_proj_w"].T.reshape(16, 128, D))
    wg_t = np.ascontiguousarray(inputs["L0_ffn_Wg"].T.reshape(8, 128, 8))
    lnsc_i = np.ascontiguousarray(inputs["L0_ffn_ln_scale"].T.reshape(8, 128, 8))
    lnbi_i = np.ascontiguousarray(inputs["L0_ffn_ln_bias"].T.reshape(8, 128, 8))
    w1_i = np.ascontiguousarray(inputs["L0_ffn_W1"].reshape(8, 8, 128, 256))
    b1_i = np.ascontiguousarray(
        inputs["L0_ffn_b1"].reshape(8, 2, 128).transpose(2, 0, 1).reshape(128, 16))
    w2_i = np.ascontiguousarray(inputs["L0_ffn_W2"].reshape(8, 2, 128, D))
    b2_i = np.ascontiguousarray(inputs["L0_ffn_b2"].reshape(1, 8 * D))
    gw_t = np.ascontiguousarray(inputs["L1_ffn_gate_w"].T.reshape(8, 128, 8))
    ident = np.eye(128, dtype=np.float32)
    qk_i = np.ascontiguousarray(inputs["L1_attn_qk"].reshape(8, 128, D))
    ov_i = np.ascontiguousarray(inputs["L1_attn_ov"].reshape(8, 128, D))
    kio = np.tile(np.arange(512, dtype=np.float32), (128, 1))
    prng = np.arange(128)
    maps = []
    for c in range(NC):
        lo = c * TOKC
        prev = np.zeros((D, 3), np.float32) if lo % T == 0 else x_dt[:, lo - 3:lo]
        beta = c // 4
        ch0 = 256 * c
        alog_a = np.zeros((128, 32), np.float32)
        A_log = inputs["L0_attn_A_log"]
        for g in range(32):
            for dh in range(8):
                for n in range(16):
                    alog_a[16 * dh + n, g] = A_log[ch0 + 8 * g + dh, n]
        d_a = np.ascontiguousarray(inputs["L0_attn_D"][ch0:ch0 + 256].reshape(2, 128).T)
        sidx = np.zeros((128, 32), np.int32)
        for rr in range(2):
            for q in range(8):
                sidx[:, rr * 8 + q] = (q * 33 + 2 * c + rr) * 128 + prng
                sidx[:, 16 + rr * 8 + q] = (q * 33 + 16 + 2 * c + rr) * 128 + prng
        yidx = np.zeros((128, 16), np.int32)
        for g in range(16):
            q, rr = g // 2, g % 2
            yidx[:, g] = q * 2048 + (c * 2 + rr) * 128 + prng
        qidx = np.zeros((128, 16), np.float32)
        for s in range(16):
            qidx[:, s] = (16 * (c % 4) + s) * 128 + prng
        kidx = np.zeros((128, 16 * 8), np.int32)
        for j in range(16):
            gj = beta * 16 + j
            for kd in range(8):
                kidx[:, j * 8 + kd] = (gj // 4) * 1024 + kd * 128 + prng
        vidx = np.zeros((128, 16 * 4), np.int32)
        for j in range(16):
            gj = beta * 16 + j
            for m4 in range(4):
                vidx[:, j * 4 + m4] = gj * 512 + m4 * 128 + prng
        aidx = (np.arange(8, dtype=np.int32) * 8 + c).reshape(8, 1)
        wg5 = np.ascontiguousarray(
            inputs["L1_ffn_w_gate"][c].T.reshape(8, 128, 4096)
            .reshape(8, 128, 2, 2048).transpose(2, 0, 1, 3))
        wu5 = np.ascontiguousarray(
            inputs["L1_ffn_w_up"][c].T.reshape(8, 128, 4096)
            .reshape(8, 128, 2, 2048).transpose(2, 0, 1, 3))
        wd5 = np.ascontiguousarray(inputs["L1_ffn_w_down"][c].T.reshape(2, 16, 128, D))
        maps.append({
            "x_ext": _pad3(x_dt[:, lo:lo + TOKC], prev),
            "inpw_t": inpw_t, "norm_w": norm_w, "conv_wa": conv_wa,
            "conv_ba": conv_ba, "xpw_t": xpw_t, "dtpw_t": dtpw_t, "dtb_a": dtb_a,
            "ones_c": ones_c, "ones_r": ones_r, "cst_i": cst_arr,
            "alog_a": alog_a, "d_i": d_a, "sel_i": sel, "nsel_i": nselm,
            "sbsel_i": sbsel, "sidx_i": sidx, "yidx_i": yidx,
            "opw_t": opw_t, "wg_t": wg_t, "lnsc_i": lnsc_i, "lnbi_i": lnbi_i,
            "w1_i": w1_i, "b1_i": b1_i, "w2_i": w2_i, "b2_i": b2_i, "gw_t": gw_t,
            "ident_i": ident, "qk_i": qk_i, "ov_i": ov_i, "qidx_i": qidx,
            "kio_i": kio, "kidx_i": kidx, "vidx_i": vidx, "aidx_i": aidx,
            "wg5_i": wg5, "wu5_i": wu5, "wd5_i": wd5,
        })
    return maps


def kernel(**inputs):
    inputs = {k: np.asarray(v) for k, v in inputs.items()}
    nc = build_fused()
    maps = host_fused_inputs(inputs)
    res = run_bass_kernel_spmd(nc, maps, core_ids=list(range(NC)))
    out_dt = np.concatenate(
        [res.results[c]["out_o"].reshape(D, TOKC) for c in range(NC)], axis=1)
    return np.ascontiguousarray(out_dt.T.reshape(B, T, D)).astype(np.float32)


def kernel_unfused(**inputs):
    inputs = {k: np.asarray(v) for k, v in inputs.items()}
    fo = run_front(inputs)
    y_full = run_scan(fo, inputs)
    mo = run_mid(inputs, fo["xn_o"], fo["res_o"], y_full)
    attn_full = run_attn(inputs, mo["xn1"])
    dense_list = run_moe(inputs, mo["xn1"], mo["aff"])
    out_dt = run_final(mo["x1"], attn_full, dense_list)
    return np.ascontiguousarray(out_dt.T.reshape(B, T, D)).astype(np.float32)
